# version 14
# baseline (speedup 1.0000x reference)
import numpy as np
import ml_dtypes
from contextlib import ExitStack

import concourse.bass as bass
import concourse.mybir as mybir
from concourse.bass_utils import run_bass_kernel_spmd

F32 = mybir.dt.float32
BF16 = mybir.dt.bfloat16
AF = mybir.ActivationFunctionType
ALU = mybir.AluOpType
AX = mybir.AxisListType.X
bf16 = ml_dtypes.bfloat16

NCORE = 8
T = 2048
D = 1024
NK = 8
NT = 512
NTILES = T // NT

L0_SHIFT = 20.0
L0_SCALE = 0.125
L1_SHIFT = 20.0
L1_SCALE = float(1.0 / np.sqrt(np.float32(128.0)))


class S:
    pass


class Pool:

    def __init__(self, handle, nbytes):
        self.h = handle
        self.nbytes = nbytes
        self.off = 0

    def reset(self):
        self.off = 0

    def get(self, shape, dtype):
        import math
        p = shape[0]
        n = int(np.prod(shape[1:])) if len(shape) > 1 else 1
        isz = 2 if dtype == BF16 else 4
        nb = n * isz
        off = self.off
        self.off = (off + nb + 63) & ~63
        assert self.off <= self.nbytes, f"pool overflow {self.off}"
        ap = self.h.ap()[0:p, off:off + nb].bitcast(dtype)
        if len(shape) > 2:
            dims = " ".join(f"d{i}" for i in range(1, len(shape)))
            kw = {f"d{i}": shape[i] for i in range(1, len(shape))}
            ap = ap.rearrange(f"q ({dims}) -> q {dims}", **kw)
        return ap


def _run_phases(nc, bar, pairs):
    with nc.Block() as block:
        def mk(en):
            def f(e):
                for i, (ph, s) in enumerate(pairs):
                    if i:
                        e.wait_ge(bar, 5 * i)
                    ph(en, e, s)
                    e.drain().then_inc(bar, 1)
            return f
        block.sync(mk("sync"))
        block.tensor(mk("tensor"))
        block.scalar(mk("scalar"))
        block.vector(mk("vector"))
        block.gpsimd(mk("gpsimd"))


def _link(s_dst, s_src):
    for nm in ("cst", "cvals", "C", "C1", "Cn", "ones_row", "ones_col",
               "ones_col_b"):
        setattr(s_dst, nm, getattr(s_src, nm))


def _setup_consts(nc, es, s, values):
    vals = list(dict.fromkeys(values))
    s.cvals = {v: i for i, v in enumerate(vals)}
    s.cst = es.enter_context(nc.sbuf_tensor("cst", [128, max(len(vals), 1)], F32))
    s.C = lambda v: s.cst[:, s.cvals[v]:s.cvals[v] + 1]
    s.C1 = lambda v: s.cst[0:1, s.cvals[v]:s.cvals[v] + 1]
    s.Cn = lambda v, n: s.cst[0:n, s.cvals[v]:s.cvals[v] + 1]


def _emit_const_init(en, e, s):
    if getattr(s, "skip_consts", False):
        return
    if en == "vector":
        for v, i in s.cvals.items():
            e.memset(s.cst[:, i:i + 1], float(v))
        e.memset(s.ones_row[:], 1.0)
        e.memset(s.ones_col[:], 1.0)
        e.memset(s.ones_col_b[:], 1.0)



LN_NAMES = ("x2", "stat", "row", "bc", "app", "ach")


def _ln_R(sub_mean, post_scale):
    return (4 + 2 + (1 if post_scale is not None else 0) + 1) if sub_mean \
        else (1 + 2 + (1 if post_scale is not None else 0))


class LNBank:

    def __init__(self, nc, es):
        self.h = {nm: es.enter_context(nc.semaphore("LN_" + nm))
                  for nm in LN_NAMES}
        self.base = {nm: 0 for nm in LN_NAMES}

    def take(self, s, prefix, ntiles, sub_mean, post_scale, has_lnw, nk):
        for nm in LN_NAMES:
            setattr(s, f"{prefix}_{nm}", self.h[nm])
        bases = dict(self.base)
        setattr(s, f"{prefix}_bases", bases)
        R = _ln_R(sub_mean, post_scale)
        per_k = 3 if has_lnw else 2
        self.base["x2"] += ntiles
        self.base["stat"] += ntiles
        self.base["row"] += R * ntiles
        self.base["bc"] += ntiles
        self.base["app"] += ntiles
        self.base["ach"] += per_k * nk * ntiles
        return bases


def _emit_ln(en, e, s, cfg):
    p = cfg["prefix"]
    sub_mean = cfg.get("sub_mean", True)
    eps = cfg["eps"]
    post_scale = cfg.get("post_scale")
    wcol = cfg.get("weight_col")
    src_bf16 = cfg.get("src_bf16", False)
    ntiles = cfg.get("ntiles", NTILES)
    nk = cfg.get("nk", NK)
    feat = 128 * nk
    sx2 = getattr(s, f"{p}_x2"); sstat = getattr(s, f"{p}_stat")
    srow = getattr(s, f"{p}_row"); sbc = getattr(s, f"{p}_bc")
    sapp = getattr(s, f"{p}_app"); sach = getattr(s, f"{p}_ach")
    B = getattr(s, f"{p}_bases")
    bx2 = B["x2"]; bstat = B["stat"]; brow = B["row"]
    bbc = B["bc"]; bapp = B["app"]; bach = B["ach"]
    mu = s.ps_aux[0:1, 0:NT]
    sq = s.ps_aux[64:65, 0:NT]
    bcA = s.ps_aux[:, 512:1024]
    bcB = s.ps_aux[:, 1024:1536]
    ones_mu = s.ones_col_b if src_bf16 else s.ones_col
    R = _ln_R(sub_mean, post_scale)
    lnw = cfg.get("lnw_col")
    lnb = cfg.get("lnb_col")
    per_k = 3 if lnw is not None else 2
    achc = bach
    for n in range(ntiles):
        if "pre" in cfg:
            cfg["pre"](en, e, s, n)
        if en == "scalar":
            if n > 0 or bstat > 0:
                e.wait_ge(sstat, bstat + n)
            if "src_all" in cfg:
                e.activation(s.x2s[:, 0:nk, :], cfg["src_all"](n),
                             AF.Square).then_inc(sx2, 1)
            else:
                for k in range(nk):
                    ins = e.activation(s.x2s[:, k, :], cfg["src"](k, n),
                                       AF.Square)
                ins.then_inc(sx2, 1)
        if en == "tensor":
            e.wait_ge(sx2, bx2 + n + 1)
            if n > 0 or brow > 0:
                e.wait_ge(srow, brow + R * n)
            if sub_mean:
                for k in range(nk):
                    e.matmul(mu, ones_mu[:], cfg["src"](k, n),
                             start=(k == 0), stop=(k == nk - 1))
            for k in range(nk):
                ins = e.matmul(sq, s.ones_col[:], s.x2s[:, k, :],
                               start=(k == 0), stop=(k == nk - 1))
            ins.then_inc(sstat, 1)
        c = brow + R * n
        if en == "vector":
            if n > 0 or brow > 0:
                e.wait_ge(srow, c)
            e.wait_ge(sstat, bstat + n + 1)
            if sub_mean:
                e.tensor_scalar(s.row_ex2[:], sq, s.C1(1.0 / feat), None,
                                ALU.mult).then_inc(srow, 1)
                e.tensor_scalar(s.row_mu[:], mu, s.C1(1.0 / feat), None,
                                ALU.mult).then_inc(srow, 1)
                e.wait_ge(srow, c + 2)
                e.tensor_tensor(s.row_tmp[:], s.row_mu[:], s.row_mu[:],
                                ALU.mult).then_inc(srow, 1)
                e.wait_ge(srow, c + 3)
                e.tensor_tensor(s.row_ex2[:], s.row_ex2[:], s.row_tmp[:],
                                ALU.subtract).then_inc(srow, 1)
            else:
                e.tensor_scalar(s.row_ex2[:], sq, s.C1(1.0 / feat), None,
                                ALU.mult).then_inc(srow, 1)
        nvar = c + (4 if sub_mean else 1)
        if en == "scalar":
            e.wait_ge(srow, nvar)
            e.activation(s.row_tmp2[:], s.row_ex2[:], AF.Ln,
                         bias=s.C1(eps)).then_inc(srow, 1)
            e.wait_ge(srow, nvar + 1)
            e.activation(s.row_a[:], s.row_tmp2[:], AF.Exp,
                         scale=s.C1(-0.5)).then_inc(srow, 1)
            if post_scale is not None:
                e.wait_ge(srow, nvar + 2)
                e.mul(s.row_a[:], s.row_a[:],
                      s.C1(post_scale)).then_inc(srow, 1)
        na = nvar + 2 + (1 if post_scale is not None else 0)
        if sub_mean:
            if en == "vector":
                e.wait_ge(srow, na)
                e.scalar_tensor_tensor(s.row_b[:], s.row_mu[:], s.C1(-1.0),
                                       s.row_a[:], ALU.mult, ALU.mult
                                       ).then_inc(srow, 1)
        if en == "tensor":
            e.wait_ge(srow, brow + R * (n + 1))
            if n > 0 or bapp > 0:
                e.wait_ge(sapp, bapp + n)
            e.matmul(bcA, s.ones_row[:], s.row_a[:], start=True, stop=True)
            ins = e.matmul(bcB, s.ones_row[:],
                           s.row_b[:] if sub_mean else s.row_a[:],
                           start=True, stop=True)
            ins.then_inc(sbc, 1)
        if en == "vector":
            e.wait_ge(sbc, bbc + n + 1)
        for k in range(nk):
            if en == "vector":
                if achc >= per_k or bach > 0:
                    e.wait_ge(sach, achc)
                e.tensor_tensor(s.app_t1[:], cfg["src"](k, n), bcA,
                                ALU.mult).then_inc(sach, 1)
                e.wait_ge(sach, achc + 1)
                if lnw is not None:
                    e.tensor_tensor(s.app_t2[:], s.app_t1[:], bcB,
                                    ALU.add).then_inc(sach, 1)
                    e.wait_ge(sach, achc + 2)
                    ins = e.tensor_scalar(cfg["dst"](k, n), s.app_t2[:],
                                          lnw(k), lnb(k), ALU.mult, ALU.add)
                elif sub_mean:
                    ins = e.tensor_tensor(cfg["dst"](k, n), s.app_t1[:], bcB,
                                          ALU.add)
                else:
                    ins = e.tensor_scalar(cfg["dst"](k, n), s.app_t1[:],
                                          wcol(k), None, ALU.mult)
                ins.then_inc(sach, 1)
            achc += per_k
            if en == "vector" and k == nk - 1:
                e.wait_ge(sach, achc)
                e.sem_inc(sapp, 1)
        if "post" in cfg:
            cfg["post"](en, e, s, n)



def setup_A(nc, es, s, P, LN, io):
    x_d = io["x0"]
    w_d = io["wqkv0"]
    id_d = io["ident"]
    q_d = io["q0"]
    k_d = io["k0"]
    v_d = io["v0"]
    s.xsl = P.get([128, 2, NK, NT], F32)
    s.x2s = P.get([128, NK, NT], F32)
    s.xln = P.get([128, NK, T], BF16)
    for nm in ("row_ex2", "row_mu", "row_tmp", "row_tmp2", "row_a", "row_b"):
        setattr(s, nm, P.get([1, NT], F32))
    s.app_t1 = P.get([128, NT], F32)
    s.w_sb = P.get([128, 2, NK, D], BF16)
    s.out_buf = P.get([128, 2, T], BF16)
    s.vfm_buf = P.get([128, 2, T], BF16)
    s.vtm = P.get([128, T // 128, D], BF16)
    s.ident = P.get([128, 128], BF16)
    s.ld_x2 = [es.enter_context(nc.semaphore(f"Ald_x{i}")) for i in range(2)]
    s.ld_w2p = [es.enter_context(nc.semaphore(f"Ald_w{i}")) for i in range(2)]
    s.ld_id = es.enter_context(nc.semaphore("Ald_id"))
    for nm in ("s_mm", "s_ev", "s_tr", "s_cp"):
        setattr(s, nm, es.enter_context(nc.semaphore("A" + nm)))
    if getattr(s, "st2", None) is None:
        s.st2 = [es.enter_context(nc.semaphore(f"st2_{i}"))
                 for i in range(2)]
        s.st_vv = es.enter_context(nc.semaphore("st_vv"))
        s.st2_base = [0, 0]
        s.stv_base = 0
    LN.take(s, "lnA", NTILES, True, None, False, NK)

    def ph_init(en, e, s):
        _emit_const_init(en, e, s)
        if en == "sync":
            e.dma_start(out=s.ident[:], in_=id_d[:]).then_inc(s.ld_id, 16)
            e.dma_start(out=s.xsl[:, 0], in_=x_d[:, 0:NT].rearrange(
                "(m p) t -> p m t", p=128)).then_inc(s.ld_x2[0], 16)
        if en == "gpsimd":
            e.dma_start(out=s.w_sb[:, 0],
                        in_=w_d[1].rearrange("(k p) n -> p k n", p=128)
                        ).then_inc(s.ld_w2p[0], 16)

    def pre_ln(en, e, s, n):
        if en == "sync" and n + 1 < NTILES:
            if n >= 1:
                e.wait_ge(s.lnA_app, s.lnA_bases["app"] + n)
            e.dma_start(out=s.xsl[:, (n + 1) % 2],
                        in_=x_d[:, (n + 1) * NT:(n + 2) * NT].rearrange(
                            "(m p) t -> p m t", p=128)
                        ).then_inc(s.ld_x2[(n + 1) % 2], 16)
        if en in ("vector", "tensor", "scalar"):
            e.wait_ge(s.ld_x2[n % 2], 16 * (n // 2 + 1))

    def ph_ln(en, e, s):
        _emit_ln(en, e, s, dict(
            prefix="lnA", eps=1e-5, sub_mean=True,
            src=lambda k, n: s.xsl[:, n % 2, k, :],
            src_all=lambda n: s.xsl[:, n % 2, :, :],
            dst=lambda k, n: s.xln[:, k, n * NT:(n + 1) * NT],
            pre=pre_ln,
        ))

    def ph_proj(en, e, s):
        ORD = [1, 2, 0]
        KIND = ["k", "v", "q"]
        mmc = 0
        evc = 0
        trc = 0
        stc = 0
        pb = s.st2_base
        if en == "tensor":
            e.wait_ge(s.ld_id, 16)
        for j in range(3):
            kind = KIND[j]
            if en == "gpsimd" and j + 1 < 3:
                if j > 0:
                    e.wait_ge(s.s_mm, 32 * j)
                e.dma_start(out=s.w_sb[:, (j + 1) % 2],
                            in_=w_d[ORD[j + 1]].rearrange(
                                "(k p) n -> p k n", p=128)
                            ).then_inc(s.ld_w2p[(j + 1) % 2], 16)
            for m in range(NK):
                for n in range(NTILES):
                    slot = mmc % 2
                    ps = s.ps_main[:, slot * NT:(slot + 1) * NT]
                    mmc += 1
                    if en == "tensor":
                        if mmc == 1 or (n == 0 and m == 0):
                            e.wait_ge(s.ld_w2p[j % 2], 16 * (j // 2 + 1))
                        if mmc > 2:
                            e.wait_ge(s.s_ev, mmc - 2)
                        for k in range(NK):
                            ins = e.matmul(
                                ps, s.w_sb[:, j % 2, k, 128 * m:128 * (m + 1)],
                                s.xln[:, k, n * NT:(n + 1) * NT],
                                start=(k == 0), stop=(k == NK - 1))
                        ins.then_inc(s.s_mm, 1)
                    evc += 1
                    if en == "scalar":
                        e.wait_ge(s.s_mm, evc)
                        if n == 0:
                            if kind == "v":
                                if m >= 2:
                                    e.wait_ge(s.s_cp, 16 * (m - 1))
                            else:
                                sn = (0 if kind == "k" else 8) + m
                                if sn >= 2:
                                    e.wait_ge(s.st2[sn % 2],
                                              pb[sn % 2] + 16 * (sn // 2))
                        dst = s.out_buf if kind != "v" else s.vfm_buf
                        e.activation(dst[:, m % 2, n * NT:(n + 1) * NT], ps,
                                     AF.Copy).then_inc(s.s_ev, 1)
                if kind != "v":
                    stc += 1
                    if en == "sync":
                        e.wait_ge(s.s_ev, evc)
                        d = k_d if kind == "k" else q_d
                        e.dma_start(out=d[128 * m:128 * (m + 1), :],
                                    in_=s.out_buf[:, m % 2]
                                    ).then_inc(s.st2[(stc - 1) % 2], 16)
                else:
                    for tb in range(T // 128):
                        trc += 1
                        if en == "tensor":
                            e.wait_ge(s.s_ev, evc)
                            if trc > 1:
                                e.wait_ge(s.s_cp, trc - 1)
                            e.transpose(s.ps_t[:, 0:128],
                                        s.vfm_buf[:, m % 2,
                                                  128 * tb:128 * (tb + 1)],
                                        s.ident[:]).then_inc(s.s_tr, 1)
                        if en == "vector":
                            e.wait_ge(s.s_tr, trc)
                            e.tensor_copy(s.vtm[:, tb, 128 * m:128 * (m + 1)],
                                          s.ps_t[:, 0:128]).then_inc(s.s_cp, 1)
            if kind == "k" and en == "sync":
                e.wait_ge(s.st2[0], pb[0] + 16 * 4)
                e.wait_ge(s.st2[1], pb[1] + 16 * 4)
            if kind == "v" and en == "sync":
                e.wait_ge(s.s_cp, 128)
                e.dma_start(out=v_d.rearrange("(tb p) d -> p tb d", p=128),
                            in_=s.vtm[:]).then_inc(s.st_vv, 16)
        if en == "gpsimd" and getattr(s, "kgather", None):
            ksrc, kdst, lvl = s.kgather
            e.wait_ge(s.st2[0], pb[0] + 16 * 4)
            e.wait_ge(s.st2[1], pb[1] + 16 * 4)
            _issue_gather(e, s, s.io, s.cc, ksrc, kdst, s.sim_local)
            if getattr(s, "vgather0", None):
                e.wait_ge(s.st_vv, s.stv_base + 16)
                _issue_gather(e, s, s.io, s.ccv, *s.vgather0, s.sim_local)

    def ph_store(en, e, s):
        if en == "sync":
            e.wait_ge(s.st2[0], s.st2_base[0] + 16 * 8)
            e.wait_ge(s.st2[1], s.st2_base[1] + 16 * 8)
            e.wait_ge(s.st_vv, s.stv_base + 16)
    return [ph_init, ph_ln, ph_proj, ph_store]



def _attn_alloc(nc, es, s, tag):
    if getattr(s, "ld_kv2", None) is None:
        s.ld_kv2 = [es.enter_context(nc.semaphore(f"{tag}ld_kv{i}"))
                    for i in range(2)]
        s.ld_v2 = [es.enter_context(nc.semaphore(f"{tag}ld_v{i}"))
                   for i in range(2)]
    for nm in ("s_sc", "s_exp", "s_pv", "s_ep", "s_att", "s_unit"):
        setattr(s, nm, es.enter_context(nc.semaphore(tag + nm)))
    s.s_bc = s.s_ep


def _emit_attn(en, e, s, cfg):
    NU = cfg["n_units"]
    pair = cfg["pair"]
    npt = cfg.get("pt_slots", 2)
    NCH = 64
    dma_per_unit = cfg["dma_per_unit"]
    pend = []

    def emit_pv(e, s, u, eg, qbuf, c):
        pvs = s.ps_aux[:, 1024 * qbuf:1024 * qbuf + 1024]
        gc = eg * NCH + c
        if pair:
            e.wait_ge(s.s_exp, gc + 1)
            if c == 0 and eg >= 2:
                e.wait_ge(s.s_att, 2 * (eg - 1))
            e.matmul(pvs[0:65, 0:512],
                     s.vslab[:, u % 2, 0, c, :],
                     s.pt[:, gc % npt, 0:512],
                     start=(c == 0), stop=(c == NCH - 1))
            ins = e.matmul(pvs[0:65, 512:1024],
                           s.vslab[:, u % 2, 1, c, :],
                           s.pt[:, gc % npt, 512:1024],
                           start=(c == 0), stop=(c == NCH - 1))
            ins.then_inc(s.s_pv, 1)
        else:
            j = gc // 2
            e.wait_ge(s.s_exp, j + 1)
            if c == 1 and eg >= 2:
                e.wait_ge(s.s_att, 2 * (eg - 1))
            for cc in (c - 1, c):
                pts = s.pt[:, j % 2, 512 * (cc % 2):512 * (cc % 2) + 512]
                e.matmul(pvs[0:65, 0:512],
                         s.vslab[:, u % 2, 0, cc, 0:65], pts,
                         start=(cc == 0), stop=(cc == NCH - 1))
                ins = e.matmul(pvs[0:64, 512:1024],
                               s.vslab[:, u % 2, 0, cc, 65:129], pts,
                               start=(cc == 0), stop=(cc == NCH - 1))
                ins.then_inc(s.s_pv, 1)

    def flush_pe_epilogue(e):
        while pend:
            peg, pqb = pend.pop(0)
            ppvs = s.ps_aux[:, 1024 * pqb:1024 * pqb + 1024]
            e.wait_ge(s.s_ep, 6 * peg + 4)
            e.matmul(ppvs[:, 0:512], s.ones_row[:], s.rrow[0:1, 0, :],
                     start=True, stop=True).then_inc(s.s_bc, 1)
            e.matmul(ppvs[:, 512:1024], s.ones_row[:], s.rrow[0:1, 1, :],
                     start=True, stop=True).then_inc(s.s_bc, 1)

    for u in range(NU):
        if en == "sync" and u + 1 < NU:
            if u >= 1:
                e.wait_ge(s.s_unit, u)
            cfg["dma_unit"](e, s, u + 1, (u + 1) % 2)
        for qg in range(NTILES):
            eg = u * NTILES + qg
            qbuf = qg % 2
            for c in range(NCH):
                gc = eg * NCH + c
                if pair:
                    sb = gc % 2
                    ps_s = s.ps_main[:, 1024 * sb:1024 * (sb + 1)]
                    exp_needed = gc + 1
                else:
                    j = gc // 2
                    sb = j % 2
                    ps_s = s.ps_main[:, 1024 * sb:1024 * (sb + 1)]
                    exp_needed = j + 1
                if en == "tensor":
                    if c == 8:
                        flush_pe_epilogue(e)
                    if qg == 0 and c == 0:
                        e.wait_ge(s.ld_kv2[u % 2],
                                  getattr(s, "kv_base", 0)
                                  + 16 * 5 * (u // 2 + 1))
                    if pair:
                        if gc >= 2:
                            e.wait_ge(s.s_exp, gc - 1)
                        e.matmul(ps_s[:, 0:512],
                                 s.k_buf[0:64, u % 2, 128 * c:128 * (c + 1)],
                                 s.q_buf[0:64, u % 2, qg * NT:(qg + 1) * NT],
                                 start=True, stop=True, tile_position=(0, 0))
                        ins = e.matmul(
                            ps_s[:, 512:1024],
                            s.k_buf[64:128, u % 2, 128 * c:128 * (c + 1)],
                            s.q_buf[64:128, u % 2, qg * NT:(qg + 1) * NT],
                            start=True, stop=True, tile_position=(64, 0))
                    else:
                        if c % 2 == 0 and j >= 2:
                            e.wait_ge(s.s_exp, j - 1)
                        ins = e.matmul(
                            ps_s[:, 512 * (c % 2):512 * (c % 2) + 512],
                            s.k_buf[:, u % 2, 128 * c:128 * (c + 1)],
                            s.q_buf[:, u % 2, qg * NT:(qg + 1) * NT],
                            start=True, stop=True)
                    ins.then_inc(s.s_sc, 1)
                if en == "scalar":
                    if pair or c % 2 == 1:
                        if pair and npt == 4:
                            if gc % 2 == 0:
                                e.wait_ge(s.s_sc, gc + 2)
                                if gc >= 4:
                                    e.wait_ge(s.s_pv, gc - 2)
                        else:
                            e.wait_ge(s.s_sc, gc + 1)
                            if pair:
                                if gc >= 2:
                                    e.wait_ge(s.s_pv, gc - 1)
                            else:
                                if j >= 2:
                                    e.wait_ge(s.s_pv, 2 * j - 2)
                        pslot = (gc % npt) if pair else sb
                        e.activation(s.pt[:, pslot, :], ps_s, AF.Exp,
                                     bias=s.C(-cfg["shift"]),
                                     scale=s.C(cfg["scale"])
                                     ).then_inc(s.s_exp, 1)
                if en == "tensor":
                    if qg == 0 and ((pair and c == 1) or
                                    (not pair and c == 3)):
                        e.wait_ge(s.ld_v2[u % 2],
                                  getattr(s, "v_base", 0)
                                  + 16 * 2 * (u // 2 + 1))
                    if pair and c >= 1:
                        emit_pv(e, s, u, eg, qbuf, c - 1)
                    elif not pair and c % 2 == 1 and c >= 3:
                        emit_pv(e, s, u, eg, qbuf, c - 2)
                    if c == NCH - 1:
                        emit_pv(e, s, u, eg, qbuf, c)
            if en == "tensor" and qg == NTILES - 1:
                e.wait_ge(s.s_pv, NCH * (eg + 1))
                e.nop().then_inc(s.s_unit, 1)
            pvs = s.ps_aux[:, 1024 * qbuf:1024 * qbuf + 1024]
            if en == "tensor":
                pend.append((eg, qbuf))
            if en == "vector":
                e.wait_ge(s.s_pv, NCH * (eg + 1))
                if eg >= 1:
                    e.wait_ge(s.s_att, 2 * eg)
                nr1 = 65 if pair else 64
                e.tensor_copy(s.raw[0:65, 0, :], pvs[0:65, 0:512]
                              ).then_inc(s.s_ep, 1)
                e.tensor_copy(s.raw[0:nr1, 1, :], pvs[0:nr1, 512:1024]
                              ).then_inc(s.s_ep, 1)
                e.wait_ge(s.s_ep, 6 * eg + 2)
                dn0 = s.raw[64:65, 0, :]
                dn1 = s.raw[64:65, 1, :] if pair else s.raw[64:65, 0, :]
                e.reciprocal(s.rrow[0:1, 0, :], dn0).then_inc(s.s_ep, 1)
                e.reciprocal(s.rrow[0:1, 1, :], dn1).then_inc(s.s_ep, 1)
                qsl = slice(qg * NT, (qg + 1) * NT)
                e.wait_ge(s.s_bc, 6 * eg + 5)
                e.tensor_tensor(s.attnout[0:64, u, qsl], s.raw[0:64, 0, :],
                                pvs[0:64, 0:512], ALU.mult).then_inc(s.s_att, 1)
                e.wait_ge(s.s_bc, 6 * eg + 6)
                e.tensor_tensor(s.attnout[64:128, u, qsl], s.raw[0:64, 1, :],
                                pvs[0:64, 512:1024], ALU.mult
                                ).then_inc(s.s_att, 1)
    if en == "tensor":
        flush_pe_epilogue(e)



def setup_B(nc, es, s, P, LN, io):
    x_d = io["x0"]
    q_d = io["q0"]
    kf_d = io["k0g"]
    vg_d = io["v0g"]
    wo_d = io["wout0"]
    x1_d = io["x1"]
    s.x_sb = P.get([128, NK, T], F32)
    s.q_buf = P.get([128, 2, T], BF16)
    s.k_buf = P.get([128, 2, 4 * T], BF16)
    s.vslab = P.get([128, 2, 2, 64, 65], BF16)
    s.pt = P.get([128, 4, 1024], BF16)
    s.raw = P.get([128, 2, 512], F32)
    s.rrow = P.get([1, 2, 512], F32)
    s.attnout = P.get([128, NK, T], BF16)
    s.wo_sb = P.get([128, NK, D], BF16)
    for nm in ("ld_x", "ld_w", "s_mm", "s_ev", "st"):
        setattr(s, nm, es.enter_context(nc.semaphore("B" + nm)))
    _attn_alloc(nc, es, s, "B")

    def dma_unit(e, s, u, buf):
        sem = s.ld_kv2[u % 2]
        e.dma_start(out=s.q_buf[:, buf, :],
                    in_=q_d[128 * u:128 * (u + 1), :]).then_inc(sem, 16)
        if getattr(s, "cck_lvl", None):
            e.wait_ge(s.cck, s.cck_lvl)
        for r in range(4):
            e.dma_start(out=s.k_buf[:, buf, 2048 * r:2048 * (r + 1)],
                        in_=kf_d[1024 * r + 128 * u:1024 * r + 128 * (u + 1),
                                 :]).then_inc(sem, 16)
        if getattr(s, "cc_lvl", None):
            e.wait_ge(s.cc, s.cc_lvl)
        for hh in range(2):
            h = 2 * u + hh
            e.dma_start(out=s.vslab[:, buf, hh, :, 0:64],
                        in_=vg_d[:, 64 * h:64 * (h + 1)].rearrange(
                            "(c p) d -> p c d", p=128)
                        ).then_inc(s.ld_v2[u % 2], 16)

    def ph_init(en, e, s):
        _emit_const_init(en, e, s)
        if en == "vector":
            for b in range(2):
                for hh in range(2):
                    e.memset(s.vslab[:, b, hh, :, 64:65], 1.0)
        if en == "sync":
            dma_unit(e, s, 0, 0)
            e.dma_start(out=s.x_sb[:],
                        in_=x_d.rearrange("(m p) t -> p m t", p=128)
                        ).then_inc(s.ld_x, 16)
        if en == "gpsimd":
            e.dma_start(out=s.wo_sb[:],
                        in_=wo_d.rearrange("(k p) n -> p k n", p=128)
                        ).then_inc(s.ld_w, 16)

    def ph_attn(en, e, s):
        _emit_attn(en, e, s, dict(
            n_units=8, pair=True, shift=L0_SHIFT, scale=L0_SCALE,
            pt_slots=4, dma_unit=dma_unit, dma_per_unit=7))

    def ph_oproj(en, e, s):
        mmc = 0
        evc = 0
        if en == "tensor":
            e.wait_ge(s.ld_w, 16)
        if en == "vector":
            e.wait_ge(s.ld_x, 16)
        for m in range(NK):
            for n in range(NTILES):
                slot = mmc % 2
                ps = s.ps_main[:, slot * NT:(slot + 1) * NT]
                mmc += 1
                if en == "tensor":
                    if mmc > 2:
                        e.wait_ge(s.s_ev, mmc - 2)
                    for k in range(NK):
                        ins = e.matmul(ps,
                                       s.wo_sb[:, k, 128 * m:128 * (m + 1)],
                                       s.attnout[:, k, n * NT:(n + 1) * NT],
                                       start=(k == 0), stop=(k == NK - 1))
                    ins.then_inc(s.s_mm, 1)
                evc += 1
                if en == "vector":
                    e.wait_ge(s.s_mm, evc)
                    e.tensor_tensor(s.x_sb[:, m, n * NT:(n + 1) * NT], ps,
                                    s.x_sb[:, m, n * NT:(n + 1) * NT],
                                    ALU.add).then_inc(s.s_ev, 1)

    def ph_store(en, e, s):
        if en == "sync":
            e.dma_start(out=x1_d.rearrange("(m p) t -> p m t", p=128),
                        in_=s.x_sb[:]).then_inc(s.st, 16)
            e.wait_ge(s.st, 16)

    allp = [ph_init, ph_attn, ph_oproj, ph_store]
    lim = globals().get("_B_LIM")
    return allp[:lim] if lim else allp



H0 = 4096

def setup_C(nc, es, s, P, LN, io, cfg_external_prep=False):
    x_d = io["x1"]
    w1t_d = io["f0w1t"]
    w1r_d = io["f0w1r"]
    b1_d = io["f0b1c"]
    w2t_d = io["f0w2t"]
    w2r_d = io["f0w2r"]
    b2_d = io["f0b2c"]
    x2_d = io["x2"]
    h_d = io["hdr"]
    s.xsl = P.get([128, 2, NK, NT], F32)
    s.x2s = P.get([128, NK, NT], F32)
    s.xln = P.get([128, NK, T], BF16)
    for nm in ("row_ex2", "row_mu", "row_tmp", "row_tmp2", "row_a", "row_b"):
        setattr(s, nm, P.get([1, NT], F32))
    s.app_t1 = P.get([128, NT], F32)
    s.wa = P.get([128, 2, D], F32)
    s.ws = P.get([128, 2, D], F32)
    s.sw1 = P.get([128, NK, H0], BF16)
    s.sw2s = P.get([128, 2, D], BF16)
    s.w2ts = P.get([128, 2, D], F32)
    s.hsl = P.get([128, 2, NT], BF16)
    s.hbuf = P.get([128, 2, NT], BF16)
    s.alpha1 = P.get([128, 32], F32)
    s.alpha2 = P.get([128, 8], F32)
    s.a_tmp = P.get([128, 8], F32)
    s.b1_sb = P.get([128, 32], F32)
    s.b2_sb = P.get([128, 8], F32)
    s.ev_t = P.get([128, NT], F32)
    s.xout = P.get([128, 2, NT], F32)
    for nm in ("ld_b", "s_al", "s_sg1", "s_mm", "s_ev", "s_sg2", "s_mm2",
               "s_ev2"):
        setattr(s, nm, es.enter_context(nc.semaphore("C" + nm)))
    s.s_ch = s.s_al
    for nm in ("ld_a", "ld_s", "ld_x", "ld_h", "ld_w2", "st", "st_h"):
        setattr(s, nm + "2",
                [es.enter_context(nc.semaphore(f"C{nm}{i}"))
                 for i in range(2)])
    LN.take(s, "lnC", NTILES, True, None, False, NK)

    def ph_init(en, e, s):
        _emit_const_init(en, e, s)
        if en == "sync":
            e.dma_start(out=s.b1_sb[:], in_=b1_d[:]).then_inc(s.ld_b, 16)
            e.dma_start(out=s.b2_sb[:], in_=b2_d[:]).then_inc(s.ld_b, 16)
            e.wait_ge(s.ld_b, 32)
        if en == "gpsimd":
            e.dma_start(out=s.wa[:, 0, :], in_=w1r_d[0:128, :]
                        ).then_inc(s.ld_a2[0], 16)
            e.dma_start(out=s.ws[:, 0, 0:D], in_=w1t_d[0:128, 0:1024]
                        ).then_inc(s.ld_s2[0], 16)

    WS_SEQ = [(j, k) for j in range(4) for k in range(NK)]
    AFTER_REDUCE = []
    _cc = 64
    for _q in range(32):
        _cc += 1
        AFTER_REDUCE.append(_cc)
        if _q % 4 == 3:
            _cc += 2

    def prep_dma(e, s):
        def ws_load(ii):
            if ii >= 2:
                e.wait_ge(s.s_sg1, ii - 1)
            j, k = WS_SEQ[ii]
            e.dma_start(out=s.ws[:, ii % 2, :],
                        in_=w1t_d[128 * k:128 * (k + 1),
                                  1024 * j:1024 * (j + 1)]
                        ).then_inc(s.ld_s2[ii % 2], 16)

        def wa_load(m):
            if m >= 2:
                e.wait_ge(s.s_al, 2 * (m - 1))
            e.dma_start(out=s.wa[:, m % 2, :],
                        in_=w1r_d[128 * m:128 * (m + 1), :]
                        ).then_inc(s.ld_a2[m % 2], 16)

        for ii in range(1, 10):
            ws_load(ii)
        for ii in range(10, 32):
            wa_load(ii - 9)
            ws_load(ii)
        for m in range(23, 32):
            wa_load(m)
        for q in range(32):
            i2 = 32 + q
            if q == 0:
                e.wait_ge(s.s_al, 64)
            elif q >= 2:
                e.wait_ge(s.s_al, AFTER_REDUCE[q - 2])
            nm2, nj = divmod(q, 4)
            e.dma_start(out=s.wa[:, i2 % 2, :],
                        in_=w2r_d[128 * nm2:128 * (nm2 + 1),
                                  1024 * nj:1024 * (nj + 1)]
                        ).then_inc(s.ld_a2[i2 % 2], 16)

    def prep_vector(e, s):
        rc = 0
        for m in range(32):
            rc += 2
            e.wait_ge(s.ld_a2[m % 2], 16 * (m // 2 + 1))
            if m >= 1:
                e.wait_ge(s.s_al, 2 * m)
            e.tensor_reduce(s.a_tmp[:, 0:1], s.wa[:, m % 2, :], AX,
                            ALU.add, apply_absolute_value=True
                            ).then_inc(s.s_al, 1)
            e.wait_ge(s.s_al, rc - 1)
            e.tensor_scalar(s.alpha1[:, m:m + 1], s.a_tmp[:, 0:1],
                            s.C(1.0 / D), None, ALU.mult
                            ).then_inc(s.s_al, 1)
        for m in range(NK):
            for j in range(4):
                q = m * 4 + j
                i = 32 + q
                rc += 1
                e.wait_ge(s.ld_a2[i % 2], 16 * (i // 2 + 1))
                if j == 0 and m >= 1:
                    e.wait_ge(s.s_al, rc - 1)
                e.tensor_reduce(s.a_tmp[:, j + 1:j + 2], s.wa[:, i % 2, :],
                                AX, ALU.add, apply_absolute_value=True
                                ).then_inc(s.s_al, 1)
            rc += 2
            e.wait_ge(s.s_al, rc - 2)
            e.tensor_reduce(s.a_tmp[:, 0:1], s.a_tmp[:, 1:5], AX,
                            ALU.add).then_inc(s.s_al, 1)
            e.wait_ge(s.s_al, rc - 1)
            e.tensor_scalar(s.alpha2[:, m:m + 1], s.a_tmp[:, 0:1],
                            s.C(1.0 / H0), None, ALU.mult
                            ).then_inc(s.s_al, 1)

    def prep_sign(e, s, ii):
        j, k = WS_SEQ[ii]
        e.wait_ge(s.ld_s2[ii % 2], 16 * (ii // 2 + 1))
        e.activation(s.sw1[:, k, 1024 * j:1024 * (j + 1)],
                     s.ws[:, ii % 2, :], AF.Sign).then_inc(s.s_sg1, 1)

    def pre_lnC(en, e, s, n):
        if en == "sync":
            if n == 0:
                e.dma_start(out=s.xsl[:, 0], in_=x_d[:, 0:NT].rearrange(
                    "(m p) t -> p m t", p=128)).then_inc(s.ld_x2[0], 16)
            if n + 1 < NTILES:
                if n >= 1:
                    e.wait_ge(s.lnC_app, s.lnC_bases["app"] + n)
                e.dma_start(out=s.xsl[:, (n + 1) % 2],
                            in_=x_d[:, (n + 1) * NT:(n + 2) * NT].rearrange(
                                "(m p) t -> p m t", p=128)
                            ).then_inc(s.ld_x2[(n + 1) % 2], 16)
        if en in ("vector", "tensor", "scalar"):
            e.wait_ge(s.ld_x2[n % 2], 16 * (n // 2 + 1))

    def ph_ln(en, e, s):
        _emit_ln(en, e, s, dict(
            prefix="lnC", eps=1e-5, sub_mean=True,
            src=lambda k, n: s.xsl[:, n % 2, k, :],
            src_all=lambda n: s.xsl[:, n % 2, :, :],
            dst=lambda k, n: s.xln[:, k, n * NT:(n + 1) * NT],
            pre=pre_lnC,
        ))

    def ph_w1(en, e, s):
        if en == "gpsimd":
            prep_dma(e, s)
        if en == "vector":
            prep_vector(e, s)
        if en == "scalar":
            for ii in range(8):
                prep_sign(e, s, ii)
        mmc = 0
        for n in range(NTILES):
            for m in range(32):
                slot = mmc % 2
                ps = s.ps_main[:, slot * NT:(slot + 1) * NT]
                mmc += 1
                if en == "tensor":
                    if n == 0 and m % 8 == 0:
                        e.wait_ge(s.s_sg1, 8 * (m // 8 + 1))
                    if mmc > 2:
                        e.wait_ge(s.s_ev, mmc - 2)
                    for k in range(NK):
                        ins = e.matmul(ps,
                                       s.sw1[:, k, 128 * m:128 * (m + 1)],
                                       s.xln[:, k, n * NT:(n + 1) * NT],
                                       start=(k == 0), stop=(k == NK - 1))
                    ins.then_inc(s.s_mm, 1)
                if en == "scalar":
                    if n == 0 and m < 24:
                        prep_sign(e, s, m + 8)
                    e.wait_ge(s.s_mm, mmc)
                    if n == 0:
                        e.wait_ge(s.s_al, 2 * (m + 1))
                    if mmc > 2:
                        e.wait_ge(s.st_h2[(mmc - 1) % 2],
                                  16 * ((mmc - 1) // 2))
                    e.activation(s.hbuf[:, slot, :], ps, AF.Gelu,
                                 bias=s.b1_sb[:, m:m + 1],
                                 scale=s.alpha1[:, m:m + 1]
                                 ).then_inc(s.s_ev, 1)
                if en == "sync":
                    e.wait_ge(s.s_ev, mmc)
                    e.dma_start(out=h_d[128 * m:128 * (m + 1),
                                        n * NT:(n + 1) * NT],
                                in_=s.hbuf[:, slot, :]
                                ).then_inc(s.st_h2[(mmc - 1) % 2], 16)
        if en == "sync":
            e.wait_ge(s.st_h2[0], 16 * (mmc // 2))
            e.wait_ge(s.st_h2[1], 16 * (mmc // 2))

    def ph_w2(en, e, s):
        idx = 0
        for n in range(NTILES):
            for k2 in range(32):
                idx += 1
                slot = idx % 2
                if en == "gpsimd":
                    if idx > 2:
                        e.wait_ge(s.s_sg2, idx - 2)
                    e.dma_start(out=s.w2ts[:, slot, :],
                                in_=w2t_d[128 * k2:128 * (k2 + 1), :]
                                ).then_inc(s.ld_w22[slot], 16)
                if en == "scalar":
                    e.wait_ge(s.ld_w22[slot], 16 * ((idx - 1) // 2 + 1))
                    if idx > 2:
                        e.wait_ge(s.s_mm2, 8 * (idx - 2))
                    e.activation(s.sw2s[:, slot, :], s.w2ts[:, slot, :],
                                 AF.Sign).then_inc(s.s_sg2, 1)
                if en == "sync":
                    if idx > 2:
                        e.wait_ge(s.s_mm2, 8 * (idx - 2))
                    e.dma_start(out=s.hsl[:, slot, :],
                                in_=h_d[128 * k2:128 * (k2 + 1),
                                        n * NT:(n + 1) * NT]
                                ).then_inc(s.ld_h2[slot], 16)
                if en == "tensor":
                    e.wait_ge(s.s_sg2, idx)
                    e.wait_ge(s.ld_h2[slot], 16 * ((idx - 1) // 2 + 1))
                    if k2 == 0 and n > 0:
                        e.wait_ge(s.s_ev2, 8 * n)
                    for m2 in range(NK):
                        arena = s.ps_main if m2 < 4 else s.ps_aux
                        acc = arena[:, 512 * (m2 % 4):512 * (m2 % 4) + 512]
                        ins = e.matmul(acc,
                                       s.sw2s[:, slot, 128 * m2:128 * (m2 + 1)],
                                       s.hsl[:, slot, :],
                                       start=(k2 == 0), stop=(k2 == 31))
                        if m2 == NK - 1:
                            ins.then_inc(s.s_mm2, 8)
            if en == "sync":
                if n >= 2:
                    e.wait_ge(s.s_ev2, 8 * (n - 1))
                e.dma_start(out=s.xsl[:, n % 2],
                            in_=x_d[:, n * NT:(n + 1) * NT].rearrange(
                                "(m p) t -> p m t", p=128)
                            ).then_inc(s.ld_x2[n % 2], 16)
            for m2 in range(NK):
                g = n * NK + m2 + 1
                if en == "vector":
                    if n == 0:
                        e.wait_ge(s.s_al, 64 + 6 * (m2 + 1))
                    if m2 == 0:
                        e.wait_ge(s.s_mm2, 8 * idx)
                        e.wait_ge(s.ld_x2[n % 2], 16 * (2 + n // 2 + 1))
                    if g > 2:
                        e.wait_ge(s.st2[g % 2], 16 * ((g - 2 + 1) // 2))
                    arena = s.ps_main if m2 < 4 else s.ps_aux
                    acc = arena[:, 512 * (m2 % 4):512 * (m2 % 4) + 512]
                    if g >= 2:
                        e.wait_ge(s.s_ev2, g - 1)
                    e.tensor_scalar(s.ev_t[:], acc, s.alpha2[:, m2:m2 + 1],
                                    s.b2_sb[:, m2:m2 + 1], ALU.mult, ALU.add
                                    ).then_inc(s.s_ch, 1)
                    e.wait_ge(s.s_ch, 112 + g)
                    ins = e.tensor_tensor(s.xout[:, g % 2, :], s.ev_t[:],
                                          s.xsl[:, n % 2, m2, :], ALU.add)
                    ins.then_inc(s.s_ev2, 1)
                if en == "sync":
                    e.wait_ge(s.s_ev2, g)
                    e.dma_start(out=x2_d[128 * m2:128 * (m2 + 1),
                                         n * NT:(n + 1) * NT],
                                in_=s.xout[:, g % 2, :]
                                ).then_inc(s.st2[g % 2], 16)
        if en == "sync":
            e.wait_ge(s.st2[0], 16 * NTILES * NK // 2)
            e.wait_ge(s.st2[1], 16 * NTILES * NK // 2)

    return [ph_init, ph_ln, ph_w1, ph_w2]


def _stage_C(inputs, x1_by_core):
    w1 = np.asarray(inputs["L0_ffn_w1"], np.float32)
    w2 = np.asarray(inputs["L0_ffn_w2"], np.float32)
    b1 = np.asarray(inputs["L0_ffn_b1"], np.float32)
    b2 = np.asarray(inputs["L0_ffn_b2"], np.float32)
    com = {
        "w1t": np.ascontiguousarray(w1.T), "w1r": w1,
        "b1c": np.ascontiguousarray(b1.reshape(32, 128).T),
        "w2t": np.ascontiguousarray(w2.T), "w2r": w2,
        "b2c": np.ascontiguousarray(b2.reshape(8, 128).T),
    }
    return [dict(com, x=np.asarray(x1_by_core[c])) for c in range(NCORE)]



def setup_D(nc, es, s, P, LN, io):
    x_d = io["x2"]
    w_d = io["wqkv1"]
    b_d = io["bqkv1"]
    lam_d = io["lams"]
    id_d = io["ident"]
    q_d = io["q1"]
    k_d = io["k1"]
    v_d = io["ve1"]
    s.xsl = P.get([128, 2, NK, NT], F32)
    s.x2s = P.get([128, NK, NT], F32)
    s.xln = P.get([128, NK, T], BF16)
    for nm in ("row_ex2", "row_mu", "row_tmp", "row_tmp2", "row_a", "row_b"):
        setattr(s, nm, P.get([1, NT], F32))
    s.app_t1 = P.get([128, NT], F32)
    s.w_sb = P.get([128, 2, NK, D], BF16)
    s.b_sb = P.get([128, 3, 8], F32)
    s.lam_sb = P.get([128, 4], F32)
    s.lamexp = P.get([1, 2], F32)
    s.lamrow = P.get([1, 2], F32)
    s.neglam = P.get([128, 1], F32)
    s.out_buf = P.get([128, 2, T], BF16)
    s.vfm = P.get([128, NK, T], BF16)
    s.veff_fm = P.get([128, 4, T], BF16)
    s.vtm = P.get([128, T // 128, 512], BF16)
    s.ident = P.get([128, 128], BF16)
    s.ld_x2 = [es.enter_context(nc.semaphore(f"Dld_x{i}")) for i in range(2)]
    s.ld_w2p = [es.enter_context(nc.semaphore(f"Dld_w{i}")) for i in range(2)]
    s.ld_id = es.enter_context(nc.semaphore("Dld_id"))
    for nm in ("s_mm", "s_ev", "s_tr", "s_lam"):
        setattr(s, nm, es.enter_context(nc.semaphore("D" + nm)))
    s.s_cp = s.s_lam
    if getattr(s, "st2", None) is None:
        s.st2 = [es.enter_context(nc.semaphore(f"st2_{i}"))
                 for i in range(2)]
        s.st_vv = es.enter_context(nc.semaphore("st_vv"))
        s.st2_base = [0, 0]
        s.stv_base = 0
    LN.take(s, "lnD", NTILES, True, None, False, NK)

    def ph_init(en, e, s):
        _emit_const_init(en, e, s)
        if en == "sync":
            e.dma_start(out=s.ident[:], in_=id_d[:]).then_inc(s.ld_id, 16)
            e.dma_start(out=s.b_sb[:], in_=b_d.rearrange("j p c -> p j c")
                        ).then_inc(s.ld_id, 16)
            e.dma_start(out=s.lam_sb[:], in_=lam_d[:]).then_inc(s.ld_id, 16)
            e.dma_start(out=s.xsl[:, 0], in_=x_d[:, 0:NT].rearrange(
                "(m p) t -> p m t", p=128)).then_inc(s.ld_x2[0], 16)
            e.wait_ge(s.ld_id, 48)
        if en == "gpsimd":
            e.dma_start(out=s.w_sb[:, 0],
                        in_=w_d[1].rearrange("(k p) n -> p k n", p=128)
                        ).then_inc(s.ld_w2p[0], 16)

    def pre_ln(en, e, s, n):
        if en == "sync" and n + 1 < NTILES:
            if n >= 1:
                e.wait_ge(s.lnD_app, s.lnD_bases["app"] + n)
            e.dma_start(out=s.xsl[:, (n + 1) % 2],
                        in_=x_d[:, (n + 1) * NT:(n + 2) * NT].rearrange(
                            "(m p) t -> p m t", p=128)
                        ).then_inc(s.ld_x2[(n + 1) % 2], 16)
        if en in ("vector", "tensor", "scalar"):
            e.wait_ge(s.ld_x2[n % 2], 16 * (n // 2 + 1))

    def ph_ln(en, e, s):
        _emit_ln(en, e, s, dict(
            prefix="lnD", eps=1e-5, sub_mean=True,
            src=lambda k, n: s.xsl[:, n % 2, k, :],
            src_all=lambda n: s.xsl[:, n % 2, :, :],
            dst=lambda k, n: s.xln[:, k, n * NT:(n + 1) * NT],
            pre=pre_ln,
        ))

    def ph_lam(en, e, s):
        if en == "tensor":
            e.matmul(s.ps_aux[0:1, 0:1], s.lam_sb[:, 0:1], s.lam_sb[:, 1:2],
                     start=True, stop=True)
            e.matmul(s.ps_aux[0:1, 1:2], s.lam_sb[:, 2:3], s.lam_sb[:, 3:4],
                     start=True, stop=True).then_inc(s.s_lam, 1)
        if en == "scalar":
            e.wait_ge(s.s_lam, 1)
            e.activation(s.lamexp[:], s.ps_aux[0:1, 0:2], AF.Exp
                         ).then_inc(s.s_lam, 1)
        if en == "vector":
            e.wait_ge(s.s_lam, 2)
            e.tensor_tensor(s.lamrow[0:1, 0:1], s.lamexp[0:1, 0:1],
                            s.lamexp[0:1, 1:2], ALU.subtract
                            ).then_inc(s.s_lam, 1)
            e.wait_ge(s.s_lam, 3)
            e.tensor_scalar(s.lamrow[0:1, 1:2], s.lamrow[0:1, 0:1],
                            s.C1(0.8), s.C1(-1.0), ALU.add, ALU.mult
                            ).then_inc(s.s_lam, 1)
        if en == "tensor":
            e.wait_ge(s.s_lam, 4)
            e.matmul(s.ps_aux[0:128, 512:513], s.ones_row[:],
                     s.lamrow[0:1, 1:2], start=True, stop=True
                     ).then_inc(s.s_lam, 1)
        if en == "vector":
            e.wait_ge(s.s_lam, 5)
            e.tensor_copy(s.neglam[:], s.ps_aux[:, 512:513]
                          ).then_inc(s.s_lam, 1)

    def ph_proj(en, e, s):
        ORD = [1, 0, 2]
        KIND = ["k", "q", "v"]
        mmc = 0
        evc = 0
        stc = 0
        pb = s.st2_base
        for j in range(3):
            kind = KIND[j]
            if en == "gpsimd" and j + 1 < 3:
                if j > 0:
                    e.wait_ge(s.s_mm, 32 * j)
                e.dma_start(out=s.w_sb[:, (j + 1) % 2],
                            in_=w_d[ORD[j + 1]].rearrange(
                                "(k p) n -> p k n", p=128)
                            ).then_inc(s.ld_w2p[(j + 1) % 2], 16)
            for m in range(NK):
                for n in range(NTILES):
                    slot = mmc % 2
                    ps = s.ps_main[:, slot * NT:(slot + 1) * NT]
                    mmc += 1
                    if en == "tensor":
                        if n == 0 and m == 0:
                            e.wait_ge(s.ld_w2p[j % 2], 16 * (j // 2 + 1))
                        if mmc > 2:
                            e.wait_ge(s.s_ev, mmc - 2)
                        for k in range(NK):
                            ins = e.matmul(
                                ps, s.w_sb[:, j % 2, k, 128 * m:128 * (m + 1)],
                                s.xln[:, k, n * NT:(n + 1) * NT],
                                start=(k == 0), stop=(k == NK - 1))
                        ins.then_inc(s.s_mm, 1)
                    evc += 1
                    if en == "scalar":
                        e.wait_ge(s.s_mm, evc)
                        if kind != "v":
                            if n == 0:
                                sn = (0 if kind == "k" else 8) + m
                                if sn >= 2:
                                    e.wait_ge(s.st2[sn % 2],
                                              pb[sn % 2] + 16 * (sn // 2))
                            dst = s.out_buf[:, m % 2, n * NT:(n + 1) * NT]
                        else:
                            dst = s.vfm[:, m, n * NT:(n + 1) * NT]
                        e.activation(dst, ps, AF.Identity,
                                     bias=s.b_sb[:, ORD[j], m:m + 1]
                                     ).then_inc(s.s_ev, 1)
                if kind != "v":
                    stc += 1
                    if en == "sync":
                        e.wait_ge(s.s_ev, evc)
                        d = k_d if kind == "k" else q_d
                        e.dma_start(out=d[128 * m:128 * (m + 1), :],
                                    in_=s.out_buf[:, m % 2]
                                    ).then_inc(s.st2[(stc - 1) % 2], 16)
            if kind == "k" and en == "sync":
                e.wait_ge(s.st2[0], pb[0] + 16 * 4)
                e.wait_ge(s.st2[1], pb[1] + 16 * 4)
            if kind == "k" and en == "gpsimd" and getattr(s, "kgather", None):
                ksrc, kdst, lvl = s.kgather
                e.wait_ge(s.st2[0], pb[0] + 16 * 4)
                e.wait_ge(s.st2[1], pb[1] + 16 * 4)
                _issue_gather(e, s, s.io, s.cc, ksrc, kdst, s.sim_local)

    def ph_veff(en, e, s):
        vc = 0
        for i in range(4):
            vc += 1
            if en == "vector":
                e.scalar_tensor_tensor(s.veff_fm[:, i, :], s.vfm[:, i + 4, :],
                                       s.neglam[:], s.vfm[:, i, :],
                                       ALU.mult, ALU.add).then_inc(s.s_ev, 1)
        trc = 0
        cpc = 0
        for i in range(4):
            for tb in range(T // 128):
                trc += 1
                if en == "tensor":
                    e.wait_ge(s.s_ev, 96 + 4)
                    if trc > 1:
                        e.wait_ge(s.s_cp, 6 + trc - 1)
                    e.transpose(s.ps_t[:, 0:128],
                                s.veff_fm[:, i, 128 * tb:128 * (tb + 1)],
                                s.ident[:]).then_inc(s.s_tr, 1)
                cpc += 1
                if en == "vector":
                    e.wait_ge(s.s_tr, trc)
                    e.tensor_copy(s.vtm[:, tb, 128 * i:128 * (i + 1)],
                                  s.ps_t[:, 0:128]).then_inc(s.s_cp, 1)

    def ph_store(en, e, s):
        if en == "sync":
            e.dma_start(out=v_d.rearrange("(tb p) d -> p tb d", p=128),
                        in_=s.vtm[:]).then_inc(s.st_vv, 16)
            e.wait_ge(s.st2[0], s.st2_base[0] + 16 * 8)
            e.wait_ge(s.st2[1], s.st2_base[1] + 16 * 8)
            e.wait_ge(s.st_vv, s.stv_base + 16)

    return [ph_init, ph_ln, ph_lam, ph_proj, ph_veff, ph_store]


def _stage_D(inputs, x2_by_core):
    w3 = np.ascontiguousarray(np.stack([
        np.asarray(inputs["L1_attn_Wq"], np.float32),
        np.asarray(inputs["L1_attn_Wk"], np.float32),
        np.asarray(inputs["L1_attn_Wv"], np.float32)]))
    b3 = np.ascontiguousarray(np.stack([
        np.asarray(inputs["L1_attn_bq"], np.float32).reshape(8, 128).T,
        np.asarray(inputs["L1_attn_bk"], np.float32).reshape(8, 128).T,
        np.asarray(inputs["L1_attn_bv"], np.float32).reshape(8, 128).T]))
    lams = np.ascontiguousarray(np.stack(
        [np.asarray(inputs[f"L1_attn_lambda_{nm}"], np.float32)
         for nm in ("q1", "k1", "q2", "k2")], axis=1))
    ident = np.eye(128, dtype=bf16)
    com = {"wqkv": w3, "bqkv": b3, "lams": lams, "ident": ident}
    return [dict(com, x=np.asarray(x2_by_core[c])) for c in range(NCORE)]



def setup_E(nc, es, s, P, LN, io):
    x_d = io["x2"]
    q_d = io["q1"]
    kf_d = io["k1g"]
    veg_d = io["ve1g"]
    wo_d = io["wo1"]
    bo_d = io["boc"]
    lnw_d = io["lnwc"]
    lnb_d = io["lnbc"]
    x3_d = io["x3"]
    s.q_buf = P.get([128, 2, T], BF16)
    s.k_buf = P.get([128, 2, 4 * T], BF16)
    s.vslab = P.get([128, 2, 1, 64, 129], BF16)
    s.pt = P.get([128, 2, 1024], BF16)
    s.raw = P.get([128, 2, 512], F32)
    s.rrow = P.get([1, 2, 512], F32)
    s.attnout = P.get([128, NK, T], BF16)
    s.x2s = P.get([128, NK, NT], F32)
    for nm in ("row_ex2", "row_mu", "row_tmp", "row_tmp2", "row_a", "row_b"):
        setattr(s, nm, P.get([1, NT], F32))
    s.app_t1 = P.get([128, NT], F32)
    s.app_t2 = P.get([128, NT], F32)
    s.wo_sb = P.get([128, NK, D], BF16)
    s.bo_sb = P.get([128, 8], F32)
    s.lnw_sb = P.get([128, 8], F32)
    s.lnb_sb = P.get([128, 8], F32)
    s.lnb02 = P.get([128, 8], F32)
    s.xsl = P.get([128, 2, NK, NT], F32)
    s.ev_t = P.get([128, NT], F32)
    s.xout = P.get([128, 2, NT], F32)
    s.ld_x2 = [es.enter_context(nc.semaphore(f"Eld_x{i}")) for i in range(2)]
    s.ld_w = es.enter_context(nc.semaphore("Eld_w"))
    s.ld_wo = es.enter_context(nc.semaphore("Eld_wo"))
    for nm in ("s_mm", "s_ev", "s_ch"):
        setattr(s, nm, es.enter_context(nc.semaphore("E" + nm)))
    s.st2 = [es.enter_context(nc.semaphore(f"Est{i}")) for i in range(2)]
    _attn_alloc(nc, es, s, "E")
    LN.take(s, "lnE", NTILES, True, 0.2, True, NK)

    def dma_unit(e, s, u, buf):
        sem = s.ld_kv2[u % 2]
        e.dma_start(out=s.q_buf[:, buf, :],
                    in_=q_d[128 * u:128 * (u + 1), :]).then_inc(sem, 16)
        if getattr(s, "cck_lvl", None):
            e.wait_ge(s.cck, s.cck_lvl)
        for r in range(4):
            e.dma_start(out=s.k_buf[:, buf, 2048 * r:2048 * (r + 1)],
                        in_=kf_d[1024 * r + 128 * u:1024 * r + 128 * (u + 1),
                                 :]).then_inc(sem, 16)
        if getattr(s, "cc_lvl", None):
            e.wait_ge(s.cc, s.cc_lvl)
        m = u % 4
        e.dma_start(out=s.vslab[:, buf, 0, :, 0:64],
                    in_=veg_d[:, 128 * m:128 * m + 64].rearrange(
                        "(c p) d -> p c d", p=128)).then_inc(s.ld_v2[u % 2], 16)
        e.dma_start(out=s.vslab[:, buf, 0, :, 65:129],
                    in_=veg_d[:, 128 * m + 64:128 * (m + 1)].rearrange(
                        "(c p) d -> p c d", p=128)).then_inc(s.ld_v2[u % 2], 16)

    def ph_init(en, e, s):
        _emit_const_init(en, e, s)
        if en == "vector":
            for b in range(2):
                e.memset(s.vslab[:, b, 0, :, 64:65], 1.0)
        if en == "sync":
            e.dma_start(out=s.bo_sb[:], in_=bo_d[:]).then_inc(s.ld_w, 16)
            e.dma_start(out=s.lnw_sb[:], in_=lnw_d[:]).then_inc(s.ld_w, 16)
            e.dma_start(out=s.lnb_sb[:], in_=lnb_d[:]).then_inc(s.ld_w, 16)
            e.wait_ge(s.ld_w, 48)
        if en == "vector":
            e.wait_ge(s.ld_w, 48)
            e.tensor_scalar(s.lnb02[:], s.lnb_sb[:], s.C(0.2), None,
                            ALU.mult)
        if en == "gpsimd":
            e.dma_start(out=s.wo_sb[:],
                        in_=wo_d.rearrange("(k p) n -> p k n", p=128)
                        ).then_inc(s.ld_wo, 16)

    def ph_attn(en, e, s):
        if en == "gpsimd" and getattr(s, "vgather", None):
            _issue_gather(e, s, s.io, s.cc, *s.vgather, s.sim_local)
        if en == "sync":
            dma_unit(e, s, 0, 0)
        _emit_attn(en, e, s, dict(
            n_units=8, pair=False, shift=L1_SHIFT, scale=L1_SCALE,
            dma_unit=dma_unit, dma_per_unit=7))

    def ph_innerln(en, e, s):
        _emit_ln(en, e, s, dict(
            prefix="lnE", eps=1e-5, sub_mean=True, src_bf16=True,
            post_scale=0.2,
            src=lambda k, n: s.attnout[:, k, n * NT:(n + 1) * NT],
            src_all=lambda n: s.attnout[:, :, n * NT:(n + 1) * NT],
            dst=lambda k, n: s.attnout[:, k, n * NT:(n + 1) * NT],
            lnw_col=lambda k: s.lnw_sb[:, k:k + 1],
            lnb_col=lambda k: s.lnb02[:, k:k + 1],
        ))

    def ph_oproj(en, e, s):
        mmc = 0
        for n in range(NTILES):
            if en == "sync":
                if n >= 2:
                    e.wait_ge(s.s_ev, 8 * (n - 1))
                e.dma_start(out=s.xsl[:, n % 2],
                            in_=x_d[:, n * NT:(n + 1) * NT].rearrange(
                                "(m p) t -> p m t", p=128)
                            ).then_inc(s.ld_x2[n % 2], 16)
            for m in range(NK):
                g = n * NK + m + 1
                slot = mmc % 2
                ps = s.ps_main[:, slot * NT:(slot + 1) * NT]
                mmc += 1
                if en == "tensor":
                    if mmc == 1:
                        e.wait_ge(s.ld_wo, 16)
                    if mmc > 2:
                        e.wait_ge(s.s_ev, mmc - 2)
                    for k in range(NK):
                        ins = e.matmul(ps,
                                       s.wo_sb[:, k, 128 * m:128 * (m + 1)],
                                       s.attnout[:, k, n * NT:(n + 1) * NT],
                                       start=(k == 0), stop=(k == NK - 1))
                    ins.then_inc(s.s_mm, 1)
                if en == "vector":
                    if m == 0:
                        e.wait_ge(s.ld_x2[n % 2], 16 * (n // 2 + 1))
                    e.wait_ge(s.s_mm, g)
                    if g > 2:
                        e.wait_ge(s.st2[g % 2], 16 * ((g - 1) // 2))
                    if g >= 2:
                        e.wait_ge(s.s_ev, g - 1)
                    e.tensor_scalar(s.ev_t[:], ps, s.bo_sb[:, m:m + 1], None,
                                    ALU.add).then_inc(s.s_ch, 1)
                    e.wait_ge(s.s_ch, g)
                    e.tensor_tensor(s.xout[:, g % 2, :], s.ev_t[:],
                                    s.xsl[:, n % 2, m, :], ALU.add
                                    ).then_inc(s.s_ev, 1)
                if en == "sync":
                    e.wait_ge(s.s_ev, g)
                    e.dma_start(out=x3_d[128 * m:128 * (m + 1),
                                         n * NT:(n + 1) * NT],
                                in_=s.xout[:, g % 2, :]
                                ).then_inc(s.st2[g % 2], 16)
        if en == "sync":
            e.wait_ge(s.st2[0], 16 * 16)
            e.wait_ge(s.st2[1], 16 * 16)

    return [ph_init, ph_attn, ph_innerln, ph_oproj]


def _stage_E(inputs, resD):
    Wo = np.asarray(inputs["L1_attn_Wo"], np.float32)
    bo = np.asarray(inputs["L1_attn_bo"], np.float32).reshape(8, 128).T
    lnw = np.asarray(inputs["L1_attn_ln_w"], np.float32).reshape(8, 128).T
    lnb = np.asarray(inputs["L1_attn_ln_b"], np.float32).reshape(8, 128).T
    com = {"wo": Wo, "boc": np.ascontiguousarray(bo),
           "lnwc": np.ascontiguousarray(lnw),
           "lnbc": np.ascontiguousarray(lnb)}
    maps = []
    for g in range(2):
        cs = range(4 * g, 4 * g + 4)
        kf = np.concatenate([np.asarray(resD[c]["k"]) for c in cs], axis=1)
        vf = np.concatenate([np.asarray(resD[c]["veff"]) for c in cs], axis=0)
        v4 = vf.reshape(64, 128, 4, 128)
        vsa = np.ascontiguousarray(v4[:, :, :, 0:64].transpose(2, 1, 0, 3))
        vsb = np.ascontiguousarray(v4[:, :, :, 64:128].transpose(2, 1, 0, 3))
        for c in cs:
            maps.append(dict(com, q=np.asarray(resD[c]["q"]), kf=kf,
                             vsa=vsa, vsb=vsb))
    return maps



H1 = 2048
TH = 1024
NTH = TH // NT


def setup_F(nc, es, s, P, LN, io):
    x_d = io["x3"]
    nw_d = io["nwc"]
    gw_d = io["gwt"]
    w1_d = io["mw1t"]
    b1_d = io["mb1c"]
    w2_d = io["mw2t"]
    b2_d = io["mb2r"]
    sel_d = io["sel8"]
    x4_d = io["x4"]
    s.acc = P.get([128, NK, TH], F32)
    s.xln_sb = P.get([128, NK, TH], BF16)
    s.t_sb = s.xln_sb
    s.x2s = P.get([128, NK, NT], F32)
    for nm in ("row_ex2", "row_mu", "row_tmp", "row_tmp2", "row_a", "row_b"):
        setattr(s, nm, P.get([1, NT], F32))
    s.app_t1 = P.get([128, NT], F32)
    s.w1_sb = P.get([128, NK, H1], BF16)
    s.w2_sb = P.get([128, 16, D], BF16)
    s.h_sb = P.get([128, 16, NT], BF16)
    s.gw_sb = P.get([128, NK, 8], BF16)
    s.b1_sb = P.get([128, 8, 16], F32)
    s.b2_sb = P.get([8, D], BF16)
    s.nw_sb = P.get([128, 8], F32)
    s.sel_sb = P.get([8, 8, 128], BF16)
    s.g_sb = P.get([8, NT], F32)
    s.gtmp = P.get([8, NT], F32)
    s.gm4 = P.get([4, NT], F32)
    s.gm2 = P.get([2, NT], F32)
    s.m1_row = P.get([1, NT], F32)
    s.m2_row = P.get([1, NT], F32)
    s.e_row = P.get([1, NT], F32)
    s.r1_row = P.get([1, NT], F32)
    s.r2_row = P.get([1, NT], F32)
    s.eq1 = P.get([8, NT], F32)
    s.eq2 = P.get([8, NT], F32)
    s.msk = P.get([8, NT], F32)
    s.ct1 = P.get([8, NT], F32)
    s.c_bf = P.get([8, NTH, NT], BF16)
    s.cbc_sb = P.get([128, NT], F32)
    s.ev_t = P.get([128, NT], F32)
    for nm in ("ld_c", "st_x4", "s_g", "s_gv",
               "s_gb", "s_b2", "s_b2e", "s_mm1", "s_gel", "s_mm2", "s_ev2",
               "s_ch", "s_cbc", "s_cbcmm", "ld_cast", "s_addln",
               "ld_w1", "ld_w2x"):
        setattr(s, nm, es.enter_context(nc.semaphore("G" + nm)))
    s.ld_acc = s.ld_c
    s.sh = s.ld_cast
    for half in range(2):
        LN.take(s, f"lnf{half}", NTH, True, None, False, NK)
        LN.take(s, f"rms{half}", NTH, False, None, False, NK)

    def ph_init_g(en, e, s):
        _emit_const_init(en, e, s)
        if en == "sync":
            e.dma_start(out=s.nw_sb[:], in_=nw_d[:]).then_inc(s.ld_c, 16)
            e.dma_start(out=s.b1_sb[:], in_=b1_d.rearrange("e p c -> p e c")
                        ).then_inc(s.ld_c, 16)
            e.dma_start(out=s.sel_sb[:], in_=sel_d[:]).then_inc(s.ld_c, 16)
            e.dma_start(out=s.acc[:], in_=x_d[:, 0:TH].rearrange(
                "(m p) t -> p m t", p=128)).then_inc(s.ld_acc, 16)
            e.wait_ge(s.ld_c, 64)
        if en == "gpsimd":
            e.dma_start(out=s.gw_sb[:], in_=gw_d.rearrange(
                "(k p) c -> p k c", p=128)).then_inc(s.ld_cast, 16)
            e.dma_start(out=s.b2_sb[:], in_=b2_d[:]).then_inc(s.ld_cast, 16)
            e.dma_start(out=s.w1_sb[:], in_=w1_d[0].rearrange(
                "(k p) h -> p k h", p=128)).then_inc(s.ld_w1, 16)
            e.dma_start(out=s.w2_sb[:], in_=w2_d[0].rearrange(
                "(k p) d -> p k d", p=128)).then_inc(s.ld_w2x, 16)
            e.wait_ge(s.ld_cast, 32)

    def mk_init_h(half):
        def ph(en, e, s):
            if half == 1 and en == "sync":
                e.dma_start(out=s.acc[:], in_=x_d[:, TH:2 * TH].rearrange(
                    "(m p) t -> p m t", p=128)).then_inc(s.ld_acc, 16)
                e.wait_ge(s.ld_acc, 80)
        return ph

    def mk_preln(half):
        def ph(en, e, s):
            _emit_ln(en, e, s, dict(
                prefix=f"lnf{half}", eps=1e-5, sub_mean=True, ntiles=NTH,
                src=lambda k, n: s.acc[:, k, n * NT:(n + 1) * NT],
                src_all=lambda n: s.acc[:, :, n * NT:(n + 1) * NT],
                dst=lambda k, n: s.xln_sb[:, k, n * NT:(n + 1) * NT],
            ))
        return ph

    def mk_addln(half):
        def ph(en, e, s):
            cnt = 16 * half
            for n in range(NTH):
                for k in range(NK):
                    cnt += 1
                    if en == "vector":
                        nsl = slice(n * NT, (n + 1) * NT)
                        e.tensor_tensor(s.acc[:, k, nsl], s.xln_sb[:, k, nsl],
                                        s.acc[:, k, nsl], ALU.add
                                        ).then_inc(s.s_addln, 1)
        return ph

    def mk_rms(half):
        def ph(en, e, s):
            _emit_ln(en, e, s, dict(
                prefix=f"rms{half}", eps=1e-6, sub_mean=False, ntiles=NTH,
                src_bf16=True,
                src=lambda k, n: s.xln_sb[:, k, n * NT:(n + 1) * NT],
                src_all=lambda n: s.xln_sb[:, :, n * NT:(n + 1) * NT],
                dst=lambda k, n: s.xln_sb[:, k, n * NT:(n + 1) * NT],
                weight_col=lambda k: s.nw_sb[:, k:k + 1],
            ))
        return ph

    GATE_P = 18

    def mk_gate(half):
        def ph(en, e, s):
            ps_g = s.ps_main[0:8, 0:512]
            ps_b = s.ps_main[:, 512:1024]
            gvc = GATE_P * half * NTH
            shc = 6 * half * NTH
            gbc = 4 * half * NTH
            b2c = NK * half * NTH
            for n in range(NTH):
                gn = half * NTH + n
                if en == "tensor":
                    if gn > 0:
                        e.wait_ge(s.s_gv, GATE_P * (gn - 1) + 1)
                    for k in range(NK):
                        ins = e.matmul(ps_g, s.gw_sb[:, k, :],
                                       s.t_sb[:, k, n * NT:(n + 1) * NT],
                                       start=(k == 0), stop=(k == NK - 1))
                    ins.then_inc(s.s_g, 1)

                def dve(op):
                    nonlocal gvc
                    if en == "vector":
                        e.wait_ge(s.s_gv, gvc)
                        op().then_inc(s.s_gv, 1)
                    gvc += 1

                def shift(src_ap, dst_ap):
                    nonlocal shc
                    shc += 1
                    if en == "gpsimd":
                        e.wait_ge(s.s_gv, gvc)
                        e.dma_start(out=dst_ap, in_=src_ap).then_inc(s.sh, 16)


                def bcast(row):
                    nonlocal gbc
                    gbc += 1
                    if en == "tensor":
                        e.wait_ge(s.s_gv, gvc)
                        e.matmul(ps_b[0:8, :], s.ones_row[0:1, 0:8], row,
                                 start=True, stop=True).then_inc(s.s_gb, 1)
                    if en == "vector":
                        e.wait_ge(s.s_gb, gbc)

                if en == "vector":
                    e.wait_ge(s.s_g, gn + 1)
                dve(lambda: e.tensor_copy(s.g_sb[:], ps_g))
                shift(s.g_sb[4:8, :], s.gtmp[0:4, :])
                if en == "vector":
                    e.wait_ge(s.sh, 32 + 16 * shc)
                dve(lambda: e.tensor_tensor(s.gm4[:], s.g_sb[0:4, :],
                                            s.gtmp[0:4, :], ALU.max))
                shift(s.gm4[2:4, :], s.gtmp[0:2, :])
                if en == "vector":
                    e.wait_ge(s.sh, 32 + 16 * shc)
                dve(lambda: e.tensor_tensor(s.gm2[:], s.gm4[0:2, :],
                                            s.gtmp[0:2, :], ALU.max))
                shift(s.gm2[1:2, :], s.gtmp[0:1, :])
                if en == "vector":
                    e.wait_ge(s.sh, 32 + 16 * shc)
                dve(lambda: e.tensor_tensor(s.m1_row[:], s.gm2[0:1, :],
                                            s.gtmp[0:1, :], ALU.max))
                bcast(s.m1_row[:])
                dve(lambda: e.tensor_tensor(s.eq1[:], s.g_sb[:], ps_b[0:8, :],
                                            ALU.is_equal))
                dve(lambda: e.scalar_tensor_tensor(s.msk[:], s.eq1[:],
                                                   s.Cn(-1e30, 8), s.g_sb[:],
                                                   ALU.mult, ALU.add))
                shift(s.msk[4:8, :], s.gtmp[0:4, :])
                if en == "vector":
                    e.wait_ge(s.sh, 32 + 16 * shc)
                dve(lambda: e.tensor_tensor(s.gm4[:], s.msk[0:4, :],
                                            s.gtmp[0:4, :], ALU.max))
                shift(s.gm4[2:4, :], s.gtmp[0:2, :])
                if en == "vector":
                    e.wait_ge(s.sh, 32 + 16 * shc)
                dve(lambda: e.tensor_tensor(s.gm2[:], s.gm4[0:2, :],
                                            s.gtmp[0:2, :], ALU.max))
                shift(s.gm2[1:2, :], s.gtmp[0:1, :])
                if en == "vector":
                    e.wait_ge(s.sh, 32 + 16 * shc)
                dve(lambda: e.tensor_tensor(s.m2_row[:], s.gm2[0:1, :],
                                            s.gtmp[0:1, :], ALU.max))
                bcast(s.m2_row[:])
                dve(lambda: e.tensor_tensor(s.eq2[:], s.msk[:], ps_b[0:8, :],
                                            ALU.is_equal))
                dve(lambda: e.tensor_tensor(s.e_row[:], s.m2_row[:],
                                            s.m1_row[:], ALU.subtract))
                if en == "scalar":
                    e.wait_ge(s.s_gv, gvc)
                    e.activation(s.e_row[:], s.e_row[:], AF.Exp
                                 ).then_inc(s.s_gv, 1)
                gvc += 1
                dve(lambda: e.tensor_scalar(s.r2_row[:], s.e_row[:],
                                            s.C1(1.0), None, ALU.add))
                dve(lambda: e.reciprocal(s.r1_row[:], s.r2_row[:]))
                dve(lambda: e.tensor_tensor(s.r2_row[:], s.e_row[:],
                                            s.r1_row[:], ALU.mult))
                bcast(s.r1_row[:])
                dve(lambda: e.tensor_tensor(s.ct1[:], s.eq1[:], ps_b[0:8, :],
                                            ALU.mult))
                bcast(s.r2_row[:])
                dve(lambda: e.tensor_tensor(s.eq2[:], s.eq2[:], ps_b[0:8, :],
                                            ALU.mult))
                dve(lambda: e.tensor_tensor(s.c_bf[:, n, :], s.ct1[:],
                                            s.eq2[:], ALU.add))
                assert gvc == GATE_P * (gn + 1), (gvc, gn)
                for m2 in range(NK):
                    b2c += 1
                    slot = b2c % 2
                    psb2 = s.ps_main[:, 1024 + 512 * slot:1536 + 512 * slot]
                    if en == "tensor":
                        e.wait_ge(s.s_gv, GATE_P * (gn + 1))
                        if b2c > 2:
                            e.wait_ge(s.s_b2e, b2c - 2)
                        e.matmul(psb2, s.b2_sb[0:8, 128 * m2:128 * (m2 + 1)],
                                 s.c_bf[:, n, :], start=True, stop=True
                                 ).then_inc(s.s_b2, 1)
                    if en == "vector":
                        e.wait_ge(s.s_b2, b2c)
                        e.tensor_tensor(s.acc[:, m2, n * NT:(n + 1) * NT],
                                        psb2,
                                        s.acc[:, m2, n * NT:(n + 1) * NT],
                                        ALU.add).then_inc(s.s_b2e, 1)
        return ph

    def mk_experts(half):
        def ph(en, e, s):
            for ei in range(8):
                ge = half * 8 + ei
                if en == "gpsimd" and ge + 1 < 16:
                    nxt = (ge + 1) % 8
                    e.wait_ge(s.s_mm1, 32 * (ge + 1))
                    e.dma_start(out=s.w1_sb[:], in_=w1_d[nxt].rearrange(
                        "(k p) h -> p k h", p=128)).then_inc(s.ld_w1, 16)
                    e.wait_ge(s.s_mm2, 32 * (ge + 1))
                    e.dma_start(out=s.w2_sb[:], in_=w2_d[nxt].rearrange(
                        "(k p) d -> p k d", p=128)).then_inc(s.ld_w2x, 16)
                for n in range(NTH):
                    tau = NTH * ge + n
                    nsl = slice(n * NT, (n + 1) * NT)
                    for m1 in range(16):
                        g1 = 16 * tau + m1 + 1
                        slot = g1 % 2
                        ps = s.ps_main[:, 512 * slot:512 * (slot + 1)]
                        if en == "tensor":
                            if m1 == 0 and n == 0:
                                e.wait_ge(s.ld_w1, 16 * (ge + 1))
                            if m1 == 0:
                                if tau > 0:
                                    e.wait_ge(s.s_ev2, 8 * (tau - 1) + 2)
                            if g1 > 2:
                                e.wait_ge(s.s_gel, g1 - 2)
                            for k in range(NK):
                                ins = e.matmul(
                                    ps, s.w1_sb[:, k, 128 * m1:128 * (m1 + 1)],
                                    s.t_sb[:, k, nsl],
                                    start=(k == 0), stop=(k == NK - 1))
                            ins.then_inc(s.s_mm1, 1)
                        if en == "scalar":
                            e.wait_ge(s.s_mm1, g1)
                            if tau > 0 and m1 == 0:
                                e.wait_ge(s.s_mm2, 16 * tau)
                            e.activation(s.h_sb[:, m1, :], ps, AF.Gelu,
                                         bias=s.b1_sb[:, ei, m1:m1 + 1]
                                         ).then_inc(s.s_gel, 1)
                    if en == "tensor":
                        if tau > 0:
                            e.wait_ge(s.s_ev2, 8 * (tau - 1) + 3)
                        e.matmul(s.ps_main[:, 1024:1536],
                                 s.sel_sb[:, ei, :], s.c_bf[:, n, :],
                                 start=True, stop=True).then_inc(s.s_cbcmm, 1)
                    if en == "vector":
                        e.wait_ge(s.s_cbcmm, tau + 1)
                        if tau > 0:
                            e.wait_ge(s.s_ev2, 8 * tau)
                        e.tensor_copy(s.cbc_sb[:], s.ps_main[:, 1024:1536]
                                      ).then_inc(s.s_cbc, 1)
                    for k2 in range(16):
                        g2 = 16 * tau + k2 + 1
                        if en == "tensor":
                            if k2 == 0:
                                if n == 0:
                                    e.wait_ge(s.ld_w2x, 16 * (ge + 1))
                                e.wait_ge(s.s_gel, 16 * (tau + 1))
                                e.wait_ge(s.s_cbc, tau + 1)
                                if tau > 0:
                                    e.wait_ge(s.s_ev2, 8 * tau)
                            for m2 in range(NK):
                                arena = s.ps_main if m2 < 4 else s.ps_aux
                                accp = arena[:, 512 * (m2 % 4):
                                             512 * (m2 % 4) + 512]
                                ins = e.matmul(
                                    accp,
                                    s.w2_sb[:, k2, 128 * m2:128 * (m2 + 1)],
                                    s.h_sb[:, k2, :],
                                    start=(k2 == 0), stop=(k2 == 15))
                                if m2 == NK - 1:
                                    ins.then_inc(s.s_mm2, 1)
                    for m2 in range(NK):
                        gv = 8 * tau + m2 + 1
                        if en == "vector":
                            if m2 == 0:
                                e.wait_ge(s.s_mm2, 16 * (tau + 1))
                                e.wait_ge(s.s_cbc, tau + 1)
                            arena = s.ps_main if m2 < 4 else s.ps_aux
                            accp = arena[:, 512 * (m2 % 4):
                                         512 * (m2 % 4) + 512]
                            if gv >= 2:
                                e.wait_ge(s.s_ev2, gv - 1)
                            e.tensor_tensor(s.ev_t[:], accp, s.cbc_sb[:],
                                            ALU.mult).then_inc(s.s_ch, 1)
                            e.wait_ge(s.s_ch, gv)
                            e.tensor_tensor(s.acc[:, m2, nsl], s.ev_t[:],
                                            s.acc[:, m2, nsl], ALU.add
                                            ).then_inc(s.s_ev2, 1)
        return ph

    def mk_store(half):
        def ph(en, e, s):
            if en == "sync":
                e.dma_start(out=x4_d[:, half * TH:(half + 1) * TH].rearrange(
                    "(m p) t -> p m t", p=128), in_=s.acc[:]
                    ).then_inc(s.st_x4, 16)
                e.wait_ge(s.st_x4, 16 * (half + 1))
        return ph

    phases = [ph_init_g]
    for half in range(2):
        phases += [mk_init_h(half), mk_preln(half), mk_addln(half),
                   mk_rms(half), mk_gate(half), mk_experts(half),
                   mk_store(half)]
    return phases


def _stage_F(inputs, x3_by_core):
    w1 = np.asarray(inputs["L1_ffn_w1"], np.float32)
    w2 = np.asarray(inputs["L1_ffn_w2"], np.float32)
    b1 = np.asarray(inputs["L1_ffn_b1"], np.float32)
    b2 = np.asarray(inputs["L1_ffn_b2"], np.float32)
    nw = np.asarray(inputs["L1_ffn_norm_weight"], np.float32)
    gw = np.asarray(inputs["L1_ffn_gate_w"], np.float32)
    sel = np.zeros((8, 8, 128), dtype=bf16)
    for ee in range(8):
        sel[ee, ee, :] = 1
    com = {
        "nwc": np.ascontiguousarray(nw.reshape(8, 128).T),
        "gwt": np.ascontiguousarray(gw.T),
        "w1t": np.ascontiguousarray(w1.transpose(0, 2, 1)),
        "b1c": np.ascontiguousarray(b1.reshape(8, 16, 128).transpose(0, 2, 1)),
        "w2t": np.ascontiguousarray(w2.transpose(0, 2, 1)),
        "b2r": b2,
        "sel8": sel,
    }
    return [dict(com, x=np.asarray(x3_by_core[c])) for c in range(NCORE)]



U8 = mybir.dt.uint8
ARENA_BYTES = 208000
CONSTS_ALL = [1.0 / D, -1.0, -0.5, 1e-5, 1e-6, 0.8, -L0_SHIFT, L0_SCALE,
              L1_SCALE, 0.2, 1.0 / H0, -1e30, 1.0]
GROUPS4 = [[0, 1, 2, 3], [4, 5, 6, 7]]


def _mk_base(nc, es, s):
    _setup_consts(nc, es, s, CONSTS_ALL)
    s.ones_row = es.enter_context(nc.sbuf_tensor("ones_row", [1, 128], F32))
    s.ones_col = es.enter_context(nc.sbuf_tensor("ones_col", [128, 1], F32))
    s.ones_col_b = es.enter_context(
        nc.sbuf_tensor("ones_col_b", [128, 1], BF16))
    arena = es.enter_context(nc.sbuf_tensor("arena", [128, ARENA_BYTES], U8))
    P = Pool(arena, ARENA_BYTES)
    pm = es.enter_context(nc.psum_tensor("ps_main_t", [128, 2048], F32))
    pa = es.enter_context(nc.psum_tensor("ps_aux_t", [128, 2048], F32))
    s.ps_main = pm.ap()
    s.ps_aux = pa.ap()
    s.ps_t = s.ps_aux[:, 1536:2048].bitcast(BF16)
    bar = es.enter_context(nc.semaphore("bar"))
    return P, bar


def _share(nc, es, s0):
    s = S()
    _link(s, s0)
    s.ps_main = s0.ps_main
    s.ps_aux = s0.ps_aux
    s.ps_t = s0.ps_t
    s.skip_consts = True
    return s


_IOSPEC = dict(
    x0=([D, T], F32), wqkv0=([3, D, D], F32), ident=([128, 128], BF16),
    q0=([D, T], BF16), k0=([D, T], BF16), v0=([T, D], BF16),
    k0g=([4 * D, T], BF16), v0g=([4 * T, D], BF16),
    kv0=([2 * D, T], BF16), kv0g=([8 * D, T], BF16),
    kv1=([D + 512, T], BF16), kv1g=([4 * (D + 512), T], BF16),
    wout0=([D, D], F32), x1=([D, T], F32),
    f0w1t=([D, H0], F32), f0w1r=([H0, D], F32), f0b1c=([128, 32], F32),
    f0w2t=([H0, D], F32), f0w2r=([D, H0], F32), f0b2c=([128, 8], F32),
    hdr=([H0, T], BF16), x2=([D, T], F32),
    wqkv1=([3, D, D], F32), bqkv1=([3, 128, 8], F32), lams=([128, 4], F32),
    q1=([D, T], BF16), k1=([D, T], BF16), ve1=([T, 512], BF16),
    k1g=([4 * D, T], BF16), ve1g=([4 * T, 512], BF16),
    wo1=([D, D], F32), boc=([128, 8], F32), lnwc=([128, 8], F32),
    lnbc=([128, 8], F32), x3=([D, T], F32),
    nwc=([128, 8], F32), gwt=([D, 8], F32), mw1t=([8, D, H1], F32),
    mb1c=([8, 128, 16], F32), mw2t=([8, H1, D], F32), mb2r=([8, D], F32),
    sel8=([8, 8, 128], BF16), x4=([D, T], F32),
)


def _mk_io(nc, names, outs, interns=()):
    io = {}
    for n in names:
        sh, dt = _IOSPEC[n]
        io[n] = nc.dram_tensor(n, sh, dt, kind="ExternalInput").ap()
    for n in outs:
        sh, dt = _IOSPEC[n]
        io[n] = nc.dram_tensor(n, sh, dt, kind="ExternalOutput").ap()
    for n in interns:
        sh, dt = _IOSPEC[n]
        io[n] = nc.dram_tensor(n, sh, dt, kind="Internal").ap()
    return io


def _standalone(setup, ins, outs, interns=()):
    nc = bass.Bass("TRN2", target_bir_lowering=False)
    es = ExitStack()
    s = S()
    io = _mk_io(nc, ins, outs, interns)
    P, bar = _mk_base(nc, es, s)
    LN = LNBank(nc, es)
    phs = setup(nc, es, s, P, LN, io)
    with es:
        _run_phases(nc, bar, [(p, s) for p in phs])
    return nc


def build_A():
    return _standalone(setup_A, ["x0", "wqkv0", "ident"],
                       ["q0", "k0", "v0"])


def build_B():
    return _standalone(setup_B, ["x0", "q0", "k0g", "v0g", "wout0"], ["x1"])


def build_C():
    return _standalone(setup_C, ["x1", "f0w1t", "f0w1r", "f0b1c", "f0w2t",
                                 "f0w2r", "f0b2c"], ["x2"], ["hdr"])


def build_D():
    return _standalone(setup_D, ["x2", "wqkv1", "bqkv1", "lams", "ident"],
                       ["q1", "k1", "ve1"])


def build_E():
    return _standalone(setup_E, ["x2", "q1", "k1g", "ve1g", "wo1", "boc",
                                 "lnwc", "lnbc"], ["x3"])


def build_F():
    return _standalone(setup_F, ["x3", "nwc", "gwt", "mw1t", "mb1c", "mw2t",
                                 "mb2r", "sel8"], ["x4"])


def _mk_gather(io, cc_v, issues, cc_k, wait_k, sim_local=False):
    def ph(en, e, s):
        if en == "gpsimd":
            for src_t, dst_t in issues:
                _issue_gather(e, s, io, cc_v, src_t, dst_t, sim_local)
            e.wait_ge(cc_k, wait_k)
    return ph


def _issue_gather(e, s, io, cc, src_t, dst_t, sim_local=False):
    if sim_local:
        rows = io[src_t].shape[0]
        for r in range(4):
            e.dma_start(out=io[dst_t][r * rows:(r + 1) * rows, :],
                        in_=io[src_t][:, :]).then_inc(cc, 16)
    else:
        e.collective_compute(
            "AllGather", ALU.bypass, replica_groups=GROUPS4,
            ins=[io[src_t]], outs=[io[dst_t]]).then_inc(cc, 1)


def build_fused(sim_local=False, upto="F"):
    nc = bass.Bass("TRN2", target_bir_lowering=False)
    es = ExitStack()
    s0 = S()
    ins = ["x0", "wqkv0", "ident", "wout0",
           "f0w1t", "f0w1r", "f0b1c", "f0w2t", "f0w2r", "f0b2c",
           "wqkv1", "bqkv1", "lams", "wo1", "boc", "lnwc", "lnbc",
           "nwc", "gwt", "mw1t", "mb1c", "mw2t", "mb2r", "sel8"]
    interns = ["q0", "k0", "v0", "k0g", "v0g", "x1", "hdr", "x2",
               "q1", "k1", "ve1", "k1g", "ve1g", "x3"]
    outs = {"A": ["q0", "k0", "v0"], "B": ["x1"], "C": ["x2"],
            "D": ["q1", "k1", "ve1"], "E": ["x3"], "F": ["x4"]}[upto]
    interns = [t for t in interns if t not in outs]
    io = _mk_io(nc, ins, outs + (["x4"] if upto != "F" else []), interns)
    P, bar = _mk_base(nc, es, s0)
    LN = LNBank(nc, es)
    cc_k = es.enter_context(nc.semaphore("cc_k"))
    cc_v = es.enter_context(nc.semaphore("cc_v"))
    pairs = []
    sg = _share(nc, es, s0)

    def add(setup, s):
        P.reset()
        pairs.extend((p, s) for p in setup(nc, es, s, P, LN, io))

    unit = 64 if sim_local else 1
    if upto != "A":
        s0.cc = cc_k
        s0.ccv = cc_v
        s0.sim_local = sim_local
        s0.kgather = ("k0", "k0g", 0)
        s0.vgather0 = ("v0", "v0g")
        s0.io = io
    add(setup_A, s0)
    if upto != "A":
        sB = _share(nc, es, s0)
        sB.cc = cc_v
        sB.cc_lvl = 1 * unit
        sB.cck = cc_k
        sB.cck_lvl = 1 * unit
        add(setup_B, sB)
    if upto in ("C", "D", "E", "F"):
        add(setup_C, _share(nc, es, s0))
    if upto in ("D", "E", "F"):
        sD = _share(nc, es, s0)
        sD.st2 = s0.st2
        sD.st_vv = s0.st_vv
        sD.st2_base = [16 * 8, 16 * 8]
        sD.stv_base = 16
        if upto in ("E", "F"):
            sD.cc = cc_k
            sD.sim_local = sim_local
            sD.kgather = ("k1", "k1g", 2)
            sD.io = io
        add(setup_D, sD)
    if upto in ("E", "F"):
        sE = _share(nc, es, s0)
        sE.cc = cc_v
        sE.cc_lvl = 2 * unit
        sE.cck = cc_k
        sE.cck_lvl = 2 * unit
        sE.vgather = ("ve1", "ve1g")
        sE.io = io
        sE.sim_local = sim_local
        sE.ld_kv2 = sB.ld_kv2
        sE.ld_v2 = sB.ld_v2
        sE.kv_base = 16 * 5 * 4
        sE.v_base = 16 * 2 * 4
        add(setup_E, sE)
    if upto == "F":
        add(setup_F, _share(nc, es, s0))
    with es:
        _run_phases(nc, bar, pairs)
    return nc



def _common_stage(inputs):
    Wq = np.asarray(inputs["L0_attn_W_qkv"], np.float32)
    w1 = np.asarray(inputs["L0_ffn_w1"], np.float32)
    w2 = np.asarray(inputs["L0_ffn_w2"], np.float32)
    b1 = np.asarray(inputs["L0_ffn_b1"], np.float32)
    b2 = np.asarray(inputs["L0_ffn_b2"], np.float32)
    mw1 = np.asarray(inputs["L1_ffn_w1"], np.float32)
    mw2 = np.asarray(inputs["L1_ffn_w2"], np.float32)
    mb1 = np.asarray(inputs["L1_ffn_b1"], np.float32)
    sel = np.zeros((8, 8, 128), dtype=bf16)
    for ee in range(8):
        sel[ee, ee, :] = 1
    return {
        "wqkv0": np.ascontiguousarray(
            Wq.reshape(D, 16, 64, 3).transpose(3, 0, 1, 2).reshape(3, D, D)),
        "ident": np.eye(128, dtype=bf16),
        "wout0": np.asarray(inputs["L0_attn_W_out"], np.float32),
        "f0w1t": np.ascontiguousarray(w1.T), "f0w1r": w1,
        "f0b1c": np.ascontiguousarray(b1.reshape(32, 128).T),
        "f0w2t": np.ascontiguousarray(w2.T), "f0w2r": w2,
        "f0b2c": np.ascontiguousarray(b2.reshape(8, 128).T),
        "wqkv1": np.ascontiguousarray(np.stack(
            [np.asarray(inputs["L1_attn_Wq"], np.float32),
             np.asarray(inputs["L1_attn_Wk"], np.float32),
             np.asarray(inputs["L1_attn_Wv"], np.float32)])),
        "bqkv1": np.ascontiguousarray(np.stack(
            [np.asarray(inputs["L1_attn_bq"], np.float32).reshape(8, 128).T,
             np.asarray(inputs["L1_attn_bk"], np.float32).reshape(8, 128).T,
             np.asarray(inputs["L1_attn_bv"], np.float32).reshape(8, 128).T])),
        "lams": np.ascontiguousarray(np.stack(
            [np.asarray(inputs[f"L1_attn_lambda_{nm}"], np.float32)
             for nm in ("q1", "k1", "q2", "k2")], axis=1)),
        "wo1": np.asarray(inputs["L1_attn_Wo"], np.float32),
        "boc": np.ascontiguousarray(
            np.asarray(inputs["L1_attn_bo"], np.float32).reshape(8, 128).T),
        "lnwc": np.ascontiguousarray(
            np.asarray(inputs["L1_attn_ln_w"], np.float32).reshape(8, 128).T),
        "lnbc": np.ascontiguousarray(
            np.asarray(inputs["L1_attn_ln_b"], np.float32).reshape(8, 128).T),
        "nwc": np.ascontiguousarray(
            np.asarray(inputs["L1_ffn_norm_weight"],
                       np.float32).reshape(8, 128).T),
        "gwt": np.ascontiguousarray(
            np.asarray(inputs["L1_ffn_gate_w"], np.float32).T),
        "mw1t": np.ascontiguousarray(mw1.transpose(0, 2, 1)),
        "mb1c": np.ascontiguousarray(
            mb1.reshape(8, 16, 128).transpose(0, 2, 1)),
        "mw2t": np.ascontiguousarray(mw2.transpose(0, 2, 1)),
        "mb2r": np.asarray(inputs["L1_ffn_b2"], np.float32),
        "sel8": sel,
    }


def _stage_A(inputs):
    com = _common_stage(inputs)
    X = np.ascontiguousarray(np.asarray(inputs["L0_attn_X"], np.float32)
                             .reshape(2 * 8192, D))
    maps = []
    for c in range(NCORE):
        maps.append({"x0": np.ascontiguousarray(X[c * T:(c + 1) * T].T),
                     "wqkv0": com["wqkv0"], "ident": com["ident"]})
    return maps


def _stage_B(inputs, resA):
    com = _common_stage(inputs)
    X = np.ascontiguousarray(np.asarray(inputs["L0_attn_X"], np.float32)
                             .reshape(2 * 8192, D))
    maps = []
    for g in range(2):
        cs = range(4 * g, 4 * g + 4)
        kg = np.concatenate([np.asarray(resA[c]["k0"]) for c in cs], axis=0)
        vg = np.concatenate([np.asarray(resA[c]["v0"]) for c in cs], axis=0)
        for c in cs:
            maps.append({"q0": np.asarray(resA[c]["q0"]), "k0g": kg,
                         "v0g": vg, "wout0": com["wout0"],
                         "x0": np.ascontiguousarray(X[c * T:(c + 1) * T].T)})
    return maps


def _stage_C(inputs, x1_by_core):
    com = _common_stage(inputs)
    ks = ["f0w1t", "f0w1r", "f0b1c", "f0w2t", "f0w2r", "f0b2c"]
    return [dict({k: com[k] for k in ks}, x1=np.asarray(x1_by_core[c]))
            for c in range(NCORE)]


def _stage_D(inputs, x2_by_core):
    com = _common_stage(inputs)
    ks = ["wqkv1", "bqkv1", "lams", "ident"]
    return [dict({k: com[k] for k in ks}, x2=np.asarray(x2_by_core[c]))
            for c in range(NCORE)]


def _stage_E(inputs, resD):
    com = _common_stage(inputs)
    ks = ["wo1", "boc", "lnwc", "lnbc"]
    maps = []
    for g in range(2):
        cs = range(4 * g, 4 * g + 4)
        kg = np.concatenate([np.asarray(resD[c]["k1"]) for c in cs], axis=0)
        vg = np.concatenate([np.asarray(resD[c]["ve1"]) for c in cs], axis=0)
        for c in cs:
            maps.append(dict({k: com[k] for k in ks},
                             q1=np.asarray(resD[c]["q1"]), k1g=kg, ve1g=vg))
    return maps


def _stage_F(inputs, x3_by_core):
    com = _common_stage(inputs)
    ks = ["nwc", "gwt", "mw1t", "mb1c", "mw2t", "mb2r", "sel8"]
    return [dict({k: com[k] for k in ks}, x3=np.asarray(x3_by_core[c]))
            for c in range(NCORE)]


def _stage_fused(inputs):
    com = _common_stage(inputs)
    X = np.ascontiguousarray(np.asarray(inputs["L0_attn_X"], np.float32)
                             .reshape(2 * 8192, D))
    maps = []
    for c in range(NCORE):
        maps.append(dict(com,
                         x0=np.ascontiguousarray(X[c * T:(c + 1) * T].T)))
    return maps


def kernel_unfused(**inputs):
    cores = list(range(NCORE))
    resA = run_bass_kernel_spmd(build_A(), _stage_A(inputs), core_ids=cores
                                ).results
    resB = run_bass_kernel_spmd(build_B(), _stage_B(inputs, resA),
                                core_ids=cores).results
    x1 = [np.asarray(r["x1"]) for r in resB]
    resC = run_bass_kernel_spmd(build_C(), _stage_C(inputs, x1),
                                core_ids=cores).results
    x2 = [np.asarray(r["x2"]) for r in resC]
    resD = run_bass_kernel_spmd(build_D(), _stage_D(inputs, x2),
                                core_ids=cores).results
    mapsE = _stage_E(inputs, resD)
    for c in cores:
        mapsE[c]["x2"] = x2[c]
    resE = run_bass_kernel_spmd(build_E(), mapsE, core_ids=cores).results
    x3 = [np.asarray(r["x3"]) for r in resE]
    resF = run_bass_kernel_spmd(build_F(), _stage_F(inputs, x3),
                                core_ids=cores).results
    full = np.concatenate([np.asarray(r["x4"]).T for r in resF], axis=0)
    return np.ascontiguousarray(full.reshape(2, 8192, D))


def kernel(**inputs):
    cores = list(range(NCORE))
    res = run_bass_kernel_spmd(build_fused(), _stage_fused(inputs),
                               core_ids=cores).results
    full = np.concatenate([np.asarray(r["x4"]).T for r in res], axis=0)
    return np.ascontiguousarray(full.reshape(2, 8192, D))


# version 15
# speedup vs baseline: 1.1868x; 1.1868x over previous
import numpy as np
import ml_dtypes
from contextlib import ExitStack

import concourse.bass as bass
import concourse.mybir as mybir
from concourse.bass_utils import run_bass_kernel_spmd

F32 = mybir.dt.float32
BF16 = mybir.dt.bfloat16
AF = mybir.ActivationFunctionType
ALU = mybir.AluOpType
AX = mybir.AxisListType.X
bf16 = ml_dtypes.bfloat16

NCORE = 8
T = 2048
D = 1024
NK = 8
NT = 512
NTILES = T // NT

L0_SHIFT = 20.0
L0_SCALE = 0.125
L1_SHIFT = 20.0
L1_SCALE = float(1.0 / np.sqrt(np.float32(128.0)))


class S:
    pass


class Pool:

    def __init__(self, handle, nbytes):
        self.h = handle
        self.nbytes = nbytes
        self.off = 0

    def reset(self):
        self.off = 0

    def get(self, shape, dtype):
        import math
        p = shape[0]
        n = int(np.prod(shape[1:])) if len(shape) > 1 else 1
        isz = 2 if dtype == BF16 else 4
        nb = n * isz
        off = self.off
        self.off = (off + nb + 63) & ~63
        assert self.off <= self.nbytes, f"pool overflow {self.off}"
        ap = self.h.ap()[0:p, off:off + nb].bitcast(dtype)
        if len(shape) > 2:
            dims = " ".join(f"d{i}" for i in range(1, len(shape)))
            kw = {f"d{i}": shape[i] for i in range(1, len(shape))}
            ap = ap.rearrange(f"q ({dims}) -> q {dims}", **kw)
        return ap


def _run_phases(nc, bar, pairs):
    with nc.Block() as block:
        def mk(en):
            def f(e):
                for i, (ph, s) in enumerate(pairs):
                    if i:
                        e.wait_ge(bar, 5 * i)
                    ph(en, e, s)
                    e.drain().then_inc(bar, 1)
            return f
        block.sync(mk("sync"))
        block.tensor(mk("tensor"))
        block.scalar(mk("scalar"))
        block.vector(mk("vector"))
        block.gpsimd(mk("gpsimd"))


def _link(s_dst, s_src):
    for nm in ("cst", "cvals", "C", "C1", "Cn", "ones_row", "ones_col",
               "ones_col_b"):
        setattr(s_dst, nm, getattr(s_src, nm))


def _setup_consts(nc, es, s, values):
    vals = list(dict.fromkeys(values))
    s.cvals = {v: i for i, v in enumerate(vals)}
    s.cst = es.enter_context(nc.sbuf_tensor("cst", [128, max(len(vals), 1)], F32))
    s.C = lambda v: s.cst[:, s.cvals[v]:s.cvals[v] + 1]
    s.C1 = lambda v: s.cst[0:1, s.cvals[v]:s.cvals[v] + 1]
    s.Cn = lambda v, n: s.cst[0:n, s.cvals[v]:s.cvals[v] + 1]


def _emit_const_init(en, e, s):
    if getattr(s, "skip_consts", False):
        return
    if en == "vector":
        for v, i in s.cvals.items():
            e.memset(s.cst[:, i:i + 1], float(v))
        e.memset(s.ones_row[:], 1.0)
        e.memset(s.ones_col[:], 1.0)
        e.memset(s.ones_col_b[:], 1.0)



LN_NAMES = ("x2", "stat", "row", "bc", "app", "ach")


def _ln_R(sub_mean, post_scale):
    return (4 + 2 + (1 if post_scale is not None else 0) + 1) if sub_mean \
        else (1 + 2 + (1 if post_scale is not None else 0))


class LNBank:

    def __init__(self, nc, es):
        self.h = {nm: es.enter_context(nc.semaphore("LN_" + nm))
                  for nm in LN_NAMES}
        self.base = {nm: 0 for nm in LN_NAMES}

    def take(self, s, prefix, ntiles, sub_mean, post_scale, has_lnw, nk):
        for nm in LN_NAMES:
            setattr(s, f"{prefix}_{nm}", self.h[nm])
        bases = dict(self.base)
        setattr(s, f"{prefix}_bases", bases)
        R = _ln_R(sub_mean, post_scale)
        per_k = 3 if has_lnw else 2
        self.base["x2"] += ntiles
        self.base["stat"] += ntiles
        self.base["row"] += R * ntiles
        self.base["bc"] += ntiles
        self.base["app"] += ntiles
        self.base["ach"] += per_k * nk * ntiles
        return bases


def _emit_ln(en, e, s, cfg):
    p = cfg["prefix"]
    sub_mean = cfg.get("sub_mean", True)
    eps = cfg["eps"]
    post_scale = cfg.get("post_scale")
    wcol = cfg.get("weight_col")
    src_bf16 = cfg.get("src_bf16", False)
    ntiles = cfg.get("ntiles", NTILES)
    nk = cfg.get("nk", NK)
    feat = 128 * nk
    sx2 = getattr(s, f"{p}_x2"); sstat = getattr(s, f"{p}_stat")
    srow = getattr(s, f"{p}_row"); sbc = getattr(s, f"{p}_bc")
    sapp = getattr(s, f"{p}_app"); sach = getattr(s, f"{p}_ach")
    B = getattr(s, f"{p}_bases")
    bx2 = B["x2"]; bstat = B["stat"]; brow = B["row"]
    bbc = B["bc"]; bapp = B["app"]; bach = B["ach"]
    mu = s.ps_aux[0:1, 0:NT]
    sq = s.ps_aux[64:65, 0:NT]
    bcA = s.ps_aux[:, 512:1024]
    bcB = s.ps_aux[:, 1024:1536]
    ones_mu = s.ones_col_b if src_bf16 else s.ones_col
    R = _ln_R(sub_mean, post_scale)
    lnw = cfg.get("lnw_col")
    lnb = cfg.get("lnb_col")
    per_k = 3 if lnw is not None else 2
    achc = bach
    for n in range(ntiles):
        if "pre" in cfg:
            cfg["pre"](en, e, s, n)
        if en == "scalar":
            if n > 0 or bstat > 0:
                e.wait_ge(sstat, bstat + n)
            if "src_all" in cfg:
                e.activation(s.x2s[:, 0:nk, :], cfg["src_all"](n),
                             AF.Square).then_inc(sx2, 1)
            else:
                for k in range(nk):
                    ins = e.activation(s.x2s[:, k, :], cfg["src"](k, n),
                                       AF.Square)
                ins.then_inc(sx2, 1)
        if en == "tensor":
            e.wait_ge(sx2, bx2 + n + 1)
            if n > 0 or brow > 0:
                e.wait_ge(srow, brow + R * n)
            if sub_mean:
                for k in range(nk):
                    e.matmul(mu, ones_mu[:], cfg["src"](k, n),
                             start=(k == 0), stop=(k == nk - 1))
            for k in range(nk):
                ins = e.matmul(sq, s.ones_col[:], s.x2s[:, k, :],
                               start=(k == 0), stop=(k == nk - 1))
            ins.then_inc(sstat, 1)
        c = brow + R * n
        if en == "vector":
            if n > 0 or brow > 0:
                e.wait_ge(srow, c)
            e.wait_ge(sstat, bstat + n + 1)
            if sub_mean:
                e.tensor_scalar(s.row_ex2[:], sq, s.C1(1.0 / feat), None,
                                ALU.mult).then_inc(srow, 1)
                e.tensor_scalar(s.row_mu[:], mu, s.C1(1.0 / feat), None,
                                ALU.mult).then_inc(srow, 1)
                e.wait_ge(srow, c + 2)
                e.tensor_tensor(s.row_tmp[:], s.row_mu[:], s.row_mu[:],
                                ALU.mult).then_inc(srow, 1)
                e.wait_ge(srow, c + 3)
                e.tensor_tensor(s.row_ex2[:], s.row_ex2[:], s.row_tmp[:],
                                ALU.subtract).then_inc(srow, 1)
            else:
                e.tensor_scalar(s.row_ex2[:], sq, s.C1(1.0 / feat), None,
                                ALU.mult).then_inc(srow, 1)
        nvar = c + (4 if sub_mean else 1)
        if en == "scalar":
            e.wait_ge(srow, nvar)
            e.activation(s.row_tmp2[:], s.row_ex2[:], AF.Ln,
                         bias=s.C1(eps)).then_inc(srow, 1)
            e.wait_ge(srow, nvar + 1)
            e.activation(s.row_a[:], s.row_tmp2[:], AF.Exp,
                         scale=s.C1(-0.5)).then_inc(srow, 1)
            if post_scale is not None:
                e.wait_ge(srow, nvar + 2)
                e.mul(s.row_a[:], s.row_a[:],
                      s.C1(post_scale)).then_inc(srow, 1)
        na = nvar + 2 + (1 if post_scale is not None else 0)
        if sub_mean:
            if en == "vector":
                e.wait_ge(srow, na)
                e.scalar_tensor_tensor(s.row_b[:], s.row_mu[:], s.C1(-1.0),
                                       s.row_a[:], ALU.mult, ALU.mult
                                       ).then_inc(srow, 1)
        if en == "tensor":
            e.wait_ge(srow, brow + R * (n + 1))
            if n > 0 or bapp > 0:
                e.wait_ge(sapp, bapp + n)
            e.matmul(bcA, s.ones_row[:], s.row_a[:], start=True, stop=True)
            ins = e.matmul(bcB, s.ones_row[:],
                           s.row_b[:] if sub_mean else s.row_a[:],
                           start=True, stop=True)
            ins.then_inc(sbc, 1)
        if en == "vector":
            e.wait_ge(sbc, bbc + n + 1)
        for k in range(nk):
            if en == "vector":
                if achc >= per_k or bach > 0:
                    e.wait_ge(sach, achc)
                e.tensor_tensor(s.app_t1[:], cfg["src"](k, n), bcA,
                                ALU.mult).then_inc(sach, 1)
                e.wait_ge(sach, achc + 1)
                if lnw is not None:
                    e.tensor_tensor(s.app_t2[:], s.app_t1[:], bcB,
                                    ALU.add).then_inc(sach, 1)
                    e.wait_ge(sach, achc + 2)
                    ins = e.tensor_scalar(cfg["dst"](k, n), s.app_t2[:],
                                          lnw(k), lnb(k), ALU.mult, ALU.add)
                elif sub_mean:
                    ins = e.tensor_tensor(cfg["dst"](k, n), s.app_t1[:], bcB,
                                          ALU.add)
                else:
                    ins = e.tensor_scalar(cfg["dst"](k, n), s.app_t1[:],
                                          wcol(k), None, ALU.mult)
                ins.then_inc(sach, 1)
            achc += per_k
            if en == "vector" and k == nk - 1:
                e.wait_ge(sach, achc)
                e.sem_inc(sapp, 1)
        if "post" in cfg:
            cfg["post"](en, e, s, n)



def setup_A(nc, es, s, P, LN, io):
    x_d = io["x0"]
    w_d = io["wqkv0"]
    id_d = io["ident"]
    q_d = io["q0"]
    k_d = io["k0"]
    v_d = io["v0"]
    s.xsl = P.get([128, 2, NK, NT], F32)
    s.x2s = P.get([128, NK, NT], F32)
    s.xln = P.get([128, NK, T], BF16)
    for nm in ("row_ex2", "row_mu", "row_tmp", "row_tmp2", "row_a", "row_b"):
        setattr(s, nm, P.get([1, NT], F32))
    s.app_t1 = P.get([128, NT], F32)
    s.w_sb = P.get([128, 2, NK, D], BF16)
    s.out_buf = P.get([128, 2, T], BF16)
    s.vfm_buf = P.get([128, 2, T], BF16)
    s.vtm = P.get([128, T // 128, D], BF16)
    s.ident = P.get([128, 128], BF16)
    s.ld_x2 = [es.enter_context(nc.semaphore(f"Ald_x{i}")) for i in range(2)]
    s.ld_w2p = [es.enter_context(nc.semaphore(f"Ald_w{i}")) for i in range(2)]
    s.ld_id = es.enter_context(nc.semaphore("Ald_id"))
    for nm in ("s_mm", "s_ev", "s_tr", "s_cp"):
        setattr(s, nm, es.enter_context(nc.semaphore("A" + nm)))
    if getattr(s, "st2", None) is None:
        s.st2 = [es.enter_context(nc.semaphore(f"st2_{i}"))
                 for i in range(2)]
        s.st_vv = es.enter_context(nc.semaphore("st_vv"))
        s.st2_base = [0, 0]
        s.stv_base = 0
    LN.take(s, "lnA", NTILES, True, None, False, NK)

    def ph_init(en, e, s):
        _emit_const_init(en, e, s)
        if en == "sync":
            e.dma_start(out=s.ident[:], in_=id_d[:]).then_inc(s.ld_id, 16)
            e.dma_start(out=s.xsl[:, 0], in_=x_d[:, 0:NT].rearrange(
                "(m p) t -> p m t", p=128)).then_inc(s.ld_x2[0], 16)
        if en == "gpsimd":
            e.dma_start(out=s.w_sb[:, 0],
                        in_=w_d[1].rearrange("(k p) n -> p k n", p=128)
                        ).then_inc(s.ld_w2p[0], 16)

    def pre_ln(en, e, s, n):
        if en == "sync" and n + 1 < NTILES:
            if n >= 1:
                e.wait_ge(s.lnA_app, s.lnA_bases["app"] + n)
            e.dma_start(out=s.xsl[:, (n + 1) % 2],
                        in_=x_d[:, (n + 1) * NT:(n + 2) * NT].rearrange(
                            "(m p) t -> p m t", p=128)
                        ).then_inc(s.ld_x2[(n + 1) % 2], 16)
        if en in ("vector", "tensor", "scalar"):
            e.wait_ge(s.ld_x2[n % 2], 16 * (n // 2 + 1))

    def ph_ln(en, e, s):
        _emit_ln(en, e, s, dict(
            prefix="lnA", eps=1e-5, sub_mean=True,
            src=lambda k, n: s.xsl[:, n % 2, k, :],
            src_all=lambda n: s.xsl[:, n % 2, :, :],
            dst=lambda k, n: s.xln[:, k, n * NT:(n + 1) * NT],
            pre=pre_ln,
        ))

    def ph_proj(en, e, s):
        ORD = [1, 2, 0]
        KIND = ["k", "v", "q"]
        mmc = 0
        evc = 0
        trc = 0
        stc = 0
        pb = s.st2_base
        if en == "tensor":
            e.wait_ge(s.ld_id, 16)
        for j in range(3):
            kind = KIND[j]
            if en == "gpsimd" and j + 1 < 3:
                if j > 0:
                    e.wait_ge(s.s_mm, 32 * j)
                e.dma_start(out=s.w_sb[:, (j + 1) % 2],
                            in_=w_d[ORD[j + 1]].rearrange(
                                "(k p) n -> p k n", p=128)
                            ).then_inc(s.ld_w2p[(j + 1) % 2], 16)
            for m in range(NK):
                for n in range(NTILES):
                    slot = mmc % 2
                    ps = s.ps_main[:, slot * NT:(slot + 1) * NT]
                    mmc += 1
                    if en == "tensor":
                        if mmc == 1 or (n == 0 and m == 0):
                            e.wait_ge(s.ld_w2p[j % 2], 16 * (j // 2 + 1))
                        if mmc > 2:
                            e.wait_ge(s.s_ev, mmc - 2)
                        for k in range(NK):
                            ins = e.matmul(
                                ps, s.w_sb[:, j % 2, k, 128 * m:128 * (m + 1)],
                                s.xln[:, k, n * NT:(n + 1) * NT],
                                start=(k == 0), stop=(k == NK - 1))
                        ins.then_inc(s.s_mm, 1)
                    evc += 1
                    if en == "scalar":
                        e.wait_ge(s.s_mm, evc)
                        if n == 0:
                            if kind == "v":
                                if m >= 2:
                                    e.wait_ge(s.s_cp, 16 * (m - 1))
                            else:
                                sn = (0 if kind == "k" else 8) + m
                                if sn >= 2:
                                    e.wait_ge(s.st2[sn % 2],
                                              pb[sn % 2] + 16 * (sn // 2))
                        dst = s.out_buf if kind != "v" else s.vfm_buf
                        e.activation(dst[:, m % 2, n * NT:(n + 1) * NT], ps,
                                     AF.Copy).then_inc(s.s_ev, 1)
                if kind != "v":
                    stc += 1
                    if en == "sync":
                        e.wait_ge(s.s_ev, evc)
                        d = k_d if kind == "k" else q_d
                        e.dma_start(out=d[128 * m:128 * (m + 1), :],
                                    in_=s.out_buf[:, m % 2]
                                    ).then_inc(s.st2[(stc - 1) % 2], 16)
                else:
                    for tb in range(T // 128):
                        trc += 1
                        if en == "tensor":
                            e.wait_ge(s.s_ev, evc)
                            if trc > 1:
                                e.wait_ge(s.s_cp, trc - 1)
                            e.transpose(s.ps_t[:, 0:128],
                                        s.vfm_buf[:, m % 2,
                                                  128 * tb:128 * (tb + 1)],
                                        s.ident[:]).then_inc(s.s_tr, 1)
                        if en == "vector":
                            e.wait_ge(s.s_tr, trc)
                            e.tensor_copy(s.vtm[:, tb, 128 * m:128 * (m + 1)],
                                          s.ps_t[:, 0:128]).then_inc(s.s_cp, 1)
            if kind == "k" and en == "sync":
                e.wait_ge(s.st2[0], pb[0] + 16 * 4)
                e.wait_ge(s.st2[1], pb[1] + 16 * 4)
            if kind == "v" and en == "sync":
                e.wait_ge(s.s_cp, 128)
                e.dma_start(out=v_d.rearrange("(tb p) d -> p tb d", p=128),
                            in_=s.vtm[:]).then_inc(s.st_vv, 16)
        if en == "gpsimd" and getattr(s, "kgather", None):
            ksrc, kdst, lvl = s.kgather
            e.wait_ge(s.st2[0], pb[0] + 16 * 4)
            e.wait_ge(s.st2[1], pb[1] + 16 * 4)
            _issue_gather(e, s, s.io, s.cc, ksrc, kdst, s.sim_local)
            if getattr(s, "vgather0", None):
                e.wait_ge(s.st_vv, s.stv_base + 16)
                _issue_gather(e, s, s.io, s.ccv, *s.vgather0, s.sim_local)

    def ph_store(en, e, s):
        if en == "sync":
            e.wait_ge(s.st2[0], s.st2_base[0] + 16 * 8)
            e.wait_ge(s.st2[1], s.st2_base[1] + 16 * 8)
            e.wait_ge(s.st_vv, s.stv_base + 16)
    return [ph_init, ph_ln, ph_proj, ph_store]



def _attn_alloc(nc, es, s, tag):
    if getattr(s, "ld_kv2", None) is None:
        s.ld_kv2 = [es.enter_context(nc.semaphore(f"{tag}ld_kv{i}"))
                    for i in range(2)]
        s.ld_v2 = [es.enter_context(nc.semaphore(f"{tag}ld_v{i}"))
                   for i in range(2)]
    for nm in ("s_sc", "s_exp", "s_pv", "s_ep", "s_att", "s_unit"):
        setattr(s, nm, es.enter_context(nc.semaphore(tag + nm)))
    s.s_bc = s.s_ep


def _emit_attn(en, e, s, cfg):
    NU = cfg["n_units"]
    pair = cfg["pair"]
    npt = cfg.get("pt_slots", 2)
    NCH = 64
    dma_per_unit = cfg["dma_per_unit"]
    pend = []

    def emit_pv(e, s, u, eg, qbuf, c):
        pvs = s.ps_aux[:, 1024 * qbuf:1024 * qbuf + 1024]
        gc = eg * NCH + c
        if pair:
            e.wait_ge(s.s_exp, gc + 1)
            if c == 0 and eg >= 2:
                e.wait_ge(s.s_att, 2 * (eg - 1))
            e.matmul(pvs[0:65, 0:512],
                     s.vslab[:, u % 2, 0, c, :],
                     s.pt[:, gc % npt, 0:512],
                     start=(c == 0), stop=(c == NCH - 1))
            ins = e.matmul(pvs[0:65, 512:1024],
                           s.vslab[:, u % 2, 1, c, :],
                           s.pt[:, gc % npt, 512:1024],
                           start=(c == 0), stop=(c == NCH - 1))
            ins.then_inc(s.s_pv, 1)
        else:
            j = gc // 2
            e.wait_ge(s.s_exp, j + 1)
            if c == 1 and eg >= 2:
                e.wait_ge(s.s_att, 2 * (eg - 1))
            for cc in (c - 1, c):
                pts = s.pt[:, j % 2, 512 * (cc % 2):512 * (cc % 2) + 512]
                e.matmul(pvs[0:65, 0:512],
                         s.vslab[:, u % 2, 0, cc, 0:65], pts,
                         start=(cc == 0), stop=(cc == NCH - 1))
                ins = e.matmul(pvs[0:64, 512:1024],
                               s.vslab[:, u % 2, 0, cc, 65:129], pts,
                               start=(cc == 0), stop=(cc == NCH - 1))
                ins.then_inc(s.s_pv, 1)

    def flush_pe_epilogue(e):
        while pend:
            peg, pqb = pend.pop(0)
            ppvs = s.ps_aux[:, 1024 * pqb:1024 * pqb + 1024]
            e.wait_ge(s.s_ep, 6 * peg + 4)
            e.matmul(ppvs[:, 0:512], s.ones_row[:], s.rrow[0:1, 0, :],
                     start=True, stop=True).then_inc(s.s_bc, 1)
            e.matmul(ppvs[:, 512:1024], s.ones_row[:], s.rrow[0:1, 1, :],
                     start=True, stop=True).then_inc(s.s_bc, 1)

    for u in range(NU):
        if en == "sync" and u + 1 < NU:
            if u >= 1:
                e.wait_ge(s.s_unit, u)
            cfg["dma_unit"](e, s, u + 1, (u + 1) % 2)
        for qg in range(NTILES):
            eg = u * NTILES + qg
            qbuf = qg % 2
            for c in range(NCH):
                gc = eg * NCH + c
                if pair:
                    sb = gc % 2
                    ps_s = s.ps_main[:, 1024 * sb:1024 * (sb + 1)]
                    exp_needed = gc + 1
                else:
                    j = gc // 2
                    sb = j % 2
                    ps_s = s.ps_main[:, 1024 * sb:1024 * (sb + 1)]
                    exp_needed = j + 1
                if en == "tensor":
                    if c == 8:
                        flush_pe_epilogue(e)
                    if qg == 0 and c == 0:
                        e.wait_ge(s.ld_kv2[u % 2],
                                  getattr(s, "kv_base", 0)
                                  + 16 * 5 * (u // 2 + 1))
                    if pair:
                        if gc >= 2:
                            e.wait_ge(s.s_exp, gc - 1)
                        e.matmul(ps_s[:, 0:512],
                                 s.k_buf[0:64, u % 2, 128 * c:128 * (c + 1)],
                                 s.q_buf[0:64, u % 2, qg * NT:(qg + 1) * NT],
                                 start=True, stop=True, tile_position=(0, 0))
                        ins = e.matmul(
                            ps_s[:, 512:1024],
                            s.k_buf[64:128, u % 2, 128 * c:128 * (c + 1)],
                            s.q_buf[64:128, u % 2, qg * NT:(qg + 1) * NT],
                            start=True, stop=True, tile_position=(64, 0))
                    else:
                        if c % 2 == 0 and j >= 2:
                            e.wait_ge(s.s_exp, j - 1)
                        ins = e.matmul(
                            ps_s[:, 512 * (c % 2):512 * (c % 2) + 512],
                            s.k_buf[:, u % 2, 128 * c:128 * (c + 1)],
                            s.q_buf[:, u % 2, qg * NT:(qg + 1) * NT],
                            start=True, stop=True)
                    ins.then_inc(s.s_sc, 1)
                if en == "scalar":
                    if pair or c % 2 == 1:
                        if pair and npt == 4:
                            e.wait_ge(s.s_sc, gc + 1)
                            if gc % 2 == 0 and gc >= 4:
                                e.wait_ge(s.s_pv, gc - 2)
                        else:
                            e.wait_ge(s.s_sc, gc + 1)
                            if pair:
                                if gc >= 2:
                                    e.wait_ge(s.s_pv, gc - 1)
                            else:
                                if j >= 2:
                                    e.wait_ge(s.s_pv, 2 * j - 2)
                        pslot = (gc % npt) if pair else sb
                        e.activation(s.pt[:, pslot, :], ps_s, AF.Exp,
                                     bias=s.C(-cfg["shift"]),
                                     scale=s.C(cfg["scale"])
                                     ).then_inc(s.s_exp, 1)
                if en == "tensor":
                    if qg == 0 and ((pair and c == 1) or
                                    (not pair and c == 3)):
                        e.wait_ge(s.ld_v2[u % 2],
                                  getattr(s, "v_base", 0)
                                  + 16 * 2 * (u // 2 + 1))
                    if pair and c >= 1:
                        emit_pv(e, s, u, eg, qbuf, c - 1)
                    elif not pair and c % 2 == 1 and c >= 3:
                        emit_pv(e, s, u, eg, qbuf, c - 2)
                    if c == NCH - 1:
                        emit_pv(e, s, u, eg, qbuf, c)
            if en == "tensor" and qg == NTILES - 1:
                e.wait_ge(s.s_pv, NCH * (eg + 1))
                e.nop().then_inc(s.s_unit, 1)
            pvs = s.ps_aux[:, 1024 * qbuf:1024 * qbuf + 1024]
            if en == "tensor":
                pend.append((eg, qbuf))
            if en == "vector":
                e.wait_ge(s.s_pv, NCH * (eg + 1))
                if eg >= 1:
                    e.wait_ge(s.s_att, 2 * eg)
                nr1 = 65 if pair else 64
                e.tensor_copy(s.raw[0:65, 0, :], pvs[0:65, 0:512]
                              ).then_inc(s.s_ep, 1)
                e.tensor_copy(s.raw[0:nr1, 1, :], pvs[0:nr1, 512:1024]
                              ).then_inc(s.s_ep, 1)
                e.wait_ge(s.s_ep, 6 * eg + 2)
                dn0 = s.raw[64:65, 0, :]
                dn1 = s.raw[64:65, 1, :] if pair else s.raw[64:65, 0, :]
                e.reciprocal(s.rrow[0:1, 0, :], dn0).then_inc(s.s_ep, 1)
                e.reciprocal(s.rrow[0:1, 1, :], dn1).then_inc(s.s_ep, 1)
                qsl = slice(qg * NT, (qg + 1) * NT)
                e.wait_ge(s.s_bc, 6 * eg + 5)
                e.tensor_tensor(s.attnout[0:64, u, qsl], s.raw[0:64, 0, :],
                                pvs[0:64, 0:512], ALU.mult).then_inc(s.s_att, 1)
                e.wait_ge(s.s_bc, 6 * eg + 6)
                e.tensor_tensor(s.attnout[64:128, u, qsl], s.raw[0:64, 1, :],
                                pvs[0:64, 512:1024], ALU.mult
                                ).then_inc(s.s_att, 1)
    if en == "tensor":
        flush_pe_epilogue(e)



def setup_B(nc, es, s, P, LN, io):
    x_d = io["x0"]
    q_d = io["q0"]
    kf_d = io["k0g"]
    vg_d = io["v0g"]
    wo_d = io["wout0"]
    x1_d = io["x1"]
    s.x_sb = P.get([128, NK, T], F32)
    s.q_buf = P.get([128, 2, T], BF16)
    s.k_buf = P.get([128, 2, 4 * T], BF16)
    s.vslab = P.get([128, 2, 2, 64, 65], BF16)
    s.pt = P.get([128, 4, 1024], BF16)
    s.raw = P.get([128, 2, 512], F32)
    s.rrow = P.get([1, 2, 512], F32)
    s.attnout = P.get([128, NK, T], BF16)
    s.wo_sb = P.get([128, NK, D], BF16)
    for nm in ("ld_x", "ld_w", "s_mm", "s_ev", "st"):
        setattr(s, nm, es.enter_context(nc.semaphore("B" + nm)))
    _attn_alloc(nc, es, s, "B")

    def dma_unit(e, s, u, buf):
        sem = s.ld_kv2[u % 2]
        e.dma_start(out=s.q_buf[:, buf, :],
                    in_=q_d[128 * u:128 * (u + 1), :]).then_inc(sem, 16)
        if getattr(s, "cck_lvl", None):
            e.wait_ge(s.cck, s.cck_lvl)
        for r in range(4):
            e.dma_start(out=s.k_buf[:, buf, 2048 * r:2048 * (r + 1)],
                        in_=kf_d[1024 * r + 128 * u:1024 * r + 128 * (u + 1),
                                 :]).then_inc(sem, 16)
        if getattr(s, "cc_lvl", None):
            e.wait_ge(s.cc, s.cc_lvl)
        for hh in range(2):
            h = 2 * u + hh
            e.dma_start(out=s.vslab[:, buf, hh, :, 0:64],
                        in_=vg_d[:, 64 * h:64 * (h + 1)].rearrange(
                            "(c p) d -> p c d", p=128)
                        ).then_inc(s.ld_v2[u % 2], 16)

    def ph_init(en, e, s):
        _emit_const_init(en, e, s)
        if en == "vector":
            for b in range(2):
                for hh in range(2):
                    e.memset(s.vslab[:, b, hh, :, 64:65], 1.0)
        if en == "sync":
            dma_unit(e, s, 0, 0)
            e.dma_start(out=s.x_sb[:],
                        in_=x_d.rearrange("(m p) t -> p m t", p=128)
                        ).then_inc(s.ld_x, 16)
        if en == "gpsimd":
            e.dma_start(out=s.wo_sb[:],
                        in_=wo_d.rearrange("(k p) n -> p k n", p=128)
                        ).then_inc(s.ld_w, 16)

    def ph_attn(en, e, s):
        _emit_attn(en, e, s, dict(
            n_units=8, pair=True, shift=L0_SHIFT, scale=L0_SCALE,
            pt_slots=4, dma_unit=dma_unit, dma_per_unit=7))

    def ph_oproj(en, e, s):
        mmc = 0
        evc = 0
        if en == "tensor":
            e.wait_ge(s.ld_w, 16)
        if en == "vector":
            e.wait_ge(s.ld_x, 16)
        for m in range(NK):
            for n in range(NTILES):
                slot = mmc % 2
                ps = s.ps_main[:, slot * NT:(slot + 1) * NT]
                mmc += 1
                if en == "tensor":
                    if mmc > 2:
                        e.wait_ge(s.s_ev, mmc - 2)
                    for k in range(NK):
                        ins = e.matmul(ps,
                                       s.wo_sb[:, k, 128 * m:128 * (m + 1)],
                                       s.attnout[:, k, n * NT:(n + 1) * NT],
                                       start=(k == 0), stop=(k == NK - 1))
                    ins.then_inc(s.s_mm, 1)
                evc += 1
                if en == "vector":
                    e.wait_ge(s.s_mm, evc)
                    e.tensor_tensor(s.x_sb[:, m, n * NT:(n + 1) * NT], ps,
                                    s.x_sb[:, m, n * NT:(n + 1) * NT],
                                    ALU.add).then_inc(s.s_ev, 1)

    def ph_store(en, e, s):
        if en == "sync":
            e.dma_start(out=x1_d.rearrange("(m p) t -> p m t", p=128),
                        in_=s.x_sb[:]).then_inc(s.st, 16)
            e.wait_ge(s.st, 16)

    allp = [ph_init, ph_attn, ph_oproj, ph_store]
    lim = globals().get("_B_LIM")
    return allp[:lim] if lim else allp



H0 = 4096

def setup_C(nc, es, s, P, LN, io, cfg_external_prep=False):
    x_d = io["x1"]
    w1t_d = io["f0w1t"]
    w1r_d = io["f0w1r"]
    b1_d = io["f0b1c"]
    w2t_d = io["f0w2t"]
    w2r_d = io["f0w2r"]
    b2_d = io["f0b2c"]
    x2_d = io["x2"]
    h_d = io["hdr"]
    s.xsl = P.get([128, 2, NK, NT], F32)
    s.x2s = P.get([128, NK, NT], F32)
    s.xln = P.get([128, NK, T], BF16)
    for nm in ("row_ex2", "row_mu", "row_tmp", "row_tmp2", "row_a", "row_b"):
        setattr(s, nm, P.get([1, NT], F32))
    s.app_t1 = P.get([128, NT], F32)
    s.wa = P.get([128, 2, D], F32)
    s.ws = P.get([128, 2, D], F32)
    s.sw1 = P.get([128, NK, H0], BF16)
    s.sw2s = P.get([128, 2, D], BF16)
    s.w2ts = P.get([128, 2, D], F32)
    s.hsl = P.get([128, 2, NT], BF16)
    s.hbuf = P.get([128, 2, NT], BF16)
    s.alpha1 = P.get([128, 32], F32)
    s.alpha2 = P.get([128, 8], F32)
    s.a_tmp = P.get([128, 8], F32)
    s.b1_sb = P.get([128, 32], F32)
    s.b2_sb = P.get([128, 8], F32)
    s.ev_t = P.get([128, NT], F32)
    s.xout = P.get([128, 2, NT], F32)
    for nm in ("ld_b", "s_al", "s_sg1", "s_mm", "s_ev", "s_sg2", "s_mm2",
               "s_ev2"):
        setattr(s, nm, es.enter_context(nc.semaphore("C" + nm)))
    s.s_ch = s.s_al
    for nm in ("ld_a", "ld_s", "ld_x", "ld_h", "ld_w2", "st", "st_h"):
        setattr(s, nm + "2",
                [es.enter_context(nc.semaphore(f"C{nm}{i}"))
                 for i in range(2)])
    LN.take(s, "lnC", NTILES, True, None, False, NK)

    def ph_init(en, e, s):
        _emit_const_init(en, e, s)
        if en == "sync":
            e.dma_start(out=s.b1_sb[:], in_=b1_d[:]).then_inc(s.ld_b, 16)
            e.dma_start(out=s.b2_sb[:], in_=b2_d[:]).then_inc(s.ld_b, 16)
            e.wait_ge(s.ld_b, 32)
        if en == "gpsimd":
            e.dma_start(out=s.wa[:, 0, :], in_=w1r_d[0:128, :]
                        ).then_inc(s.ld_a2[0], 16)
            e.dma_start(out=s.ws[:, 0, 0:D], in_=w1t_d[0:128, 0:1024]
                        ).then_inc(s.ld_s2[0], 16)

    WS_SEQ = [(j, k) for j in range(4) for k in range(NK)]
    AFTER_REDUCE = []
    _cc = 64
    for _q in range(32):
        _cc += 1
        AFTER_REDUCE.append(_cc)
        if _q % 4 == 3:
            _cc += 2

    def prep_dma(e, s):
        def ws_load(ii):
            if ii >= 2:
                e.wait_ge(s.s_sg1, ii - 1)
            j, k = WS_SEQ[ii]
            e.dma_start(out=s.ws[:, ii % 2, :],
                        in_=w1t_d[128 * k:128 * (k + 1),
                                  1024 * j:1024 * (j + 1)]
                        ).then_inc(s.ld_s2[ii % 2], 16)

        def wa_load(m):
            if m >= 2:
                e.wait_ge(s.s_al, 2 * (m - 1))
            e.dma_start(out=s.wa[:, m % 2, :],
                        in_=w1r_d[128 * m:128 * (m + 1), :]
                        ).then_inc(s.ld_a2[m % 2], 16)

        for ii in range(1, 10):
            ws_load(ii)
        for ii in range(10, 32):
            wa_load(ii - 9)
            ws_load(ii)
        for m in range(23, 32):
            wa_load(m)
        for q in range(32):
            i2 = 32 + q
            if q == 0:
                e.wait_ge(s.s_al, 64)
            elif q >= 2:
                e.wait_ge(s.s_al, AFTER_REDUCE[q - 2])
            nm2, nj = divmod(q, 4)
            e.dma_start(out=s.wa[:, i2 % 2, :],
                        in_=w2r_d[128 * nm2:128 * (nm2 + 1),
                                  1024 * nj:1024 * (nj + 1)]
                        ).then_inc(s.ld_a2[i2 % 2], 16)

    def prep_vector(e, s):
        rc = 0
        for m in range(32):
            rc += 2
            e.wait_ge(s.ld_a2[m % 2], 16 * (m // 2 + 1))
            if m >= 1:
                e.wait_ge(s.s_al, 2 * m)
            e.tensor_reduce(s.a_tmp[:, 0:1], s.wa[:, m % 2, :], AX,
                            ALU.add, apply_absolute_value=True
                            ).then_inc(s.s_al, 1)
            e.wait_ge(s.s_al, rc - 1)
            e.tensor_scalar(s.alpha1[:, m:m + 1], s.a_tmp[:, 0:1],
                            s.C(1.0 / D), None, ALU.mult
                            ).then_inc(s.s_al, 1)
        for m in range(NK):
            for j in range(4):
                q = m * 4 + j
                i = 32 + q
                rc += 1
                e.wait_ge(s.ld_a2[i % 2], 16 * (i // 2 + 1))
                if j == 0 and m >= 1:
                    e.wait_ge(s.s_al, rc - 1)
                e.tensor_reduce(s.a_tmp[:, j + 1:j + 2], s.wa[:, i % 2, :],
                                AX, ALU.add, apply_absolute_value=True
                                ).then_inc(s.s_al, 1)
            rc += 2
            e.wait_ge(s.s_al, rc - 2)
            e.tensor_reduce(s.a_tmp[:, 0:1], s.a_tmp[:, 1:5], AX,
                            ALU.add).then_inc(s.s_al, 1)
            e.wait_ge(s.s_al, rc - 1)
            e.tensor_scalar(s.alpha2[:, m:m + 1], s.a_tmp[:, 0:1],
                            s.C(1.0 / H0), None, ALU.mult
                            ).then_inc(s.s_al, 1)

    def prep_sign(e, s, ii):
        j, k = WS_SEQ[ii]
        e.wait_ge(s.ld_s2[ii % 2], 16 * (ii // 2 + 1))
        e.activation(s.sw1[:, k, 1024 * j:1024 * (j + 1)],
                     s.ws[:, ii % 2, :], AF.Sign).then_inc(s.s_sg1, 1)

    def pre_lnC(en, e, s, n):
        if en == "sync":
            if n == 0:
                e.dma_start(out=s.xsl[:, 0], in_=x_d[:, 0:NT].rearrange(
                    "(m p) t -> p m t", p=128)).then_inc(s.ld_x2[0], 16)
            if n + 1 < NTILES:
                if n >= 1:
                    e.wait_ge(s.lnC_app, s.lnC_bases["app"] + n)
                e.dma_start(out=s.xsl[:, (n + 1) % 2],
                            in_=x_d[:, (n + 1) * NT:(n + 2) * NT].rearrange(
                                "(m p) t -> p m t", p=128)
                            ).then_inc(s.ld_x2[(n + 1) % 2], 16)
        if en in ("vector", "tensor", "scalar"):
            e.wait_ge(s.ld_x2[n % 2], 16 * (n // 2 + 1))

    def ph_ln(en, e, s):
        _emit_ln(en, e, s, dict(
            prefix="lnC", eps=1e-5, sub_mean=True,
            src=lambda k, n: s.xsl[:, n % 2, k, :],
            src_all=lambda n: s.xsl[:, n % 2, :, :],
            dst=lambda k, n: s.xln[:, k, n * NT:(n + 1) * NT],
            pre=pre_lnC,
        ))

    def ph_w1(en, e, s):
        if en == "gpsimd":
            prep_dma(e, s)
        if en == "vector":
            prep_vector(e, s)
        if en == "scalar":
            for ii in range(8):
                prep_sign(e, s, ii)
        mmc = 0
        for n in range(NTILES):
            for m in range(32):
                slot = mmc % 2
                ps = s.ps_main[:, slot * NT:(slot + 1) * NT]
                mmc += 1
                if en == "tensor":
                    if n == 0 and m % 8 == 0:
                        e.wait_ge(s.s_sg1, 8 * (m // 8 + 1))
                    if mmc > 2:
                        e.wait_ge(s.s_ev, mmc - 2)
                    for k in range(NK):
                        ins = e.matmul(ps,
                                       s.sw1[:, k, 128 * m:128 * (m + 1)],
                                       s.xln[:, k, n * NT:(n + 1) * NT],
                                       start=(k == 0), stop=(k == NK - 1))
                    ins.then_inc(s.s_mm, 1)
                if en == "scalar":
                    if n == 0 and m < 24:
                        prep_sign(e, s, m + 8)
                    e.wait_ge(s.s_mm, mmc)
                    if n == 0:
                        e.wait_ge(s.s_al, 2 * (m + 1))
                    if mmc > 2:
                        e.wait_ge(s.st_h2[(mmc - 1) % 2],
                                  16 * ((mmc - 1) // 2))
                    e.activation(s.hbuf[:, slot, :], ps, AF.Gelu,
                                 bias=s.b1_sb[:, m:m + 1],
                                 scale=s.alpha1[:, m:m + 1]
                                 ).then_inc(s.s_ev, 1)
                if en == "sync":
                    e.wait_ge(s.s_ev, mmc)
                    e.dma_start(out=h_d[128 * m:128 * (m + 1),
                                        n * NT:(n + 1) * NT],
                                in_=s.hbuf[:, slot, :]
                                ).then_inc(s.st_h2[(mmc - 1) % 2], 16)
        if en == "sync":
            e.wait_ge(s.st_h2[0], 16 * (mmc // 2))
            e.wait_ge(s.st_h2[1], 16 * (mmc // 2))

    def ph_w2(en, e, s):
        idx = 0
        for n in range(NTILES):
            for k2 in range(32):
                idx += 1
                slot = idx % 2
                if en == "gpsimd":
                    if idx > 2:
                        e.wait_ge(s.s_sg2, idx - 2)
                    e.dma_start(out=s.w2ts[:, slot, :],
                                in_=w2t_d[128 * k2:128 * (k2 + 1), :]
                                ).then_inc(s.ld_w22[slot], 16)
                if en == "scalar":
                    e.wait_ge(s.ld_w22[slot], 16 * ((idx - 1) // 2 + 1))
                    if idx > 2:
                        e.wait_ge(s.s_mm2, 8 * (idx - 2))
                    e.activation(s.sw2s[:, slot, :], s.w2ts[:, slot, :],
                                 AF.Sign).then_inc(s.s_sg2, 1)
                if en == "sync":
                    if idx > 2:
                        e.wait_ge(s.s_mm2, 8 * (idx - 2))
                    e.dma_start(out=s.hsl[:, slot, :],
                                in_=h_d[128 * k2:128 * (k2 + 1),
                                        n * NT:(n + 1) * NT]
                                ).then_inc(s.ld_h2[slot], 16)
                if en == "tensor":
                    e.wait_ge(s.s_sg2, idx)
                    e.wait_ge(s.ld_h2[slot], 16 * ((idx - 1) // 2 + 1))
                    if k2 == 0 and n > 0:
                        e.wait_ge(s.s_ev2, 8 * n)
                    for m2 in range(NK):
                        arena = s.ps_main if m2 < 4 else s.ps_aux
                        acc = arena[:, 512 * (m2 % 4):512 * (m2 % 4) + 512]
                        ins = e.matmul(acc,
                                       s.sw2s[:, slot, 128 * m2:128 * (m2 + 1)],
                                       s.hsl[:, slot, :],
                                       start=(k2 == 0), stop=(k2 == 31))
                        if m2 == NK - 1:
                            ins.then_inc(s.s_mm2, 8)
            if en == "sync":
                if n >= 2:
                    e.wait_ge(s.s_ev2, 8 * (n - 1))
                e.dma_start(out=s.xsl[:, n % 2],
                            in_=x_d[:, n * NT:(n + 1) * NT].rearrange(
                                "(m p) t -> p m t", p=128)
                            ).then_inc(s.ld_x2[n % 2], 16)
            for m2 in range(NK):
                g = n * NK + m2 + 1
                if en == "vector":
                    if n == 0:
                        e.wait_ge(s.s_al, 64 + 6 * (m2 + 1))
                    if m2 == 0:
                        e.wait_ge(s.s_mm2, 8 * idx)
                        e.wait_ge(s.ld_x2[n % 2], 16 * (2 + n // 2 + 1))
                    if g > 2:
                        e.wait_ge(s.st2[g % 2], 16 * ((g - 2 + 1) // 2))
                    arena = s.ps_main if m2 < 4 else s.ps_aux
                    acc = arena[:, 512 * (m2 % 4):512 * (m2 % 4) + 512]
                    if g >= 2:
                        e.wait_ge(s.s_ev2, g - 1)
                    e.tensor_scalar(s.ev_t[:], acc, s.alpha2[:, m2:m2 + 1],
                                    s.b2_sb[:, m2:m2 + 1], ALU.mult, ALU.add
                                    ).then_inc(s.s_ch, 1)
                    e.wait_ge(s.s_ch, 112 + g)
                    ins = e.tensor_tensor(s.xout[:, g % 2, :], s.ev_t[:],
                                          s.xsl[:, n % 2, m2, :], ALU.add)
                    ins.then_inc(s.s_ev2, 1)
                if en == "sync":
                    e.wait_ge(s.s_ev2, g)
                    e.dma_start(out=x2_d[128 * m2:128 * (m2 + 1),
                                         n * NT:(n + 1) * NT],
                                in_=s.xout[:, g % 2, :]
                                ).then_inc(s.st2[g % 2], 16)
        if en == "sync":
            e.wait_ge(s.st2[0], 16 * NTILES * NK // 2)
            e.wait_ge(s.st2[1], 16 * NTILES * NK // 2)

    return [ph_init, ph_ln, ph_w1, ph_w2]


def _stage_C(inputs, x1_by_core):
    w1 = np.asarray(inputs["L0_ffn_w1"], np.float32)
    w2 = np.asarray(inputs["L0_ffn_w2"], np.float32)
    b1 = np.asarray(inputs["L0_ffn_b1"], np.float32)
    b2 = np.asarray(inputs["L0_ffn_b2"], np.float32)
    com = {
        "w1t": np.ascontiguousarray(w1.T), "w1r": w1,
        "b1c": np.ascontiguousarray(b1.reshape(32, 128).T),
        "w2t": np.ascontiguousarray(w2.T), "w2r": w2,
        "b2c": np.ascontiguousarray(b2.reshape(8, 128).T),
    }
    return [dict(com, x=np.asarray(x1_by_core[c])) for c in range(NCORE)]



def setup_D(nc, es, s, P, LN, io):
    x_d = io["x2"]
    w_d = io["wqkv1"]
    b_d = io["bqkv1"]
    lam_d = io["lams"]
    id_d = io["ident"]
    q_d = io["q1"]
    k_d = io["k1"]
    v_d = io["ve1"]
    s.xsl = P.get([128, 2, NK, NT], F32)
    s.x2s = P.get([128, NK, NT], F32)
    s.xln = P.get([128, NK, T], BF16)
    for nm in ("row_ex2", "row_mu", "row_tmp", "row_tmp2", "row_a", "row_b"):
        setattr(s, nm, P.get([1, NT], F32))
    s.app_t1 = P.get([128, NT], F32)
    s.w_sb = P.get([128, 2, NK, D], BF16)
    s.b_sb = P.get([128, 3, 8], F32)
    s.lam_sb = P.get([128, 4], F32)
    s.lamexp = P.get([1, 2], F32)
    s.lamrow = P.get([1, 2], F32)
    s.neglam = P.get([128, 1], F32)
    s.out_buf = P.get([128, 2, T], BF16)
    s.vfm = P.get([128, NK, T], BF16)
    s.veff_fm = P.get([128, 4, T], BF16)
    s.vtm = P.get([128, T // 128, 512], BF16)
    s.ident = P.get([128, 128], BF16)
    s.ld_x2 = [es.enter_context(nc.semaphore(f"Dld_x{i}")) for i in range(2)]
    s.ld_w2p = [es.enter_context(nc.semaphore(f"Dld_w{i}")) for i in range(2)]
    s.ld_id = es.enter_context(nc.semaphore("Dld_id"))
    for nm in ("s_mm", "s_ev", "s_tr", "s_lam"):
        setattr(s, nm, es.enter_context(nc.semaphore("D" + nm)))
    s.s_cp = s.s_lam
    if getattr(s, "st2", None) is None:
        s.st2 = [es.enter_context(nc.semaphore(f"st2_{i}"))
                 for i in range(2)]
        s.st_vv = es.enter_context(nc.semaphore("st_vv"))
        s.st2_base = [0, 0]
        s.stv_base = 0
    LN.take(s, "lnD", NTILES, True, None, False, NK)

    def ph_init(en, e, s):
        _emit_const_init(en, e, s)
        if en == "sync":
            e.dma_start(out=s.ident[:], in_=id_d[:]).then_inc(s.ld_id, 16)
            e.dma_start(out=s.b_sb[:], in_=b_d.rearrange("j p c -> p j c")
                        ).then_inc(s.ld_id, 16)
            e.dma_start(out=s.lam_sb[:], in_=lam_d[:]).then_inc(s.ld_id, 16)
            e.dma_start(out=s.xsl[:, 0], in_=x_d[:, 0:NT].rearrange(
                "(m p) t -> p m t", p=128)).then_inc(s.ld_x2[0], 16)
            e.wait_ge(s.ld_id, 48)
        if en == "gpsimd":
            e.dma_start(out=s.w_sb[:, 0],
                        in_=w_d[1].rearrange("(k p) n -> p k n", p=128)
                        ).then_inc(s.ld_w2p[0], 16)

    def pre_ln(en, e, s, n):
        if en == "sync" and n + 1 < NTILES:
            if n >= 1:
                e.wait_ge(s.lnD_app, s.lnD_bases["app"] + n)
            e.dma_start(out=s.xsl[:, (n + 1) % 2],
                        in_=x_d[:, (n + 1) * NT:(n + 2) * NT].rearrange(
                            "(m p) t -> p m t", p=128)
                        ).then_inc(s.ld_x2[(n + 1) % 2], 16)
        if en in ("vector", "tensor", "scalar"):
            e.wait_ge(s.ld_x2[n % 2], 16 * (n // 2 + 1))

    def ph_ln(en, e, s):
        _emit_ln(en, e, s, dict(
            prefix="lnD", eps=1e-5, sub_mean=True,
            src=lambda k, n: s.xsl[:, n % 2, k, :],
            src_all=lambda n: s.xsl[:, n % 2, :, :],
            dst=lambda k, n: s.xln[:, k, n * NT:(n + 1) * NT],
            pre=pre_ln,
        ))

    def ph_lam(en, e, s):
        if en == "tensor":
            e.matmul(s.ps_aux[0:1, 0:1], s.lam_sb[:, 0:1], s.lam_sb[:, 1:2],
                     start=True, stop=True)
            e.matmul(s.ps_aux[0:1, 1:2], s.lam_sb[:, 2:3], s.lam_sb[:, 3:4],
                     start=True, stop=True).then_inc(s.s_lam, 1)
        if en == "scalar":
            e.wait_ge(s.s_lam, 1)
            e.activation(s.lamexp[:], s.ps_aux[0:1, 0:2], AF.Exp
                         ).then_inc(s.s_lam, 1)
        if en == "vector":
            e.wait_ge(s.s_lam, 2)
            e.tensor_tensor(s.lamrow[0:1, 0:1], s.lamexp[0:1, 0:1],
                            s.lamexp[0:1, 1:2], ALU.subtract
                            ).then_inc(s.s_lam, 1)
            e.wait_ge(s.s_lam, 3)
            e.tensor_scalar(s.lamrow[0:1, 1:2], s.lamrow[0:1, 0:1],
                            s.C1(0.8), s.C1(-1.0), ALU.add, ALU.mult
                            ).then_inc(s.s_lam, 1)
        if en == "tensor":
            e.wait_ge(s.s_lam, 4)
            e.matmul(s.ps_aux[0:128, 512:513], s.ones_row[:],
                     s.lamrow[0:1, 1:2], start=True, stop=True
                     ).then_inc(s.s_lam, 1)
        if en == "vector":
            e.wait_ge(s.s_lam, 5)
            e.tensor_copy(s.neglam[:], s.ps_aux[:, 512:513]
                          ).then_inc(s.s_lam, 1)

    def ph_proj(en, e, s):
        ORD = [1, 0, 2]
        KIND = ["k", "q", "v"]
        mmc = 0
        evc = 0
        stc = 0
        pb = s.st2_base
        for j in range(3):
            kind = KIND[j]
            if en == "gpsimd" and j + 1 < 3:
                if j > 0:
                    e.wait_ge(s.s_mm, 32 * j)
                e.dma_start(out=s.w_sb[:, (j + 1) % 2],
                            in_=w_d[ORD[j + 1]].rearrange(
                                "(k p) n -> p k n", p=128)
                            ).then_inc(s.ld_w2p[(j + 1) % 2], 16)
            for m in range(NK):
                for n in range(NTILES):
                    slot = mmc % 2
                    ps = s.ps_main[:, slot * NT:(slot + 1) * NT]
                    mmc += 1
                    if en == "tensor":
                        if n == 0 and m == 0:
                            e.wait_ge(s.ld_w2p[j % 2], 16 * (j // 2 + 1))
                        if mmc > 2:
                            e.wait_ge(s.s_ev, mmc - 2)
                        for k in range(NK):
                            ins = e.matmul(
                                ps, s.w_sb[:, j % 2, k, 128 * m:128 * (m + 1)],
                                s.xln[:, k, n * NT:(n + 1) * NT],
                                start=(k == 0), stop=(k == NK - 1))
                        ins.then_inc(s.s_mm, 1)
                    evc += 1
                    if en == "scalar":
                        e.wait_ge(s.s_mm, evc)
                        if kind != "v":
                            if n == 0:
                                sn = (0 if kind == "k" else 8) + m
                                if sn >= 2:
                                    e.wait_ge(s.st2[sn % 2],
                                              pb[sn % 2] + 16 * (sn // 2))
                            dst = s.out_buf[:, m % 2, n * NT:(n + 1) * NT]
                        else:
                            dst = s.vfm[:, m, n * NT:(n + 1) * NT]
                        e.activation(dst, ps, AF.Identity,
                                     bias=s.b_sb[:, ORD[j], m:m + 1]
                                     ).then_inc(s.s_ev, 1)
                if kind != "v":
                    stc += 1
                    if en == "sync":
                        e.wait_ge(s.s_ev, evc)
                        d = k_d if kind == "k" else q_d
                        e.dma_start(out=d[128 * m:128 * (m + 1), :],
                                    in_=s.out_buf[:, m % 2]
                                    ).then_inc(s.st2[(stc - 1) % 2], 16)
            if kind == "k" and en == "sync":
                e.wait_ge(s.st2[0], pb[0] + 16 * 4)
                e.wait_ge(s.st2[1], pb[1] + 16 * 4)
            if kind == "k" and en == "gpsimd" and getattr(s, "kgather", None):
                ksrc, kdst, lvl = s.kgather
                e.wait_ge(s.st2[0], pb[0] + 16 * 4)
                e.wait_ge(s.st2[1], pb[1] + 16 * 4)
                _issue_gather(e, s, s.io, s.cc, ksrc, kdst, s.sim_local)

    def ph_veff(en, e, s):
        vc = 0
        for i in range(4):
            vc += 1
            if en == "vector":
                e.scalar_tensor_tensor(s.veff_fm[:, i, :], s.vfm[:, i + 4, :],
                                       s.neglam[:], s.vfm[:, i, :],
                                       ALU.mult, ALU.add).then_inc(s.s_ev, 1)
        trc = 0
        cpc = 0
        for i in range(4):
            for tb in range(T // 128):
                trc += 1
                if en == "tensor":
                    e.wait_ge(s.s_ev, 96 + 4)
                    if trc > 1:
                        e.wait_ge(s.s_cp, 6 + trc - 1)
                    e.transpose(s.ps_t[:, 0:128],
                                s.veff_fm[:, i, 128 * tb:128 * (tb + 1)],
                                s.ident[:]).then_inc(s.s_tr, 1)
                cpc += 1
                if en == "vector":
                    e.wait_ge(s.s_tr, trc)
                    e.tensor_copy(s.vtm[:, tb, 128 * i:128 * (i + 1)],
                                  s.ps_t[:, 0:128]).then_inc(s.s_cp, 1)

    def ph_store(en, e, s):
        if en == "sync":
            e.dma_start(out=v_d.rearrange("(tb p) d -> p tb d", p=128),
                        in_=s.vtm[:]).then_inc(s.st_vv, 16)
            e.wait_ge(s.st2[0], s.st2_base[0] + 16 * 8)
            e.wait_ge(s.st2[1], s.st2_base[1] + 16 * 8)
            e.wait_ge(s.st_vv, s.stv_base + 16)

    return [ph_init, ph_ln, ph_lam, ph_proj, ph_veff, ph_store]


def _stage_D(inputs, x2_by_core):
    w3 = np.ascontiguousarray(np.stack([
        np.asarray(inputs["L1_attn_Wq"], np.float32),
        np.asarray(inputs["L1_attn_Wk"], np.float32),
        np.asarray(inputs["L1_attn_Wv"], np.float32)]))
    b3 = np.ascontiguousarray(np.stack([
        np.asarray(inputs["L1_attn_bq"], np.float32).reshape(8, 128).T,
        np.asarray(inputs["L1_attn_bk"], np.float32).reshape(8, 128).T,
        np.asarray(inputs["L1_attn_bv"], np.float32).reshape(8, 128).T]))
    lams = np.ascontiguousarray(np.stack(
        [np.asarray(inputs[f"L1_attn_lambda_{nm}"], np.float32)
         for nm in ("q1", "k1", "q2", "k2")], axis=1))
    ident = np.eye(128, dtype=bf16)
    com = {"wqkv": w3, "bqkv": b3, "lams": lams, "ident": ident}
    return [dict(com, x=np.asarray(x2_by_core[c])) for c in range(NCORE)]



def setup_E(nc, es, s, P, LN, io):
    x_d = io["x2"]
    q_d = io["q1"]
    kf_d = io["k1g"]
    veg_d = io["ve1g"]
    wo_d = io["wo1"]
    bo_d = io["boc"]
    lnw_d = io["lnwc"]
    lnb_d = io["lnbc"]
    x3_d = io["x3"]
    s.q_buf = P.get([128, 2, T], BF16)
    s.k_buf = P.get([128, 2, 4 * T], BF16)
    s.vslab = P.get([128, 2, 1, 64, 129], BF16)
    s.pt = P.get([128, 2, 1024], BF16)
    s.raw = P.get([128, 2, 512], F32)
    s.rrow = P.get([1, 2, 512], F32)
    s.attnout = P.get([128, NK, T], BF16)
    s.x2s = P.get([128, NK, NT], F32)
    for nm in ("row_ex2", "row_mu", "row_tmp", "row_tmp2", "row_a", "row_b"):
        setattr(s, nm, P.get([1, NT], F32))
    s.app_t1 = P.get([128, NT], F32)
    s.app_t2 = P.get([128, NT], F32)
    s.wo_sb = P.get([128, NK, D], BF16)
    s.bo_sb = P.get([128, 8], F32)
    s.lnw_sb = P.get([128, 8], F32)
    s.lnb_sb = P.get([128, 8], F32)
    s.lnb02 = P.get([128, 8], F32)
    s.xsl = P.get([128, 2, NK, NT], F32)
    s.ev_t = P.get([128, NT], F32)
    s.xout = P.get([128, 2, NT], F32)
    s.ld_x2 = [es.enter_context(nc.semaphore(f"Eld_x{i}")) for i in range(2)]
    s.ld_w = es.enter_context(nc.semaphore("Eld_w"))
    s.ld_wo = es.enter_context(nc.semaphore("Eld_wo"))
    for nm in ("s_mm", "s_ev", "s_ch"):
        setattr(s, nm, es.enter_context(nc.semaphore("E" + nm)))
    s.st2 = [es.enter_context(nc.semaphore(f"Est{i}")) for i in range(2)]
    _attn_alloc(nc, es, s, "E")
    LN.take(s, "lnE", NTILES, True, 0.2, True, NK)

    def dma_unit(e, s, u, buf):
        sem = s.ld_kv2[u % 2]
        e.dma_start(out=s.q_buf[:, buf, :],
                    in_=q_d[128 * u:128 * (u + 1), :]).then_inc(sem, 16)
        if getattr(s, "cck_lvl", None):
            e.wait_ge(s.cck, s.cck_lvl)
        for r in range(4):
            e.dma_start(out=s.k_buf[:, buf, 2048 * r:2048 * (r + 1)],
                        in_=kf_d[1024 * r + 128 * u:1024 * r + 128 * (u + 1),
                                 :]).then_inc(sem, 16)
        if getattr(s, "cc_lvl", None):
            e.wait_ge(s.cc, s.cc_lvl)
        m = u % 4
        e.dma_start(out=s.vslab[:, buf, 0, :, 0:64],
                    in_=veg_d[:, 128 * m:128 * m + 64].rearrange(
                        "(c p) d -> p c d", p=128)).then_inc(s.ld_v2[u % 2], 16)
        e.dma_start(out=s.vslab[:, buf, 0, :, 65:129],
                    in_=veg_d[:, 128 * m + 64:128 * (m + 1)].rearrange(
                        "(c p) d -> p c d", p=128)).then_inc(s.ld_v2[u % 2], 16)

    def ph_init(en, e, s):
        _emit_const_init(en, e, s)
        if en == "vector":
            for b in range(2):
                e.memset(s.vslab[:, b, 0, :, 64:65], 1.0)
        if en == "sync":
            e.dma_start(out=s.bo_sb[:], in_=bo_d[:]).then_inc(s.ld_w, 16)
            e.dma_start(out=s.lnw_sb[:], in_=lnw_d[:]).then_inc(s.ld_w, 16)
            e.dma_start(out=s.lnb_sb[:], in_=lnb_d[:]).then_inc(s.ld_w, 16)
            e.wait_ge(s.ld_w, 48)
        if en == "vector":
            e.wait_ge(s.ld_w, 48)
            e.tensor_scalar(s.lnb02[:], s.lnb_sb[:], s.C(0.2), None,
                            ALU.mult)
        if en == "gpsimd":
            e.dma_start(out=s.wo_sb[:],
                        in_=wo_d.rearrange("(k p) n -> p k n", p=128)
                        ).then_inc(s.ld_wo, 16)

    def ph_attn(en, e, s):
        if en == "gpsimd" and getattr(s, "vgather", None):
            _issue_gather(e, s, s.io, s.cc, *s.vgather, s.sim_local)
        if en == "sync":
            dma_unit(e, s, 0, 0)
        _emit_attn(en, e, s, dict(
            n_units=8, pair=False, shift=L1_SHIFT, scale=L1_SCALE,
            dma_unit=dma_unit, dma_per_unit=7))

    def ph_innerln(en, e, s):
        _emit_ln(en, e, s, dict(
            prefix="lnE", eps=1e-5, sub_mean=True, src_bf16=True,
            post_scale=0.2,
            src=lambda k, n: s.attnout[:, k, n * NT:(n + 1) * NT],
            src_all=lambda n: s.attnout[:, :, n * NT:(n + 1) * NT],
            dst=lambda k, n: s.attnout[:, k, n * NT:(n + 1) * NT],
            lnw_col=lambda k: s.lnw_sb[:, k:k + 1],
            lnb_col=lambda k: s.lnb02[:, k:k + 1],
        ))

    def ph_oproj(en, e, s):
        mmc = 0
        for n in range(NTILES):
            if en == "sync":
                if n >= 2:
                    e.wait_ge(s.s_ev, 8 * (n - 1))
                e.dma_start(out=s.xsl[:, n % 2],
                            in_=x_d[:, n * NT:(n + 1) * NT].rearrange(
                                "(m p) t -> p m t", p=128)
                            ).then_inc(s.ld_x2[n % 2], 16)
            for m in range(NK):
                g = n * NK + m + 1
                slot = mmc % 2
                ps = s.ps_main[:, slot * NT:(slot + 1) * NT]
                mmc += 1
                if en == "tensor":
                    if mmc == 1:
                        e.wait_ge(s.ld_wo, 16)
                    if mmc > 2:
                        e.wait_ge(s.s_ev, mmc - 2)
                    for k in range(NK):
                        ins = e.matmul(ps,
                                       s.wo_sb[:, k, 128 * m:128 * (m + 1)],
                                       s.attnout[:, k, n * NT:(n + 1) * NT],
                                       start=(k == 0), stop=(k == NK - 1))
                    ins.then_inc(s.s_mm, 1)
                if en == "vector":
                    if m == 0:
                        e.wait_ge(s.ld_x2[n % 2], 16 * (n // 2 + 1))
                    e.wait_ge(s.s_mm, g)
                    if g > 2:
                        e.wait_ge(s.st2[g % 2], 16 * ((g - 1) // 2))
                    if g >= 2:
                        e.wait_ge(s.s_ev, g - 1)
                    e.tensor_scalar(s.ev_t[:], ps, s.bo_sb[:, m:m + 1], None,
                                    ALU.add).then_inc(s.s_ch, 1)
                    e.wait_ge(s.s_ch, g)
                    e.tensor_tensor(s.xout[:, g % 2, :], s.ev_t[:],
                                    s.xsl[:, n % 2, m, :], ALU.add
                                    ).then_inc(s.s_ev, 1)
                if en == "sync":
                    e.wait_ge(s.s_ev, g)
                    e.dma_start(out=x3_d[128 * m:128 * (m + 1),
                                         n * NT:(n + 1) * NT],
                                in_=s.xout[:, g % 2, :]
                                ).then_inc(s.st2[g % 2], 16)
        if en == "sync":
            e.wait_ge(s.st2[0], 16 * 16)
            e.wait_ge(s.st2[1], 16 * 16)

    return [ph_init, ph_attn, ph_innerln, ph_oproj]


def _stage_E(inputs, resD):
    Wo = np.asarray(inputs["L1_attn_Wo"], np.float32)
    bo = np.asarray(inputs["L1_attn_bo"], np.float32).reshape(8, 128).T
    lnw = np.asarray(inputs["L1_attn_ln_w"], np.float32).reshape(8, 128).T
    lnb = np.asarray(inputs["L1_attn_ln_b"], np.float32).reshape(8, 128).T
    com = {"wo": Wo, "boc": np.ascontiguousarray(bo),
           "lnwc": np.ascontiguousarray(lnw),
           "lnbc": np.ascontiguousarray(lnb)}
    maps = []
    for g in range(2):
        cs = range(4 * g, 4 * g + 4)
        kf = np.concatenate([np.asarray(resD[c]["k"]) for c in cs], axis=1)
        vf = np.concatenate([np.asarray(resD[c]["veff"]) for c in cs], axis=0)
        v4 = vf.reshape(64, 128, 4, 128)
        vsa = np.ascontiguousarray(v4[:, :, :, 0:64].transpose(2, 1, 0, 3))
        vsb = np.ascontiguousarray(v4[:, :, :, 64:128].transpose(2, 1, 0, 3))
        for c in cs:
            maps.append(dict(com, q=np.asarray(resD[c]["q"]), kf=kf,
                             vsa=vsa, vsb=vsb))
    return maps



H1 = 2048
TH = 1024
NTH = TH // NT


def setup_F(nc, es, s, P, LN, io):
    x_d = io["x3"]
    nw_d = io["nwc"]
    gw_d = io["gwt"]
    w1_d = io["mw1t"]
    b1_d = io["mb1c"]
    w2_d = io["mw2t"]
    b2_d = io["mb2r"]
    sel_d = io["sel8"]
    x4_d = io["x4"]
    s.acc = P.get([128, NK, TH], F32)
    s.xln_sb = P.get([128, NK, TH], BF16)
    s.t_sb = s.xln_sb
    s.x2s = P.get([128, NK, NT], F32)
    for nm in ("row_ex2", "row_mu", "row_tmp", "row_tmp2", "row_a", "row_b"):
        setattr(s, nm, P.get([1, NT], F32))
    s.app_t1 = P.get([128, NT], F32)
    s.w1_sb = P.get([128, NK, H1], BF16)
    s.w2_sb = P.get([128, 16, D], BF16)
    s.h_sb = P.get([128, 16, NT], BF16)
    s.gw_sb = P.get([128, NK, 8], BF16)
    s.b1_sb = P.get([128, 8, 16], F32)
    s.b2_sb = P.get([8, D], BF16)
    s.nw_sb = P.get([128, 8], F32)
    s.sel_sb = P.get([8, 8, 128], BF16)
    s.g_sb = P.get([8, NT], F32)
    s.gtmp = P.get([8, NT], F32)
    s.gm4 = P.get([4, NT], F32)
    s.gm2 = P.get([2, NT], F32)
    s.m1_row = P.get([1, NT], F32)
    s.m2_row = P.get([1, NT], F32)
    s.e_row = P.get([1, NT], F32)
    s.r1_row = P.get([1, NT], F32)
    s.r2_row = P.get([1, NT], F32)
    s.eq1 = P.get([8, NT], F32)
    s.eq2 = P.get([8, NT], F32)
    s.msk = P.get([8, NT], F32)
    s.ct1 = P.get([8, NT], F32)
    s.c_bf = P.get([8, NTH, NT], BF16)
    s.cbc_sb = P.get([128, NT], F32)
    s.ev_t = P.get([128, NT], F32)
    for nm in ("ld_c", "st_x4", "s_g", "s_gv",
               "s_gb", "s_b2", "s_b2e", "s_mm1", "s_gel", "s_mm2", "s_ev2",
               "s_ch", "s_cbc", "s_cbcmm", "ld_cast", "s_addln",
               "ld_w1", "ld_w2x"):
        setattr(s, nm, es.enter_context(nc.semaphore("G" + nm)))
    s.ld_acc = s.ld_c
    s.sh = s.ld_cast
    for half in range(2):
        LN.take(s, f"lnf{half}", NTH, True, None, False, NK)
        LN.take(s, f"rms{half}", NTH, False, None, False, NK)

    def ph_init_g(en, e, s):
        _emit_const_init(en, e, s)
        if en == "sync":
            e.dma_start(out=s.nw_sb[:], in_=nw_d[:]).then_inc(s.ld_c, 16)
            e.dma_start(out=s.b1_sb[:], in_=b1_d.rearrange("e p c -> p e c")
                        ).then_inc(s.ld_c, 16)
            e.dma_start(out=s.sel_sb[:], in_=sel_d[:]).then_inc(s.ld_c, 16)
            e.dma_start(out=s.acc[:], in_=x_d[:, 0:TH].rearrange(
                "(m p) t -> p m t", p=128)).then_inc(s.ld_acc, 16)
            e.wait_ge(s.ld_c, 64)
        if en == "gpsimd":
            e.dma_start(out=s.gw_sb[:], in_=gw_d.rearrange(
                "(k p) c -> p k c", p=128)).then_inc(s.ld_cast, 16)
            e.dma_start(out=s.b2_sb[:], in_=b2_d[:]).then_inc(s.ld_cast, 16)
            e.dma_start(out=s.w1_sb[:], in_=w1_d[0].rearrange(
                "(k p) h -> p k h", p=128)).then_inc(s.ld_w1, 16)
            e.dma_start(out=s.w2_sb[:], in_=w2_d[0].rearrange(
                "(k p) d -> p k d", p=128)).then_inc(s.ld_w2x, 16)
            e.wait_ge(s.ld_cast, 32)

    def mk_init_h(half):
        def ph(en, e, s):
            if half == 1 and en == "sync":
                e.dma_start(out=s.acc[:], in_=x_d[:, TH:2 * TH].rearrange(
                    "(m p) t -> p m t", p=128)).then_inc(s.ld_acc, 16)
                e.wait_ge(s.ld_acc, 80)
        return ph

    def mk_preln(half):
        def ph(en, e, s):
            _emit_ln(en, e, s, dict(
                prefix=f"lnf{half}", eps=1e-5, sub_mean=True, ntiles=NTH,
                src=lambda k, n: s.acc[:, k, n * NT:(n + 1) * NT],
                src_all=lambda n: s.acc[:, :, n * NT:(n + 1) * NT],
                dst=lambda k, n: s.xln_sb[:, k, n * NT:(n + 1) * NT],
            ))
        return ph

    def mk_addln(half):
        def ph(en, e, s):
            cnt = 16 * half
            for n in range(NTH):
                for k in range(NK):
                    cnt += 1
                    if en == "vector":
                        nsl = slice(n * NT, (n + 1) * NT)
                        e.tensor_tensor(s.acc[:, k, nsl], s.xln_sb[:, k, nsl],
                                        s.acc[:, k, nsl], ALU.add
                                        ).then_inc(s.s_addln, 1)
        return ph

    def mk_rms(half):
        def ph(en, e, s):
            _emit_ln(en, e, s, dict(
                prefix=f"rms{half}", eps=1e-6, sub_mean=False, ntiles=NTH,
                src_bf16=True,
                src=lambda k, n: s.xln_sb[:, k, n * NT:(n + 1) * NT],
                src_all=lambda n: s.xln_sb[:, :, n * NT:(n + 1) * NT],
                dst=lambda k, n: s.xln_sb[:, k, n * NT:(n + 1) * NT],
                weight_col=lambda k: s.nw_sb[:, k:k + 1],
            ))
        return ph

    GATE_P = 18

    def mk_gate(half):
        def ph(en, e, s):
            ps_g = s.ps_main[0:8, 0:512]
            ps_b = s.ps_main[:, 512:1024]
            gvc = GATE_P * half * NTH
            shc = 6 * half * NTH
            gbc = 4 * half * NTH
            b2c = NK * half * NTH
            for n in range(NTH):
                gn = half * NTH + n
                if en == "tensor":
                    if gn > 0:
                        e.wait_ge(s.s_gv, GATE_P * (gn - 1) + 1)
                    for k in range(NK):
                        ins = e.matmul(ps_g, s.gw_sb[:, k, :],
                                       s.t_sb[:, k, n * NT:(n + 1) * NT],
                                       start=(k == 0), stop=(k == NK - 1))
                    ins.then_inc(s.s_g, 1)

                def dve(op):
                    nonlocal gvc
                    if en == "vector":
                        e.wait_ge(s.s_gv, gvc)
                        op().then_inc(s.s_gv, 1)
                    gvc += 1

                def shift(src_ap, dst_ap):
                    nonlocal shc
                    shc += 1
                    if en == "gpsimd":
                        e.wait_ge(s.s_gv, gvc)
                        e.dma_start(out=dst_ap, in_=src_ap).then_inc(s.sh, 16)


                def bcast(row):
                    nonlocal gbc
                    gbc += 1
                    if en == "tensor":
                        e.wait_ge(s.s_gv, gvc)
                        e.matmul(ps_b[0:8, :], s.ones_row[0:1, 0:8], row,
                                 start=True, stop=True).then_inc(s.s_gb, 1)
                    if en == "vector":
                        e.wait_ge(s.s_gb, gbc)

                if en == "vector":
                    e.wait_ge(s.s_g, gn + 1)
                dve(lambda: e.tensor_copy(s.g_sb[:], ps_g))
                shift(s.g_sb[4:8, :], s.gtmp[0:4, :])
                if en == "vector":
                    e.wait_ge(s.sh, 32 + 16 * shc)
                dve(lambda: e.tensor_tensor(s.gm4[:], s.g_sb[0:4, :],
                                            s.gtmp[0:4, :], ALU.max))
                shift(s.gm4[2:4, :], s.gtmp[0:2, :])
                if en == "vector":
                    e.wait_ge(s.sh, 32 + 16 * shc)
                dve(lambda: e.tensor_tensor(s.gm2[:], s.gm4[0:2, :],
                                            s.gtmp[0:2, :], ALU.max))
                shift(s.gm2[1:2, :], s.gtmp[0:1, :])
                if en == "vector":
                    e.wait_ge(s.sh, 32 + 16 * shc)
                dve(lambda: e.tensor_tensor(s.m1_row[:], s.gm2[0:1, :],
                                            s.gtmp[0:1, :], ALU.max))
                bcast(s.m1_row[:])
                dve(lambda: e.tensor_tensor(s.eq1[:], s.g_sb[:], ps_b[0:8, :],
                                            ALU.is_equal))
                dve(lambda: e.scalar_tensor_tensor(s.msk[:], s.eq1[:],
                                                   s.Cn(-1e30, 8), s.g_sb[:],
                                                   ALU.mult, ALU.add))
                shift(s.msk[4:8, :], s.gtmp[0:4, :])
                if en == "vector":
                    e.wait_ge(s.sh, 32 + 16 * shc)
                dve(lambda: e.tensor_tensor(s.gm4[:], s.msk[0:4, :],
                                            s.gtmp[0:4, :], ALU.max))
                shift(s.gm4[2:4, :], s.gtmp[0:2, :])
                if en == "vector":
                    e.wait_ge(s.sh, 32 + 16 * shc)
                dve(lambda: e.tensor_tensor(s.gm2[:], s.gm4[0:2, :],
                                            s.gtmp[0:2, :], ALU.max))
                shift(s.gm2[1:2, :], s.gtmp[0:1, :])
                if en == "vector":
                    e.wait_ge(s.sh, 32 + 16 * shc)
                dve(lambda: e.tensor_tensor(s.m2_row[:], s.gm2[0:1, :],
                                            s.gtmp[0:1, :], ALU.max))
                bcast(s.m2_row[:])
                dve(lambda: e.tensor_tensor(s.eq2[:], s.msk[:], ps_b[0:8, :],
                                            ALU.is_equal))
                dve(lambda: e.tensor_tensor(s.e_row[:], s.m2_row[:],
                                            s.m1_row[:], ALU.subtract))
                if en == "scalar":
                    e.wait_ge(s.s_gv, gvc)
                    e.activation(s.e_row[:], s.e_row[:], AF.Exp
                                 ).then_inc(s.s_gv, 1)
                gvc += 1
                dve(lambda: e.tensor_scalar(s.r2_row[:], s.e_row[:],
                                            s.C1(1.0), None, ALU.add))
                dve(lambda: e.reciprocal(s.r1_row[:], s.r2_row[:]))
                dve(lambda: e.tensor_tensor(s.r2_row[:], s.e_row[:],
                                            s.r1_row[:], ALU.mult))
                bcast(s.r1_row[:])
                dve(lambda: e.tensor_tensor(s.ct1[:], s.eq1[:], ps_b[0:8, :],
                                            ALU.mult))
                bcast(s.r2_row[:])
                dve(lambda: e.tensor_tensor(s.eq2[:], s.eq2[:], ps_b[0:8, :],
                                            ALU.mult))
                dve(lambda: e.tensor_tensor(s.c_bf[:, n, :], s.ct1[:],
                                            s.eq2[:], ALU.add))
                assert gvc == GATE_P * (gn + 1), (gvc, gn)
                for m2 in range(NK):
                    b2c += 1
                    slot = b2c % 2
                    psb2 = s.ps_main[:, 1024 + 512 * slot:1536 + 512 * slot]
                    if en == "tensor":
                        e.wait_ge(s.s_gv, GATE_P * (gn + 1))
                        if b2c > 2:
                            e.wait_ge(s.s_b2e, b2c - 2)
                        e.matmul(psb2, s.b2_sb[0:8, 128 * m2:128 * (m2 + 1)],
                                 s.c_bf[:, n, :], start=True, stop=True
                                 ).then_inc(s.s_b2, 1)
                    if en == "vector":
                        e.wait_ge(s.s_b2, b2c)
                        e.tensor_tensor(s.acc[:, m2, n * NT:(n + 1) * NT],
                                        psb2,
                                        s.acc[:, m2, n * NT:(n + 1) * NT],
                                        ALU.add).then_inc(s.s_b2e, 1)
        return ph

    def mk_experts(half):
        def ph(en, e, s):
            for ei in range(8):
                ge = half * 8 + ei
                if en == "gpsimd" and ge + 1 < 16:
                    nxt = (ge + 1) % 8
                    e.wait_ge(s.s_mm1, 32 * (ge + 1))
                    e.dma_start(out=s.w1_sb[:], in_=w1_d[nxt].rearrange(
                        "(k p) h -> p k h", p=128)).then_inc(s.ld_w1, 16)
                    e.wait_ge(s.s_mm2, 32 * (ge + 1))
                    e.dma_start(out=s.w2_sb[:], in_=w2_d[nxt].rearrange(
                        "(k p) d -> p k d", p=128)).then_inc(s.ld_w2x, 16)
                for n in range(NTH):
                    tau = NTH * ge + n
                    nsl = slice(n * NT, (n + 1) * NT)
                    for m1 in range(16):
                        g1 = 16 * tau + m1 + 1
                        slot = g1 % 2
                        ps = s.ps_main[:, 512 * slot:512 * (slot + 1)]
                        if en == "tensor":
                            if m1 == 0 and n == 0:
                                e.wait_ge(s.ld_w1, 16 * (ge + 1))
                            if m1 == 0:
                                if tau > 0:
                                    e.wait_ge(s.s_ev2, 8 * (tau - 1) + 2)
                            if g1 > 2:
                                e.wait_ge(s.s_gel, g1 - 2)
                            for k in range(NK):
                                ins = e.matmul(
                                    ps, s.w1_sb[:, k, 128 * m1:128 * (m1 + 1)],
                                    s.t_sb[:, k, nsl],
                                    start=(k == 0), stop=(k == NK - 1))
                            ins.then_inc(s.s_mm1, 1)
                        if en == "scalar":
                            e.wait_ge(s.s_mm1, g1)
                            if tau > 0 and m1 == 0:
                                e.wait_ge(s.s_mm2, 16 * tau)
                            e.activation(s.h_sb[:, m1, :], ps, AF.Gelu,
                                         bias=s.b1_sb[:, ei, m1:m1 + 1]
                                         ).then_inc(s.s_gel, 1)
                    if en == "tensor":
                        if tau > 0:
                            e.wait_ge(s.s_ev2, 8 * (tau - 1) + 3)
                        e.matmul(s.ps_main[:, 1024:1536],
                                 s.sel_sb[:, ei, :], s.c_bf[:, n, :],
                                 start=True, stop=True).then_inc(s.s_cbcmm, 1)
                    if en == "vector":
                        e.wait_ge(s.s_cbcmm, tau + 1)
                        if tau > 0:
                            e.wait_ge(s.s_ev2, 8 * tau)
                        e.tensor_copy(s.cbc_sb[:], s.ps_main[:, 1024:1536]
                                      ).then_inc(s.s_cbc, 1)
                    for k2 in range(16):
                        g2 = 16 * tau + k2 + 1
                        if en == "tensor":
                            if k2 == 0:
                                if n == 0:
                                    e.wait_ge(s.ld_w2x, 16 * (ge + 1))
                                e.wait_ge(s.s_gel, 16 * (tau + 1))
                                e.wait_ge(s.s_cbc, tau + 1)
                                if tau > 0:
                                    e.wait_ge(s.s_ev2, 8 * tau)
                            for m2 in range(NK):
                                arena = s.ps_main if m2 < 4 else s.ps_aux
                                accp = arena[:, 512 * (m2 % 4):
                                             512 * (m2 % 4) + 512]
                                ins = e.matmul(
                                    accp,
                                    s.w2_sb[:, k2, 128 * m2:128 * (m2 + 1)],
                                    s.h_sb[:, k2, :],
                                    start=(k2 == 0), stop=(k2 == 15))
                                if m2 == NK - 1:
                                    ins.then_inc(s.s_mm2, 1)
                    for m2 in range(NK):
                        gv = 8 * tau + m2 + 1
                        if en == "vector":
                            if m2 == 0:
                                e.wait_ge(s.s_mm2, 16 * (tau + 1))
                                e.wait_ge(s.s_cbc, tau + 1)
                            arena = s.ps_main if m2 < 4 else s.ps_aux
                            accp = arena[:, 512 * (m2 % 4):
                                         512 * (m2 % 4) + 512]
                            if gv >= 2:
                                e.wait_ge(s.s_ev2, gv - 1)
                            e.tensor_tensor(s.ev_t[:], accp, s.cbc_sb[:],
                                            ALU.mult).then_inc(s.s_ch, 1)
                            e.wait_ge(s.s_ch, gv)
                            e.tensor_tensor(s.acc[:, m2, nsl], s.ev_t[:],
                                            s.acc[:, m2, nsl], ALU.add
                                            ).then_inc(s.s_ev2, 1)
        return ph

    def mk_store(half):
        def ph(en, e, s):
            if en == "sync":
                e.dma_start(out=x4_d[:, half * TH:(half + 1) * TH].rearrange(
                    "(m p) t -> p m t", p=128), in_=s.acc[:]
                    ).then_inc(s.st_x4, 16)
                e.wait_ge(s.st_x4, 16 * (half + 1))
        return ph

    phases = [ph_init_g]
    for half in range(2):
        phases += [mk_init_h(half), mk_preln(half), mk_addln(half),
                   mk_rms(half), mk_gate(half), mk_experts(half),
                   mk_store(half)]
    return phases


def _stage_F(inputs, x3_by_core):
    w1 = np.asarray(inputs["L1_ffn_w1"], np.float32)
    w2 = np.asarray(inputs["L1_ffn_w2"], np.float32)
    b1 = np.asarray(inputs["L1_ffn_b1"], np.float32)
    b2 = np.asarray(inputs["L1_ffn_b2"], np.float32)
    nw = np.asarray(inputs["L1_ffn_norm_weight"], np.float32)
    gw = np.asarray(inputs["L1_ffn_gate_w"], np.float32)
    sel = np.zeros((8, 8, 128), dtype=bf16)
    for ee in range(8):
        sel[ee, ee, :] = 1
    com = {
        "nwc": np.ascontiguousarray(nw.reshape(8, 128).T),
        "gwt": np.ascontiguousarray(gw.T),
        "w1t": np.ascontiguousarray(w1.transpose(0, 2, 1)),
        "b1c": np.ascontiguousarray(b1.reshape(8, 16, 128).transpose(0, 2, 1)),
        "w2t": np.ascontiguousarray(w2.transpose(0, 2, 1)),
        "b2r": b2,
        "sel8": sel,
    }
    return [dict(com, x=np.asarray(x3_by_core[c])) for c in range(NCORE)]



U8 = mybir.dt.uint8
ARENA_BYTES = 208000
CONSTS_ALL = [1.0 / D, -1.0, -0.5, 1e-5, 1e-6, 0.8, -L0_SHIFT, L0_SCALE,
              L1_SCALE, 0.2, 1.0 / H0, -1e30, 1.0]
GROUPS4 = [[0, 1, 2, 3], [4, 5, 6, 7]]


def _mk_base(nc, es, s):
    _setup_consts(nc, es, s, CONSTS_ALL)
    s.ones_row = es.enter_context(nc.sbuf_tensor("ones_row", [1, 128], F32))
    s.ones_col = es.enter_context(nc.sbuf_tensor("ones_col", [128, 1], F32))
    s.ones_col_b = es.enter_context(
        nc.sbuf_tensor("ones_col_b", [128, 1], BF16))
    arena = es.enter_context(nc.sbuf_tensor("arena", [128, ARENA_BYTES], U8))
    P = Pool(arena, ARENA_BYTES)
    pm = es.enter_context(nc.psum_tensor("ps_main_t", [128, 2048], F32))
    pa = es.enter_context(nc.psum_tensor("ps_aux_t", [128, 2048], F32))
    s.ps_main = pm.ap()
    s.ps_aux = pa.ap()
    s.ps_t = s.ps_aux[:, 1536:2048].bitcast(BF16)
    bar = es.enter_context(nc.semaphore("bar"))
    return P, bar


def _share(nc, es, s0):
    s = S()
    _link(s, s0)
    s.ps_main = s0.ps_main
    s.ps_aux = s0.ps_aux
    s.ps_t = s0.ps_t
    s.skip_consts = True
    return s


_IOSPEC = dict(
    x0=([D, T], F32), wqkv0=([3, D, D], F32), ident=([128, 128], BF16),
    q0=([D, T], BF16), k0=([D, T], BF16), v0=([T, D], BF16),
    k0g=([4 * D, T], BF16), v0g=([4 * T, D], BF16),
    kv0=([2 * D, T], BF16), kv0g=([8 * D, T], BF16),
    kv1=([D + 512, T], BF16), kv1g=([4 * (D + 512), T], BF16),
    wout0=([D, D], F32), x1=([D, T], F32),
    f0w1t=([D, H0], F32), f0w1r=([H0, D], F32), f0b1c=([128, 32], F32),
    f0w2t=([H0, D], F32), f0w2r=([D, H0], F32), f0b2c=([128, 8], F32),
    hdr=([H0, T], BF16), x2=([D, T], F32),
    wqkv1=([3, D, D], F32), bqkv1=([3, 128, 8], F32), lams=([128, 4], F32),
    q1=([D, T], BF16), k1=([D, T], BF16), ve1=([T, 512], BF16),
    k1g=([4 * D, T], BF16), ve1g=([4 * T, 512], BF16),
    wo1=([D, D], F32), boc=([128, 8], F32), lnwc=([128, 8], F32),
    lnbc=([128, 8], F32), x3=([D, T], F32),
    nwc=([128, 8], F32), gwt=([D, 8], F32), mw1t=([8, D, H1], F32),
    mb1c=([8, 128, 16], F32), mw2t=([8, H1, D], F32), mb2r=([8, D], F32),
    sel8=([8, 8, 128], BF16), x4=([D, T], F32),
)


def _mk_io(nc, names, outs, interns=()):
    io = {}
    for n in names:
        sh, dt = _IOSPEC[n]
        io[n] = nc.dram_tensor(n, sh, dt, kind="ExternalInput").ap()
    for n in outs:
        sh, dt = _IOSPEC[n]
        io[n] = nc.dram_tensor(n, sh, dt, kind="ExternalOutput").ap()
    for n in interns:
        sh, dt = _IOSPEC[n]
        io[n] = nc.dram_tensor(n, sh, dt, kind="Internal").ap()
    return io


def _standalone(setup, ins, outs, interns=()):
    nc = bass.Bass("TRN2", target_bir_lowering=False)
    es = ExitStack()
    s = S()
    io = _mk_io(nc, ins, outs, interns)
    P, bar = _mk_base(nc, es, s)
    LN = LNBank(nc, es)
    phs = setup(nc, es, s, P, LN, io)
    with es:
        _run_phases(nc, bar, [(p, s) for p in phs])
    return nc


def build_A():
    return _standalone(setup_A, ["x0", "wqkv0", "ident"],
                       ["q0", "k0", "v0"])


def build_B():
    return _standalone(setup_B, ["x0", "q0", "k0g", "v0g", "wout0"], ["x1"])


def build_C():
    return _standalone(setup_C, ["x1", "f0w1t", "f0w1r", "f0b1c", "f0w2t",
                                 "f0w2r", "f0b2c"], ["x2"], ["hdr"])


def build_D():
    return _standalone(setup_D, ["x2", "wqkv1", "bqkv1", "lams", "ident"],
                       ["q1", "k1", "ve1"])


def build_E():
    return _standalone(setup_E, ["x2", "q1", "k1g", "ve1g", "wo1", "boc",
                                 "lnwc", "lnbc"], ["x3"])


def build_F():
    return _standalone(setup_F, ["x3", "nwc", "gwt", "mw1t", "mb1c", "mw2t",
                                 "mb2r", "sel8"], ["x4"])


def _mk_gather(io, cc_v, issues, cc_k, wait_k, sim_local=False):
    def ph(en, e, s):
        if en == "gpsimd":
            for src_t, dst_t in issues:
                _issue_gather(e, s, io, cc_v, src_t, dst_t, sim_local)
            e.wait_ge(cc_k, wait_k)
    return ph


def _issue_gather(e, s, io, cc, src_t, dst_t, sim_local=False):
    if sim_local:
        rows = io[src_t].shape[0]
        for r in range(4):
            e.dma_start(out=io[dst_t][r * rows:(r + 1) * rows, :],
                        in_=io[src_t][:, :]).then_inc(cc, 16)
    else:
        e.collective_compute(
            "AllGather", ALU.bypass, replica_groups=GROUPS4,
            ins=[io[src_t]], outs=[io[dst_t]]).then_inc(cc, 1)


def build_fused(sim_local=False, upto="F"):
    nc = bass.Bass("TRN2", target_bir_lowering=False)
    es = ExitStack()
    s0 = S()
    ins = ["x0", "wqkv0", "ident", "wout0",
           "f0w1t", "f0w1r", "f0b1c", "f0w2t", "f0w2r", "f0b2c",
           "wqkv1", "bqkv1", "lams", "wo1", "boc", "lnwc", "lnbc",
           "nwc", "gwt", "mw1t", "mb1c", "mw2t", "mb2r", "sel8"]
    interns = ["q0", "k0", "v0", "k0g", "v0g", "x1", "hdr", "x2",
               "q1", "k1", "ve1", "k1g", "ve1g", "x3"]
    outs = {"A": ["q0", "k0", "v0"], "B": ["x1"], "C": ["x2"],
            "D": ["q1", "k1", "ve1"], "E": ["x3"], "F": ["x4"]}[upto]
    interns = [t for t in interns if t not in outs]
    io = _mk_io(nc, ins, outs + (["x4"] if upto != "F" else []), interns)
    P, bar = _mk_base(nc, es, s0)
    LN = LNBank(nc, es)
    cc_k = es.enter_context(nc.semaphore("cc_k"))
    cc_v = es.enter_context(nc.semaphore("cc_v"))
    pairs = []
    sg = _share(nc, es, s0)

    def add(setup, s):
        P.reset()
        pairs.extend((p, s) for p in setup(nc, es, s, P, LN, io))

    unit = 64 if sim_local else 1
    if upto != "A":
        s0.cc = cc_k
        s0.ccv = cc_v
        s0.sim_local = sim_local
        s0.kgather = ("k0", "k0g", 0)
        s0.vgather0 = ("v0", "v0g")
        s0.io = io
    add(setup_A, s0)
    if upto != "A":
        sB = _share(nc, es, s0)
        sB.cc = cc_v
        sB.cc_lvl = 1 * unit
        sB.cck = cc_k
        sB.cck_lvl = 1 * unit
        add(setup_B, sB)
    if upto in ("C", "D", "E", "F"):
        add(setup_C, _share(nc, es, s0))
    if upto in ("D", "E", "F"):
        sD = _share(nc, es, s0)
        sD.st2 = s0.st2
        sD.st_vv = s0.st_vv
        sD.st2_base = [16 * 8, 16 * 8]
        sD.stv_base = 16
        if upto in ("E", "F"):
            sD.cc = cc_k
            sD.sim_local = sim_local
            sD.kgather = ("k1", "k1g", 2)
            sD.io = io
        add(setup_D, sD)
    if upto in ("E", "F"):
        sE = _share(nc, es, s0)
        sE.cc = cc_v
        sE.cc_lvl = 2 * unit
        sE.cck = cc_k
        sE.cck_lvl = 2 * unit
        sE.vgather = ("ve1", "ve1g")
        sE.io = io
        sE.sim_local = sim_local
        sE.ld_kv2 = sB.ld_kv2
        sE.ld_v2 = sB.ld_v2
        sE.kv_base = 16 * 5 * 4
        sE.v_base = 16 * 2 * 4
        add(setup_E, sE)
    if upto == "F":
        add(setup_F, _share(nc, es, s0))
    with es:
        _run_phases(nc, bar, pairs)
    return nc



def _common_stage(inputs):
    Wq = np.asarray(inputs["L0_attn_W_qkv"], np.float32)
    w1 = np.asarray(inputs["L0_ffn_w1"], np.float32)
    w2 = np.asarray(inputs["L0_ffn_w2"], np.float32)
    b1 = np.asarray(inputs["L0_ffn_b1"], np.float32)
    b2 = np.asarray(inputs["L0_ffn_b2"], np.float32)
    mw1 = np.asarray(inputs["L1_ffn_w1"], np.float32)
    mw2 = np.asarray(inputs["L1_ffn_w2"], np.float32)
    mb1 = np.asarray(inputs["L1_ffn_b1"], np.float32)
    sel = np.zeros((8, 8, 128), dtype=bf16)
    for ee in range(8):
        sel[ee, ee, :] = 1
    return {
        "wqkv0": np.ascontiguousarray(
            Wq.reshape(D, 16, 64, 3).transpose(3, 0, 1, 2).reshape(3, D, D)),
        "ident": np.eye(128, dtype=bf16),
        "wout0": np.asarray(inputs["L0_attn_W_out"], np.float32),
        "f0w1t": np.ascontiguousarray(w1.T), "f0w1r": w1,
        "f0b1c": np.ascontiguousarray(b1.reshape(32, 128).T),
        "f0w2t": np.ascontiguousarray(w2.T), "f0w2r": w2,
        "f0b2c": np.ascontiguousarray(b2.reshape(8, 128).T),
        "wqkv1": np.ascontiguousarray(np.stack(
            [np.asarray(inputs["L1_attn_Wq"], np.float32),
             np.asarray(inputs["L1_attn_Wk"], np.float32),
             np.asarray(inputs["L1_attn_Wv"], np.float32)])),
        "bqkv1": np.ascontiguousarray(np.stack(
            [np.asarray(inputs["L1_attn_bq"], np.float32).reshape(8, 128).T,
             np.asarray(inputs["L1_attn_bk"], np.float32).reshape(8, 128).T,
             np.asarray(inputs["L1_attn_bv"], np.float32).reshape(8, 128).T])),
        "lams": np.ascontiguousarray(np.stack(
            [np.asarray(inputs[f"L1_attn_lambda_{nm}"], np.float32)
             for nm in ("q1", "k1", "q2", "k2")], axis=1)),
        "wo1": np.asarray(inputs["L1_attn_Wo"], np.float32),
        "boc": np.ascontiguousarray(
            np.asarray(inputs["L1_attn_bo"], np.float32).reshape(8, 128).T),
        "lnwc": np.ascontiguousarray(
            np.asarray(inputs["L1_attn_ln_w"], np.float32).reshape(8, 128).T),
        "lnbc": np.ascontiguousarray(
            np.asarray(inputs["L1_attn_ln_b"], np.float32).reshape(8, 128).T),
        "nwc": np.ascontiguousarray(
            np.asarray(inputs["L1_ffn_norm_weight"],
                       np.float32).reshape(8, 128).T),
        "gwt": np.ascontiguousarray(
            np.asarray(inputs["L1_ffn_gate_w"], np.float32).T),
        "mw1t": np.ascontiguousarray(mw1.transpose(0, 2, 1)),
        "mb1c": np.ascontiguousarray(
            mb1.reshape(8, 16, 128).transpose(0, 2, 1)),
        "mw2t": np.ascontiguousarray(mw2.transpose(0, 2, 1)),
        "mb2r": np.asarray(inputs["L1_ffn_b2"], np.float32),
        "sel8": sel,
    }


def _stage_A(inputs):
    com = _common_stage(inputs)
    X = np.ascontiguousarray(np.asarray(inputs["L0_attn_X"], np.float32)
                             .reshape(2 * 8192, D))
    maps = []
    for c in range(NCORE):
        maps.append({"x0": np.ascontiguousarray(X[c * T:(c + 1) * T].T),
                     "wqkv0": com["wqkv0"], "ident": com["ident"]})
    return maps


def _stage_B(inputs, resA):
    com = _common_stage(inputs)
    X = np.ascontiguousarray(np.asarray(inputs["L0_attn_X"], np.float32)
                             .reshape(2 * 8192, D))
    maps = []
    for g in range(2):
        cs = range(4 * g, 4 * g + 4)
        kg = np.concatenate([np.asarray(resA[c]["k0"]) for c in cs], axis=0)
        vg = np.concatenate([np.asarray(resA[c]["v0"]) for c in cs], axis=0)
        for c in cs:
            maps.append({"q0": np.asarray(resA[c]["q0"]), "k0g": kg,
                         "v0g": vg, "wout0": com["wout0"],
                         "x0": np.ascontiguousarray(X[c * T:(c + 1) * T].T)})
    return maps


def _stage_C(inputs, x1_by_core):
    com = _common_stage(inputs)
    ks = ["f0w1t", "f0w1r", "f0b1c", "f0w2t", "f0w2r", "f0b2c"]
    return [dict({k: com[k] for k in ks}, x1=np.asarray(x1_by_core[c]))
            for c in range(NCORE)]


def _stage_D(inputs, x2_by_core):
    com = _common_stage(inputs)
    ks = ["wqkv1", "bqkv1", "lams", "ident"]
    return [dict({k: com[k] for k in ks}, x2=np.asarray(x2_by_core[c]))
            for c in range(NCORE)]


def _stage_E(inputs, resD):
    com = _common_stage(inputs)
    ks = ["wo1", "boc", "lnwc", "lnbc"]
    maps = []
    for g in range(2):
        cs = range(4 * g, 4 * g + 4)
        kg = np.concatenate([np.asarray(resD[c]["k1"]) for c in cs], axis=0)
        vg = np.concatenate([np.asarray(resD[c]["ve1"]) for c in cs], axis=0)
        for c in cs:
            maps.append(dict({k: com[k] for k in ks},
                             q1=np.asarray(resD[c]["q1"]), k1g=kg, ve1g=vg))
    return maps


def _stage_F(inputs, x3_by_core):
    com = _common_stage(inputs)
    ks = ["nwc", "gwt", "mw1t", "mb1c", "mw2t", "mb2r", "sel8"]
    return [dict({k: com[k] for k in ks}, x3=np.asarray(x3_by_core[c]))
            for c in range(NCORE)]


def _stage_fused(inputs):
    com = _common_stage(inputs)
    X = np.ascontiguousarray(np.asarray(inputs["L0_attn_X"], np.float32)
                             .reshape(2 * 8192, D))
    maps = []
    for c in range(NCORE):
        maps.append(dict(com,
                         x0=np.ascontiguousarray(X[c * T:(c + 1) * T].T)))
    return maps


def kernel_unfused(**inputs):
    cores = list(range(NCORE))
    resA = run_bass_kernel_spmd(build_A(), _stage_A(inputs), core_ids=cores
                                ).results
    resB = run_bass_kernel_spmd(build_B(), _stage_B(inputs, resA),
                                core_ids=cores).results
    x1 = [np.asarray(r["x1"]) for r in resB]
    resC = run_bass_kernel_spmd(build_C(), _stage_C(inputs, x1),
                                core_ids=cores).results
    x2 = [np.asarray(r["x2"]) for r in resC]
    resD = run_bass_kernel_spmd(build_D(), _stage_D(inputs, x2),
                                core_ids=cores).results
    mapsE = _stage_E(inputs, resD)
    for c in cores:
        mapsE[c]["x2"] = x2[c]
    resE = run_bass_kernel_spmd(build_E(), mapsE, core_ids=cores).results
    x3 = [np.asarray(r["x3"]) for r in resE]
    resF = run_bass_kernel_spmd(build_F(), _stage_F(inputs, x3),
                                core_ids=cores).results
    full = np.concatenate([np.asarray(r["x4"]).T for r in resF], axis=0)
    return np.ascontiguousarray(full.reshape(2, 8192, D))


def kernel(**inputs):
    cores = list(range(NCORE))
    res = run_bass_kernel_spmd(build_fused(), _stage_fused(inputs),
                               core_ids=cores).results
    full = np.concatenate([np.asarray(r["x4"]).T for r in res], axis=0)
    return np.ascontiguousarray(full.reshape(2, 8192, D))
